# version 08
# baseline (speedup 1.0000x reference)
import math
import numpy as np
import ml_dtypes
import concourse.bass as bass
import concourse.mybir as mybir
from concourse.bass import IndirectOffsetOnAxis
from concourse.bass_utils import run_bass_kernel_spmd

F32 = mybir.dt.float32
BF16 = mybir.dt.bfloat16
I32 = mybir.dt.int32
U32 = mybir.dt.uint32
AF = mybir.ActivationFunctionType
ALU = mybir.AluOpType
AX = mybir.AxisListType

NCORES = 8
T = 2048
D = 1024
NSEQ = 2
NT = T // 128
NC = D // 128

ENGINES = ("sp", "pe", "dve", "act", "gp")


class Prog:

    def __init__(self):
        self.nc = bass.Bass("TRN2", target_bir_lowering=False)
        self.ops = {e: [] for e in ENGINES}
        self.sems = {}
        self.counts = {}
        self.waited = {e: {} for e in ENGINES}
        self.dirty_w = {e: set() for e in ENGINES}
        self.dirty_r = {e: set() for e in ENGINES}
        self.AR = None
        self.ar_off = 0
        self.ar_mark = 0
        self.ar_peak = 0
        self.name_prefix = ""
        self.fence = []

    def init_arena(self, nbytes=204800):
        assert nbytes % 4 == 0
        self.AR = self.nc.alloc_sbuf_tensor("ARENA", [128, nbytes // 4], F32)
        self.ar_bytes = nbytes

    def abuf(self, name, shape, dtype):
        if self.AR is None:
            return self.nc.alloc_sbuf_tensor(
                self.name_prefix + name, list(shape), dtype)[:]
        P, n = shape
        esz = mybir.dt.size(dtype)
        off = (self.ar_off + 63) & ~63
        self.ar_off = off + n * esz
        self.ar_peak = max(self.ar_peak, self.ar_off)
        assert self.ar_off <= self.ar_bytes, (
            f"arena overflow at {name}: {self.ar_off} > {self.ar_bytes}")
        view = self.AR[:].bitcast(dtype)[0:P, off // esz:off // esz + n]
        return view

    def ascope(self):
        self.ar_mark = self.ar_off

    def aend(self):
        self.ar_off = self.ar_mark

    def full_fence(self):
        toks = [(s, self.counts[s]) for s in list(self.sems)
                if self.counts.get(s, 0) > 0]
        self.fence = list(toks)
        for eng in ENGINES:
            self.barrier(eng, toks)

    def sem(self, name):
        if name not in self.sems:
            self.sems[name] = self.nc.alloc_semaphore(name)
            self.counts[name] = 0
        return self.sems[name]

    def sbuf(self, name, shape, dtype):
        return self.nc.alloc_sbuf_tensor(name, list(shape), dtype)

    def psum(self, name, shape, dtype=F32):
        return self.nc.alloc_psum_tensor(name, list(shape), dtype)

    def dram(self, name, shape, dtype, kind="Internal"):
        return self.nc.dram_tensor(self.name_prefix + name, list(shape),
                                   dtype, kind=kind)

    def tok(self, semname):
        self.sem(semname)
        return self.counts[semname]

    def now(self, *semnames):
        return [(s, self.tok(s)) for s in semnames]

    def emit(self, eng, fn, reads=(), writes=(), wait=(), inc=None,
             dma=False, nodrain=False):
        waits = []
        for semname, val in wait:
            if val <= 0:
                continue
            self.sem(semname)
            if self.waited[eng].get(semname, 0) < val:
                waits.append((semname, val))
                self.waited[eng][semname] = val
        need_drain = False
        if eng == "pe" and not getattr(self, "pe_drain_on", False):
            nodrain = True
        if not nodrain:
            rset, wset = set(reads), set(writes)
            if dma:
                if rset & self.dirty_w[eng]:
                    need_drain = True
            elif (rset & self.dirty_w[eng]) or (
                    wset & (self.dirty_w[eng] | self.dirty_r[eng])):
                need_drain = True
        token = None
        if inc is not None:
            semname, k = inc
            self.sem(semname)
            self.counts[semname] += k
            token = self.counts[semname]
        sems = self.sems

        def closure(engine, waits=tuple(waits), need_drain=need_drain,
                    fn=fn, inc=inc):
            for semname, val in waits:
                engine.wait_ge(sems[semname], val)
            if need_drain:
                engine.drain()
            instr = fn(engine)
            if inc is not None:
                assert instr is not None, "inc requires fn returning instr"
                instr.then_inc(sems[inc[0]], inc[1])

        self.ops[eng].append(closure)
        if need_drain:
            self.dirty_w[eng].clear()
            self.dirty_r[eng].clear()
        if not dma:
            self.dirty_w[eng].update(writes)
            self.dirty_r[eng].update(reads)
        return token

    def barrier(self, eng, wait):
        self.emit(eng, lambda e: None, wait=wait, nodrain=True)

    def finish(self):
        nc, ops = self.nc, self.ops
        with nc.Block() as block:
            @block.sync
            def _(sync):
                for fn in ops["sp"]:
                    fn(sync)

            @block.tensor
            def _(tensor):
                for fn in ops["pe"]:
                    fn(tensor)

            @block.vector
            def _(vector):
                for fn in ops["dve"]:
                    fn(vector)

            @block.scalar
            def _(scalar):
                for fn in ops["act"]:
                    fn(scalar)

            @block.gpsimd
            def _(gpsimd):
                for fn in ops["gp"]:
                    fn(gpsimd)
        return nc



def alibi_slopes(n_heads):
    start = 2.0 ** (-2.0 ** (-(math.log2(n_heads) - 3)))
    return np.array([start * start ** i for i in range(n_heads)],
                    dtype=np.float32)


CSTB_QAUG = 0
CSTB_KAUG = 16 * 2 * T
CSTB_LEN = CSTB_KAUG + 2 * T


def make_consts():
    i = np.arange(128)
    mask = np.where(i[None, :] <= i[:, None], 0.0, -1e30).astype(np.float32)
    cstf = mask.reshape(-1)
    sl = alibi_slopes(16)
    qaug = np.zeros((16, 2, T), np.float32)
    qaug[:, 0, :] = (sl * 16.0)[:, None]
    qaug[:, 1, :] = sl[:, None]
    j = np.arange(T, dtype=np.float32)
    kaug = np.stack([np.floor(j / 16.0), j % 16.0], axis=0)
    cstb = np.concatenate([qaug.reshape(-1), kaug.reshape(-1)])
    return cstf, cstb.astype(ml_dtypes.bfloat16)



class Common:
    def __init__(self, p: Prog, xt_tokens=T):
        self.p = p
        self.xt_tokens = xt_tokens
        self.RT = [p.abuf(f"RT{i}", [128, 1024], F32) for i in range(2)]
        self.XLN = [p.abuf(f"XLN{i}", [128, 1024], BF16) for i in range(2)]
        self.ST = p.abuf("STAT", [128, 16], F32)
        self.XT = None
        if p.AR is None:
            self.set_xt(xt_tokens)
        self.CF = p.abuf("CMASK", [128, 128], F32)
        self.ONES = p.abuf("ONESR", [1, 2048], BF16)
        self.IDENT = p.abuf("IDENT", [128, 128], BF16)
        self.PSM = p.psum("PSM", [128, 2048], F32)
        self.PST = [p.psum(f"PST{i}", [128, 1024], BF16) for i in range(2)]
        self.PSX = [p.psum(f"PSX{i}", [128, 512], F32) for i in range(2)]
        self.psm_free = []
        self.pst_free = [[], []]
        self.psx_free = [[], []]
        self.rt_free = [[], []]
        self.rt_store = [[], []]
        self.xln_free = [[], []]
        self.st_free = []

    def set_xt(self, xt_tokens):
        self.xt_tokens = xt_tokens
        self.XT = self.p.abuf("XT", [128, NC * xt_tokens], BF16)

    def emit_setup(self, cstf_ap):
        p = self.p
        ID, ONES, CF = self.IDENT, self.ONES, self.CF
        p.emit("gp", lambda g: g.memset(ID[:], 1.0), writes=["ident"])
        p.emit("gp", lambda g: g.affine_select(
            out=ID[:], in_=ID[:], pattern=[[-1, 128]],
            compare_op=ALU.is_equal, fill=0.0, base=0, channel_multiplier=1),
            reads=["ident"], writes=["ident"], inc=("GPC", 1))
        p.emit("gp", lambda g: g.memset(ONES[:], 1.0), writes=["ones"],
               inc=("GPC", 1))
        self.tok_setup_gp = p.tok("GPC")
        p.emit("sp", lambda s: s.dma_start(
            out=CF[:], in_=cstf_ap.rearrange("(a b) -> a b", a=128)),
            dma=True, inc=("WO", 16))
        self.tok_setup_ld = p.tok("WO")

    def emit_ln_seq(self, rin_seq, extra_tile_cb=None, tile0=0,
                    ntiles=NT, xt_tokens=None):
        p = self.p
        ST = self.ST
        if xt_tokens is None:
            xt_tokens = self.xt_tokens
        for ti in range(tile0, tile0 + ntiles):
            par = ti % 2
            RT, XLN = self.RT[par], self.XLN[par]
            rkey, xkey = f"RT{par}", f"XLN{par}"
            tok_ld = p.emit(
                "sp", lambda s, RT=RT, ti=ti, rin_seq=rin_seq: s.dma_start(
                    out=RT[:], in_=rin_seq[ti * 128:(ti + 1) * 128, :]),
                dma=True, wait=self.rt_free[par] + self.rt_store[par],
                inc=(f"RL{par}", 16))
            p.emit("act", lambda a, RT=RT, XLN=XLN: a.activation(
                out=XLN[:], in_=RT[:], func=AF.Square,
                accum_out=ST[:, 0:1]),
                reads=[rkey], writes=[xkey, "st"],
                wait=[(f"RL{par}", tok_ld)] + self.xln_free[par] +
                     self.st_free)
            tok_sums = p.emit("act", lambda a, RT=RT, XLN=XLN: a.activation(
                out=XLN[:], in_=RT[:], func=AF.Identity,
                accum_out=ST[:, 1:2]),
                reads=[rkey], writes=[xkey, "st"], inc=("ACT", 1))
            p.emit("dve", lambda v: v.tensor_scalar_mul(
                ST[:, 2:3], ST[:, 1:2], 1.0 / 1024.0),
                reads=["st"], writes=["st"], wait=[("ACT", tok_sums)])
            p.emit("dve", lambda v: v.tensor_scalar(
                out=ST[:, 3:4], in0=ST[:, 2:3], scalar1=ST[:, 2:3],
                scalar2=-1e-5, op0=ALU.mult, op1=ALU.add),
                reads=["st"], writes=["st"])
            tok_var = p.emit("dve", lambda v: v.scalar_tensor_tensor(
                out=ST[:, 4:5], in0=ST[:, 0:1], scalar=1.0 / 1024.0,
                in1=ST[:, 3:4], op0=ALU.mult, op1=ALU.subtract),
                reads=["st"], writes=["st"], inc=("DVE", 1))
            p.emit("act", lambda a: a.activation(
                out=ST[:, 5:6], in_=ST[:, 4:5], func=AF.Ln),
                reads=["st"], writes=["st2"],
                wait=[("DVE", tok_var)])
            tok_rstd = p.emit("act", lambda a: a.activation(
                out=ST[:, 6:7], in_=ST[:, 5:6], func=AF.Exp,
                scale=-0.5), reads=["st2"], writes=["st2"], inc=("ACT", 1))
            tok_xln = p.emit("dve", lambda v, RT=RT, XLN=XLN: v.tensor_scalar(
                out=XLN[:], in0=RT[:], scalar1=ST[:, 2:3],
                scalar2=ST[:, 6:7], op0=ALU.subtract, op1=ALU.mult),
                reads=[rkey, "st", "st2"], writes=[xkey],
                wait=[("ACT", tok_rstd)], inc=("DVE", 1))
            self.st_free = [("DVE", tok_xln)]
            self.rt_free[par] = [("DVE", tok_xln)]
            if extra_tile_cb is not None:
                extra_tile_cb(ti, XLN, tok_xln)
            pst = self.PST[par]
            last = None
            for ci in range(NC):
                last = p.emit(
                    "pe", lambda t, ci=ci, XLN=XLN, pst=pst: t.transpose(
                        pst[:, ci * 128:(ci + 1) * 128],
                        XLN[:, ci * 128:(ci + 1) * 128], self.IDENT[:]),
                    reads=[xkey, "ident"], writes=[f"pst{par}"],
                    wait=([("DVE", tok_xln), ("GPC", self.tok_setup_gp)] +
                          self.pst_free[par]),
                    inc=("PE", 1) if ci == NC - 1 else None)
            xt_dst = self.XT[:, 0:NC * xt_tokens].rearrange(
                "p (c t) -> p c t", c=NC)[
                :, :, (ti - tile0) * 128:(ti - tile0 + 1) * 128]
            pst_src = pst[:].rearrange("p (c t) -> p c t", c=NC)[
                :, :, 0:128]
            tok_evac = p.emit("act", lambda a, d=xt_dst, s=pst_src: a.copy(
                d, s), reads=[f"pst{par}"], writes=["xt"],
                wait=[("PE", last)], inc=("ACT", 1))
            self.pst_free[par] = [("ACT", tok_evac)]
            self.xln_free[par] = [("ACT", tok_evac)]
        return ("ACT", p.tok("ACT"))

    def emit_residual_update(self, rin_seq, rout_seq, delta_cb):
        p = self.p
        for ti in range(NT):
            par = ti % 2
            RT = self.RT[par]
            tok_ld = p.emit(
                "sp", lambda s, RT=RT, ti=ti, rin_seq=rin_seq: s.dma_start(
                    out=RT[:], in_=rin_seq[ti * 128:(ti + 1) * 128, :]),
                dma=True, wait=self.rt_free[par] + self.rt_store[par],
                inc=(f"RL{par}", 16))
            dwait = delta_cb(ti)
            tok_add = p.emit("dve", lambda v, RT=RT: v.tensor_tensor(
                RT[:], RT[:], self.PSM[:, 0:1024], ALU.add),
                reads=[f"RT{par}", "psm"], writes=[f"RT{par}"],
                wait=dwait + [(f"RL{par}", tok_ld)], inc=("DVE", 1))
            self.psm_free = [("DVE", tok_add)]
            self.rt_free[par] = [("DVE", tok_add)]
            tok_st = p.emit(
                "sp", lambda s, RT=RT, ti=ti, rout_seq=rout_seq: s.dma_start(
                    out=rout_seq[ti * 128:(ti + 1) * 128, :], in_=RT[:]),
                dma=True, wait=[("DVE", tok_add)], inc=(f"ST{par}", 16))
            self.rt_store[par] = [(f"ST{par}", tok_st)]


def cast_weight(p, name, src_ap, shape):
    wb = p.dram(name, shape, BF16)
    tok = p.emit("gp", lambda g: g.dma_start(out=wb.ap(), in_=src_ap),
                 dma=True, inc=("CAST", 16))
    return wb, tok



def emit_mha16(p: Prog, cm: Common, rin, rout, watt, wproj, use_alibi,
               cstb, tag):
    XT = cm.XT
    NH, DH = 16, 64
    wattb, _ = cast_weight(p, f"{tag}_wattb", watt, [3 * D, D])
    wprob, _ = cast_weight(p, f"{tag}_wprob", wproj, [D, D])
    tok_cast = p.tok("CAST")

    WAT = p.abuf(f"{tag}WAT", [128, NC * 3 * D], BF16)
    for ci in range(NC):
        p.emit("sp", lambda s, ci=ci: s.dma_start_transpose(
            WAT[:, ci * 3 * D:(ci + 1) * 3 * D],
            wattb.ap()[:, ci * 128:(ci + 1) * 128]),
            dma=True, wait=[("CAST", tok_cast)], inc=("WO", 16))
    WPT = p.abuf(f"{tag}WPT", [128, NC * D], BF16)
    for ci in range(NC):
        p.emit("sp", lambda s, ci=ci: s.dma_start_transpose(
            WPT[:, ci * D:(ci + 1) * D],
            wprob.ap()[:, ci * 128:(ci + 1) * 128]),
            dma=True, wait=[("CAST", tok_cast)], inc=("WO", 16))
    tok_w = p.tok("WO")

    QH = p.abuf(f"{tag}QH", [128, T], BF16)
    KH = p.abuf(f"{tag}KH", [128, T], BF16)
    VB = p.abuf(f"{tag}VB", [128, NT * D], BF16)
    EXP = [p.abuf(f"{tag}EXP{i}", [128, T], BF16) for i in range(2)]
    ATTS = [p.abuf(f"{tag}ATTS{i}", [128, T], BF16) for i in range(2)]
    CTXS = p.abuf(f"{tag}CTXS", [128, NC * T], BF16)
    RS = p.abuf(f"{tag}RS", [128, 8], F32)
    kdim = 66 if use_alibi else 64
    scale = 1.0 / math.sqrt(DH)
    qh_free = []
    exp_free = [[], []]
    atts_free = [[], []]
    ctxs_free = []

    for seq in range(NSEQ):
        rin_seq = rin[seq * T:(seq + 1) * T, :]
        rout_seq = rout[seq * T:(seq + 1) * T, :]
        cm.emit_ln_seq(rin_seq)
        tok_xt = p.tok("ACT")

        for ti in range(NT):
            last = None
            for ci in range(NC):
                for nh2 in range(2):
                    last = p.emit(
                        "pe", lambda t, ci=ci, ti=ti, nh2=nh2: t.matmul(
                            cm.PSM[:, nh2 * 512:(nh2 + 1) * 512],
                            XT[:, ci * T + ti * 128:
                                  ci * T + (ti + 1) * 128],
                            WAT[:, ci * 3 * D + 2 * D + nh2 * 512:
                                ci * 3 * D + 2 * D + (nh2 + 1) * 512],
                            start=(ci == 0), stop=(ci == NC - 1)),
                        reads=["xt", "wat"], writes=["psm"],
                        wait=[("ACT", tok_xt), ("WO", tok_w)] + cm.psm_free,
                        inc=("PE", 1) if (ci == NC - 1 and nh2 == 1) else None)
            tok_e = p.emit("act", lambda a, ti=ti: a.copy(
                VB[:, ti * D:(ti + 1) * D], cm.PSM[:, 0:1024]),
                reads=["psm"], writes=["vb"], wait=[("PE", last)],
                inc=("ACT", 1))
            cm.psm_free = [("ACT", tok_e)]
        tok_vb = p.tok("ACT")

        for h in range(NH):
            for QK, sec, OUT, okey, sc in ((0, 0, QH, f"{tag}qh", scale),
                                           (1, 1, KH, f"{tag}kh", 1.0)):
                for nt4 in range(T // 512):
                    last = None
                    for ci in range(NC):
                        roff = ci * 3 * D + sec * D + h * DH
                        last = p.emit(
                            "pe",
                            lambda t, ci=ci, nt4=nt4, roff=roff: t.matmul(
                                cm.PSM[0:64, nt4 * 512:(nt4 + 1) * 512],
                                WAT[:, roff:roff + DH],
                                XT[:, ci * T + nt4 * 512:
                                      ci * T + (nt4 + 1) * 512],
                                start=(ci == 0), stop=(ci == NC - 1)),
                            reads=["wat", "xt"], writes=["psm"],
                            wait=cm.psm_free,
                            inc=("PE", 1) if ci == NC - 1 else None)
                    tok_e = p.emit(
                        "act", lambda a, OUT=OUT, nt4=nt4, sc=sc:
                        a.activation(
                            out=OUT[0:64, nt4 * 512:(nt4 + 1) * 512],
                            in_=cm.PSM[0:64, nt4 * 512:(nt4 + 1) * 512],
                            func=AF.Copy, scale=sc),
                        reads=["psm"], writes=[okey],
                        wait=[("PE", last)] + qh_free, inc=("ACT", 1))
                    cm.psm_free = [("ACT", tok_e)]
            if use_alibi:
                p.emit("sp", lambda s, h=h: s.dma_start(
                    out=QH[64:66, :],
                    in_=cstb[CSTB_QAUG + h * 2 * T:
                             CSTB_QAUG + (h + 1) * 2 * T].rearrange(
                        "(a b) -> a b", a=2)),
                    dma=True, wait=qh_free, inc=("WO", 16))
                if h == 0 and seq == 0:
                    p.emit("sp", lambda s: s.dma_start(
                        out=KH[64:66, :],
                        in_=cstb[CSTB_KAUG:CSTB_KAUG + 2 * T].rearrange(
                            "(a b) -> a b", a=2)),
                        dma=True, inc=("WO", 16))
            tok_qk = p.tok("ACT")
            tok_qk_ld = p.tok("WO")

            def ph1(qi):
                par = qi % 2
                klen = (qi + 1) * 128
                off = par * 1024 if qi < 8 else 0
                if qi < 8:
                    pwait = half_free[par] + (cm.psm_free if qi < 2 else [])
                else:
                    pwait = cm.psm_free + half_free[0] + half_free[1]
                nch = (klen + 511) // 512
                last = None
                for ch in range(nch):
                    w = min(512, klen - ch * 512)
                    last = p.emit("pe", lambda t, ch=ch, w=w, qi=qi,
                                  off=off, kdim=kdim: t.matmul(
                        cm.PSM[:, off + ch * 512:off + ch * 512 + w],
                        QH[0:kdim, qi * 128:(qi + 1) * 128],
                        KH[0:kdim, ch * 512:ch * 512 + w],
                        start=True, stop=True),
                        reads=[f"{tag}qh", f"{tag}kh"], writes=["psm"],
                        wait=[("ACT", tok_qk), ("WO", tok_qk_ld)] + pwait,
                        inc=("PE", 1) if ch == nch - 1 else None)
                p.emit("dve", lambda v, qi=qi, off=off: v.tensor_tensor(
                    cm.PSM[:, off + qi * 128:off + (qi + 1) * 128],
                    cm.PSM[:, off + qi * 128:off + (qi + 1) * 128],
                    cm.CF[:], ALU.add),
                    reads=["psm", "cf"], writes=["psm"],
                    wait=[("PE", last), ("WO", cm.tok_setup_ld)],
                    inc=("DVE", 1))
                tok_max = p.emit(
                    "dve", lambda v, klen=klen, par=par, off=off:
                    v.reduce_max(
                        RS[:, par * 4:par * 4 + 1],
                        cm.PSM[:, off:off + klen],
                        axis=AX.X, negate=True),
                    reads=["psm", f"rs{par}"], writes=[f"rs{par}"],
                    inc=("DVE", 1))
                tok_exp = p.emit(
                    "act", lambda a, klen=klen, par=par, off=off:
                    a.activation(
                        out=EXP[par][:, 0:klen],
                        in_=cm.PSM[:, off:off + klen],
                        func=AF.Exp, bias=RS[:, par * 4:par * 4 + 1],
                        accum_out=RS[:, par * 4 + 1:par * 4 + 2]),
                    reads=["psm", f"rs{par}"],
                    writes=[f"{tag}exp{par}", f"rs{par}"],
                    wait=[("DVE", tok_max)] + exp_free[par],
                    inc=("ACT", 1))
                if qi < 8:
                    half_free[par] = [("ACT", tok_exp)]
                else:
                    cm.psm_free = [("ACT", tok_exp)]
                return tok_exp

            def ph2(qi, h, tok_exp):
                par = qi % 2
                klen = (qi + 1) * 128
                p.emit("dve", lambda v, par=par: v.reciprocal(
                    RS[:, par * 4 + 2:par * 4 + 3],
                    RS[:, par * 4 + 1:par * 4 + 2]),
                    reads=[f"rs{par}"], writes=[f"rs{par}"],
                    wait=[("ACT", tok_exp)], inc=("DVE", 1))
                tok_nrm = p.emit(
                    "dve", lambda v, klen=klen, par=par: v.tensor_scalar_mul(
                        EXP[par][:, 0:klen], EXP[par][:, 0:klen],
                        RS[:, par * 4 + 2:par * 4 + 3]),
                    reads=[f"{tag}exp{par}", f"rs{par}"],
                    writes=[f"{tag}exp{par}"], inc=("DVE", 1))
                nbk = qi + 1
                ev_toks = {}
                tp_last = {}
                for b in range((nbk + 7) // 8):
                    lo, hi = b * 8, min(nbk, b * 8 + 8)
                    for kb in range(lo, hi):
                        tp_last[b] = p.emit(
                            "pe", lambda t, kb=kb, b=b, par=par:
                            t.transpose(
                                cm.PST[b][:, (kb % 8) * 128:
                                         (kb % 8 + 1) * 128],
                                EXP[par][:, kb * 128:(kb + 1) * 128],
                                cm.IDENT[:]),
                            reads=[f"{tag}exp{par}", "ident"],
                            writes=[f"pst{b}"],
                            wait=[("DVE", tok_nrm)] + cm.pst_free[b],
                            inc=("PE", 1) if kb == hi - 1 else None)
                    ev_toks[b] = p.emit(
                        "act", lambda a, b=b, lo=lo, hi=hi, par=par: a.copy(
                            ATTS[par][:, lo * 128:hi * 128],
                            cm.PST[b][:, 0:(hi - lo) * 128]),
                        reads=[f"pst{b}"], writes=[f"{tag}atts{par}"],
                        wait=[("PE", tp_last[b])] + atts_free[par],
                        inc=("ACT", 1))
                    cm.pst_free[b] = [("ACT", ev_toks[b])]
                ctx_last = None
                for kb in range(nbk):
                    ctx_last = p.emit(
                        "pe", lambda t, kb=kb, qi=qi, h=h, par=par: t.matmul(
                            cm.PSX[par][0:64, 0:128],
                            VB[:, kb * D + h * DH:kb * D + (h + 1) * DH],
                            ATTS[par][:, kb * 128:(kb + 1) * 128],
                            start=(kb == 0), stop=(kb == nbk - 1)),
                        reads=["vb", f"{tag}atts{par}"],
                        writes=[f"psx{par}"],
                        wait=[("ACT", ev_toks[kb // 8]), ("ACT", tok_vb)] +
                             (cm.psx_free[par] if kb == 0 else []),
                        inc=("PE", 1) if kb == nbk - 1 else None)
                atts_free[par] = [("PE", ctx_last)]
                exp_free[par] = [("PE", ctx_last)]
                pb = (h % 2) * 64
                coff = (h // 2) * T + qi * 128
                tok_cv = p.emit(
                    "act", lambda a, pb=pb, coff=coff, par=par: a.copy(
                        CTXS[pb:pb + 64, coff:coff + 128],
                        cm.PSX[par][0:64, 0:128]),
                    reads=[f"psx{par}"], writes=[f"{tag}ctxs"],
                    wait=[("PE", ctx_last)] + ctxs_free, inc=("ACT", 1))
                cm.psx_free[par] = [("ACT", tok_cv)]

            half_free = [cm.psm_free, cm.psm_free]
            pend = None
            for qi in range(NT):
                te = ph1(qi)
                if pend is not None:
                    ph2(pend[0], h, pend[1])
                pend = (qi, te)
            ph2(pend[0], h, pend[1])
            cm.psm_free = cm.psm_free + half_free[0] + half_free[1]
            qh_free = [("PE", p.tok("PE"))]
        tok_ctx_done = p.tok("ACT")

        def delta_cb(ti, tok_ctx_done=tok_ctx_done):
            last = None
            for nh2 in range(2):
                for ci in range(NC):
                    last = p.emit(
                        "pe", lambda t, ti=ti, nh2=nh2, ci=ci: t.matmul(
                            cm.PSM[:, nh2 * 512:(nh2 + 1) * 512],
                            CTXS[:, ci * T + ti * 128:ci * T + (ti + 1) * 128],
                            WPT[:, ci * D + nh2 * 512:
                                ci * D + (nh2 + 1) * 512],
                            start=(ci == 0), stop=(ci == NC - 1)),
                        reads=[f"{tag}ctxs", "wpt"], writes=["psm"],
                        wait=[("ACT", tok_ctx_done)] + cm.psm_free,
                        inc=("PE", 1) if (ci == NC - 1 and nh2 == 1) else None)
            return [("PE", last)]

        cm.emit_residual_update(rin_seq, rout_seq, delta_cb)
        ctxs_free = [("PE", p.tok("PE"))]

    p.barrier("sp", [("ST0", p.tok("ST0")), ("ST1", p.tok("ST1"))])



def emit_l0_attn_b(p: Prog, cm: Common, rin, rout, wq, wk, wv, wo):
    XT = cm.XT
    DK = 128
    wqb, _ = cast_weight(p, "ab_wqb", wq, [D, DK])
    wkb, _ = cast_weight(p, "ab_wkb", wk, [D, DK])
    wvb, _ = cast_weight(p, "ab_wvb", wv, [D, DK])
    wob, _ = cast_weight(p, "ab_wob", wo, [DK, D])
    tok_cast = p.tok("CAST")

    WQ = p.abuf("abWQ", [128, NC * DK], BF16)
    WK = p.abuf("abWK", [128, NC * DK], BF16)
    WV = p.abuf("abWV", [128, NC * DK], BF16)
    WO = p.abuf("abWO", [128, D], BF16)
    for Ws, wb in ((WQ, wqb), (WK, wkb), (WV, wvb)):
        p.emit("sp", lambda s, Ws=Ws, wb=wb: s.dma_start(
            out=Ws[:].rearrange("p (c n) -> p c n", c=NC),
            in_=wb.ap().rearrange("(c p) n -> p c n", p=128)),
            dma=True, wait=[("CAST", tok_cast)], inc=("WO", 16))
    p.emit("sp", lambda s: s.dma_start(out=WO[:], in_=wob.ap()),
           dma=True, wait=[("CAST", tok_cast)], inc=("WO", 16))
    tok_w = p.tok("WO")

    QF = p.abuf("abQF", [128, T], BF16)
    KF = p.abuf("abKF", [128, T], BF16)
    VB = p.abuf("abVB", [128, NT * DK], BF16)
    EXP = p.abuf("abEXP", [128, T], BF16)
    ATTS = p.abuf("abATTS", [128, T], BF16)
    CTXS = p.abuf("abCTXS", [128, NT * DK], BF16)
    RS = p.abuf("abRS", [128, 4], F32)

    scale = 1.0 / math.sqrt(DK)
    exp_free = []
    atts_free = []

    for seq in range(NSEQ):
        rin_seq = rin[seq * T:(seq + 1) * T, :]
        rout_seq = rout[seq * T:(seq + 1) * T, :]
        cm.emit_ln_seq(rin_seq)
        tok_xt = p.tok("ACT")

        for OUT, Ws, okey, sc in ((QF, WQ, "abqf", scale),
                                  (KF, WK, "abkf", 1.0)):
            for nt4 in range(T // 512):
                last = None
                for ci in range(NC):
                    last = p.emit(
                        "pe", lambda t, ci=ci, nt4=nt4, Ws=Ws: t.matmul(
                            cm.PSM[:, nt4 * 512:(nt4 + 1) * 512],
                            Ws[:, ci * DK:(ci + 1) * DK],
                            XT[:, ci * T + nt4 * 512:
                                  ci * T + (nt4 + 1) * 512],
                            start=(ci == 0), stop=(ci == NC - 1)),
                        reads=["xt"], writes=["psm"],
                        wait=[("ACT", tok_xt), ("WO", tok_w)] + cm.psm_free,
                        inc=("PE", 1) if ci == NC - 1 else None)
                tok_e = p.emit(
                    "act", lambda a, OUT=OUT, nt4=nt4, sc=sc: a.activation(
                        out=OUT[:, nt4 * 512:(nt4 + 1) * 512],
                        in_=cm.PSM[:, nt4 * 512:(nt4 + 1) * 512],
                        func=AF.Copy, scale=sc),
                    reads=["psm"], writes=[okey], wait=[("PE", last)],
                    inc=("ACT", 1))
                cm.psm_free = [("ACT", tok_e)]

        for ti in range(NT):
            par = ti % 2
            px = cm.PSX[par]
            last = None
            for ci in range(NC):
                last = p.emit("pe", lambda t, ci=ci, ti=ti, px=px: t.matmul(
                    px[:, 0:DK],
                    XT[:, ci * T + ti * 128: ci * T + (ti + 1) * 128],
                    WV[:, ci * DK:(ci + 1) * DK],
                    start=(ci == 0), stop=(ci == NC - 1)),
                    reads=["xt"], writes=[f"psx{par}"],
                    wait=cm.psx_free[par],
                    inc=("PE", 1) if ci == NC - 1 else None)
            tok_e = p.emit("dve", lambda v, ti=ti, px=px: v.tensor_copy(
                VB[:, ti * DK:(ti + 1) * DK], px[:, 0:DK]),
                reads=[f"psx{par}"], writes=["abvb"],
                wait=[("PE", last)], inc=("DVE", 1))
            cm.psx_free[par] = [("DVE", tok_e)]
        tok_v = p.tok("DVE")

        for qi in range(NT):
            klen = (qi + 1) * 128
            nch = (klen + 511) // 512
            last = None
            for ch in range(nch):
                w = min(512, klen - ch * 512)
                last = p.emit("pe", lambda t, ch=ch, w=w, qi=qi: t.matmul(
                    cm.PSM[:, ch * 512:ch * 512 + w],
                    QF[:, qi * 128:(qi + 1) * 128],
                    KF[:, ch * 512:ch * 512 + w],
                    start=True, stop=True),
                    reads=["abqf", "abkf"], writes=["psm"],
                    wait=cm.psm_free,
                    inc=("PE", 1) if ch == nch - 1 else None)
            p.emit("dve", lambda v, qi=qi: v.tensor_tensor(
                cm.PSM[:, qi * 128:(qi + 1) * 128],
                cm.PSM[:, qi * 128:(qi + 1) * 128],
                cm.CF[:], ALU.add),
                reads=["psm", "cf"], writes=["psm"],
                wait=[("PE", last), ("WO", cm.tok_setup_ld)], inc=("DVE", 1))
            tok_max = p.emit("dve", lambda v, klen=klen: v.reduce_max(
                RS[:, 0:1], cm.PSM[:, 0:klen], axis=AX.X, negate=True),
                reads=["psm", "rs"], writes=["rs"], inc=("DVE", 1))
            tok_exp = p.emit("act", lambda a, klen=klen: a.activation(
                out=EXP[:, 0:klen], in_=cm.PSM[:, 0:klen], func=AF.Exp,
                bias=RS[:, 0:1], accum_out=RS[:, 1:2]),
                reads=["psm", "rs"], writes=["abexp", "rs"],
                wait=[("DVE", tok_max)] + exp_free, inc=("ACT", 1))
            cm.psm_free = [("ACT", tok_exp)]
            tok_rec = p.emit("dve", lambda v: v.reciprocal(
                RS[:, 2:3], RS[:, 1:2]), reads=["rs"], writes=["rs"],
                wait=[("ACT", tok_exp)], inc=("DVE", 1))
            tok_nrm = p.emit("dve", lambda v, klen=klen: v.tensor_scalar_mul(
                EXP[:, 0:klen], EXP[:, 0:klen], RS[:, 2:3]),
                reads=["abexp", "rs"], writes=["abexp"], inc=("DVE", 1))
            ctx_last = None
            for kb in range(qi + 1):
                bpar = kb % 2
                tok_tp = p.emit("pe", lambda t, kb=kb, bpar=bpar: t.transpose(
                    cm.PST[bpar][:, 0:128],
                    EXP[:, kb * 128:(kb + 1) * 128], cm.IDENT[:]),
                    reads=["abexp", "ident"], writes=[f"pst{bpar}"],
                    wait=[("DVE", tok_nrm)] + cm.pst_free[bpar],
                    inc=("PE", 1))
                tok_ev = p.emit("act", lambda a, kb=kb, bpar=bpar: a.copy(
                    ATTS[:, kb * 128:(kb + 1) * 128],
                    cm.PST[bpar][:, 0:128]),
                    reads=[f"pst{bpar}"], writes=["abatts"],
                    wait=[("PE", tok_tp)] + atts_free, inc=("ACT", 1))
                cm.pst_free[bpar] = [("ACT", tok_ev)]
                ctx_last = p.emit("pe", lambda t, kb=kb, qi=qi: t.matmul(
                    cm.PSX[0][:, 0:128],
                    VB[:, kb * DK:(kb + 1) * DK],
                    ATTS[:, kb * 128:(kb + 1) * 128],
                    start=(kb == 0), stop=(kb == qi)),
                    reads=["abvb", "abatts"], writes=["psx0"],
                    wait=[("ACT", tok_ev), ("DVE", tok_v)] +
                         (cm.psx_free[0] if kb == 0 else []),
                    inc=("PE", 1) if kb == qi else None)
            atts_free = [("PE", ctx_last)]
            exp_free = [("PE", ctx_last)]
            tok_cv = p.emit("dve", lambda v, qi=qi: v.tensor_copy(
                CTXS[:, qi * 128:(qi + 1) * 128], cm.PSX[0][:, 0:128]),
                reads=["psx0"], writes=["abctxs"],
                wait=[("PE", ctx_last)], inc=("DVE", 1))
            cm.psx_free[0] = [("DVE", tok_cv)]
        tok_ctx = p.tok("DVE")

        def delta_cb(ti, tok_ctx=tok_ctx):
            last = None
            for nh in range(2):
                last = p.emit("pe", lambda t, ti=ti, nh=nh: t.matmul(
                    cm.PSM[:, nh * 512:(nh + 1) * 512],
                    CTXS[:, ti * 128:(ti + 1) * 128],
                    WO[:, nh * 512:(nh + 1) * 512],
                    start=True, stop=True),
                    reads=["abctxs"], writes=["psm"],
                    wait=[("DVE", tok_ctx)] + cm.psm_free,
                    inc=("PE", 1) if nh == 1 else None)
            return [("PE", last)]

        cm.emit_residual_update(rin_seq, rout_seq, delta_cb)

    p.barrier("sp", [("ST0", p.tok("ST0")), ("ST1", p.tok("ST1"))])



def build_l0_attn_b():
    p = Prog()
    nc = p.nc
    rin = nc.dram_tensor("rin", [NSEQ * T, D], F32, kind="ExternalInput").ap()
    wq = nc.dram_tensor("wq", [D, 128], F32, kind="ExternalInput").ap()
    wk = nc.dram_tensor("wk", [D, 128], F32, kind="ExternalInput").ap()
    wv = nc.dram_tensor("wv", [D, 128], F32, kind="ExternalInput").ap()
    wo = nc.dram_tensor("wo", [128, D], F32, kind="ExternalInput").ap()
    cstf = nc.dram_tensor("cstf", [128 * 128], F32, kind="ExternalInput").ap()
    rout = nc.dram_tensor("rout", [NSEQ * T, D], F32,
                          kind="ExternalOutput").ap()
    cm = Common(p)
    cm.emit_setup(cstf)
    emit_l0_attn_b(p, cm, rin, rout, wq, wk, wv, wo)
    p.finish()
    return p



def emit_l0_ffn(p: Prog, cm: Common, rin, rout, gate_w, w1, w2, w3):
    XT = cm.XT
    NE, DF = 8, 2048
    gwb, _ = cast_weight(p, "fgwb", gate_w, [NE, D])
    w1b, _ = cast_weight(p, "fw1b", w1, [NE * DF, D])
    w3b, _ = cast_weight(p, "fw3b", w3, [NE * DF, D])
    w2b, _ = cast_weight(p, "fw2b", w2, [NE * D, DF])
    tok_cast = p.tok("CAST")

    GWS = p.abuf("fGWS", [8, D], BF16)
    GWT = p.abuf("fGWT", [128, NC * NE], BF16)
    p.emit("sp", lambda s: s.dma_start(out=GWS[:], in_=gwb.ap()),
           dma=True, wait=[("CAST", tok_cast)], inc=("WO", 16))
    tok_gw = p.tok("WO")
    for ci in range(NC):
        bpar = ci % 2
        tok_tp = p.emit("pe", lambda t, ci=ci, bpar=bpar: t.transpose(
            cm.PST[bpar][:, 0:8], GWS[0:8, ci * 128:(ci + 1) * 128],
            cm.IDENT[0:8, 0:8]),
            reads=["fgws", "ident"], writes=[f"pst{bpar}"],
            wait=[("WO", tok_gw), ("GPC", cm.tok_setup_gp)] +
                 cm.pst_free[bpar],
            inc=("PE", 1))
        tok_ev = p.emit("act", lambda a, ci=ci, bpar=bpar: a.copy(
            GWT[:, ci * NE:(ci + 1) * NE], cm.PST[bpar][:, 0:8]),
            reads=[f"pst{bpar}"], writes=["fgwt"],
            wait=[("PE", tok_tp)], inc=("ACT", 1))
        cm.pst_free[bpar] = [("ACT", tok_ev)]
    tok_gwt = p.tok("ACT")

    IOTA8 = p.abuf("fIOTA8", [128, 8], F32)
    p.emit("gp", lambda g: g.iota(
        IOTA8[:], pattern=[[1, 8]], base=0, channel_multiplier=0,
        allow_small_or_imprecise_dtypes=True), writes=["fiota"],
        inc=("GPC", 1))
    tok_iota = p.tok("GPC")

    W1T = p.abuf("fW1T", [128, NC * DF], BF16)
    W3T = p.abuf("fW3T", [128, NC * DF], BF16)
    W2TF = p.abuf("fW2TF", [128, NT * D], BF16)
    S1 = p.abuf("fS1", [128, 1024], BF16)
    GATED = p.abuf("fGATED", [128, NT * 1024], BF16)
    ACC = p.abuf("fACC", [128, 8 * 1024], F32)
    GT = p.abuf("fGT", [128, NT * NE], F32)
    GS = p.abuf("fGS", [128, 32], F32)
    GIX = p.abuf("fGIX", [128, 8], U32)

    state = {"w13_free": [], "w2t_free": [], "acc_free": [],
             "gated_free": []}

    def load_expert_w13(e):
        sem = f"W13{e % 2}"
        for (W, wb) in ((W1T, w1b), (W3T, w3b)):
            for ci in range(NC):
                p.emit(
                    "sp", lambda s, W=W, wb=wb, ci=ci, e=e:
                    s.dma_start_transpose(
                        W[:, ci * DF:(ci + 1) * DF],
                        wb.ap()[e * DF:(e + 1) * DF,
                                ci * 128:(ci + 1) * 128]),
                    dma=True, wait=([("CAST", tok_cast)] + state["w13_free"]),
                    inc=(sem, 16))
        return (sem, p.tok(sem))

    def load_expert_w2(e):
        sem = f"W2{e % 2}"
        for fc in range(NT):
            p.emit(
                "sp", lambda s, e=e, fc=fc: s.dma_start_transpose(
                    W2TF[:, fc * D:(fc + 1) * D],
                    w2b.ap()[e * D:(e + 1) * D, fc * 128:(fc + 1) * 128]),
                dma=True, wait=([("CAST", tok_cast)] + state["w2t_free"]),
                inc=(sem, 16))
        return (sem, p.tok(sem))

    for seq in range(NSEQ):
        rin_seq = rin[seq * T:(seq + 1) * T, :]
        rout_seq = rout[seq * T:(seq + 1) * T, :]
        for th in range(2):
            XTK = 1024
            cm.emit_ln_seq(rin_seq, tile0=th * 8, ntiles=8, xt_tokens=XTK)
            tok_xt = p.tok("ACT")

            for lt in range(8):
                px = cm.PSX[1]
                last = None
                for ci in range(NC):
                    last = p.emit(
                        "pe", lambda t, ci=ci, lt=lt, px=px: t.matmul(
                            px[:, 0:NE],
                            XT[:, ci * XTK + lt * 128:
                                  ci * XTK + (lt + 1) * 128],
                            GWT[:, ci * NE:(ci + 1) * NE],
                            start=(ci == 0), stop=(ci == NC - 1)),
                        reads=["xt", "fgwt"], writes=["psx1"],
                        wait=[("ACT", tok_xt), ("ACT", tok_gwt)] +
                             cm.psx_free[1],
                        inc=("PE", 1) if ci == NC - 1 else None)
                tok_sc = p.emit("dve", lambda v, px=px: v.tensor_copy(
                    GS[:, 0:8], px[:, 0:NE]), reads=["psx1", "fgs"],
                    writes=["fgs"], wait=[("PE", last)], inc=("DVE", 1))
                cm.psx_free[1] = [("DVE", tok_sc)]
                p.emit("dve", lambda v: v.max(GS[:, 8:16], GS[:, 0:8]),
                       reads=["fgs"], writes=["fgs"])
                p.emit("dve", lambda v: v.max_index(
                    GIX[:], GS[:, 8:16], GS[:, 0:8]),
                    reads=["fgs", "fgix"], writes=["fgix"])
                p.emit("dve", lambda v: v.tensor_copy(
                    GS[:, 16:18], GIX[:, 0:2]),
                    reads=["fgix", "fgs"], writes=["fgs"])
                tok_d = p.emit("dve", lambda v: v.tensor_tensor(
                    GS[:, 18:19], GS[:, 8:9], GS[:, 9:10], ALU.subtract),
                    reads=["fgs"], writes=["fgs"], inc=("DVE", 1))
                tok_sig = p.emit("act", lambda a: a.activation(
                    out=GS[:, 19:20], in_=GS[:, 18:19], func=AF.Sigmoid),
                    reads=["fgs"], writes=["fgs2"],
                    wait=[("DVE", tok_d)], inc=("ACT", 1))
                p.emit("dve", lambda v: v.tensor_scalar(
                    out=GS[:, 20:21], in0=GS[:, 19:20], scalar1=-1.0,
                    scalar2=1.0, op0=ALU.mult, op1=ALU.add),
                    reads=["fgs2", "fgs"], writes=["fgs"],
                    wait=[("ACT", tok_sig)])
                p.emit("dve", lambda v: v.tensor_scalar(
                    out=GS[:, 24:32], in0=IOTA8[:], scalar1=GS[:, 16:17],
                    scalar2=None, op0=ALU.is_equal),
                    reads=["fiota", "fgs"], writes=["fgs"],
                    wait=[("GPC", tok_iota)])
                p.emit("dve", lambda v, lt=lt: v.tensor_scalar_mul(
                    GT[:, lt * NE:(lt + 1) * NE], GS[:, 24:32],
                    GS[:, 19:20]),
                    reads=["fgs", "fgs2", "fgt"], writes=["fgt"])
                p.emit("dve", lambda v: v.tensor_scalar(
                    out=GS[:, 24:32], in0=IOTA8[:], scalar1=GS[:, 17:18],
                    scalar2=None, op0=ALU.is_equal),
                    reads=["fiota", "fgs"], writes=["fgs"])
                p.emit("dve", lambda v, lt=lt: v.scalar_tensor_tensor(
                    out=GT[:, lt * NE:(lt + 1) * NE], in0=GS[:, 24:32],
                    scalar=GS[:, 20:21], in1=GT[:, lt * NE:(lt + 1) * NE],
                    op0=ALU.mult, op1=ALU.add),
                    reads=["fgs", "fgt"], writes=["fgt"], inc=("DVE", 1))
            tok_gate = p.tok("DVE")

            tok_z = p.emit("dve", lambda v: v.memset(ACC[:], 0.0),
                           reads=[], writes=["facc"],
                           wait=state["acc_free"], inc=("DVE", 1))
            tok_w13 = load_expert_w13(0)
            tok_w2 = load_expert_w2(0)
            for e in range(NE):
                pst_h1 = [cm.PST[0][:].bitcast(F32), cm.PST[1][:].bitcast(F32)]
                alt_free = state.get("alt_free", [])
                for fc in range(NT):
                    par = fc % 2
                    if par == 0:
                        h1v = [cm.PSM[:, 0:512], cm.PSM[:, 512:1024]]
                        h3v = [cm.PSM[:, 1024:1536], cm.PSM[:, 1536:2048]]
                        pkey = "psm"
                        pw = cm.psm_free
                    else:
                        h1v = pst_h1
                        h3v = [cm.PSX[0][:], cm.PSX[1][:]]
                        pkey = "psalt"
                        pw = alt_free + cm.pst_free[0] + cm.pst_free[1] + \
                            cm.psx_free[0] + cm.psx_free[1]
                    lastpair = None
                    for HW, hvv, hk in ((W1T, h1v, "fw1t"),
                                        (W3T, h3v, "fw3t")):
                        for ci in range(NC):
                            for n2 in range(2):
                                lastpair = p.emit(
                                    "pe", lambda t, HW=HW, hv=hvv[n2],
                                    ci=ci, n2=n2, fc=fc: t.matmul(
                                        hv,
                                        HW[:, ci * DF + fc * 128:
                                           ci * DF + (fc + 1) * 128],
                                        XT[:, ci * XTK + n2 * 512:
                                              ci * XTK + (n2 + 1) * 512],
                                        start=(ci == 0),
                                        stop=(ci == NC - 1)),
                                    reads=["xt", hk], writes=[pkey],
                                    wait=[tok_w13, ("ACT", tok_xt)] + pw,
                                    inc=("PE", 1) if (
                                        ci == NC - 1 and n2 == 1 and
                                        hk == "fw3t") else None)
                    tok_s1a = p.emit("act", lambda a, h1v=h1v: a.activation(
                        out=S1[:, 0:512], in_=h1v[0], func=AF.Silu),
                        reads=[pkey, "fs1"], writes=["fs1"],
                        wait=[("PE", lastpair)], inc=("ACT", 1))
                    tok_s1 = p.emit("act", lambda a, h1v=h1v: a.activation(
                        out=S1[:, 512:1024], in_=h1v[1], func=AF.Silu),
                        reads=[pkey, "fs1"], writes=["fs1"],
                        inc=("ACT", 1))
                    p.emit(
                        "dve", lambda v, fc=fc, h3v=h3v: v.tensor_tensor(
                            GATED[:, fc * 1024:fc * 1024 + 512],
                            S1[:, 0:512], h3v[0], ALU.mult),
                        reads=["fs1", pkey, "fgated"], writes=["fgated"],
                        wait=[("ACT", tok_s1)] + state["gated_free"],
                        inc=("DVE", 1))
                    tok_gt = p.emit(
                        "dve", lambda v, fc=fc, h3v=h3v: v.tensor_tensor(
                            GATED[:, fc * 1024 + 512:(fc + 1) * 1024],
                            S1[:, 512:1024], h3v[1], ALU.mult),
                        reads=["fs1", pkey, "fgated"], writes=["fgated"],
                        inc=("DVE", 1))
                    if par == 0:
                        cm.psm_free = [("DVE", tok_gt)]
                    else:
                        alt_free = [("DVE", tok_gt)]
                        state["alt_free"] = alt_free
                        cm.pst_free[0] = [("DVE", tok_gt)]
                        cm.pst_free[1] = [("DVE", tok_gt)]
                        cm.psx_free[0] = [("DVE", tok_gt)]
                        cm.psx_free[1] = [("DVE", tok_gt)]
                tok_gated_all = p.tok("DVE")
                state["w13_free"] = [("PE", p.tok("PE"))]
                if e + 1 < NE:
                    tok_w13_next = load_expert_w13(e + 1)
                for tbp in range(4):
                    last = None
                    for fc in range(NT):
                        for half in range(4):
                            tb = tbp * 2 + half // 2
                            o512 = (half % 2) * 512 + (half // 2) * 1024
                            last = p.emit(
                                "pe", lambda t, fc=fc, tb=tb, o512=o512:
                                t.matmul(
                                    cm.PSM[:, o512:o512 + 512],
                                    GATED[:, fc * 1024 + tb * 128:
                                          fc * 1024 + (tb + 1) * 128],
                                    W2TF[:, fc * D + (o512 % 1024):
                                         fc * D + (o512 % 1024) + 512],
                                    start=(fc == 0), stop=(fc == NT - 1)),
                                reads=["fgated", "fw2tf"], writes=["psm"],
                                wait=[tok_w2,
                                      ("DVE", tok_gated_all)] + cm.psm_free,
                                inc=("PE", 1) if (
                                    fc == NT - 1 and half == 3) else None)
                    for half in range(2):
                        tb = tbp * 2 + half
                        g_col = GT[:].rearrange(
                            "p (t e) -> p t e", e=NE)[:, tb, e:e + 1]
                        tok_acc = p.emit(
                            "dve", lambda v, tb=tb, half=half, g_col=g_col:
                            v.scalar_tensor_tensor(
                                out=ACC[:, tb * 1024:(tb + 1) * 1024],
                                in0=cm.PSM[:, half * 1024:
                                           (half + 1) * 1024],
                                scalar=g_col,
                                in1=ACC[:, tb * 1024:(tb + 1) * 1024],
                                op0=ALU.mult, op1=ALU.add),
                            reads=["psm", "facc", "fgt"], writes=["facc"],
                            wait=[("PE", last), ("DVE", tok_gate),
                                  ("DVE", tok_z)], inc=("DVE", 1))
                    cm.psm_free = [("DVE", tok_acc)]
                state["gated_free"] = [("PE", p.tok("PE"))]
                state["w2t_free"] = [("PE", p.tok("PE"))]
                if e + 1 < NE:
                    tok_w13 = tok_w13_next
                    tok_w2 = load_expert_w2(e + 1)
            for tb in range(8):
                ti = th * 8 + tb
                par = ti % 2
                RT = cm.RT[par]
                tok_ld = p.emit(
                    "sp", lambda s, RT=RT, ti=ti, rin_seq=rin_seq:
                    s.dma_start(
                        out=RT[:], in_=rin_seq[ti * 128:(ti + 1) * 128, :]),
                    dma=True, wait=cm.rt_free[par] + cm.rt_store[par],
                    inc=(f"RL{par}", 16))
                tok_add = p.emit(
                    "dve", lambda v, RT=RT, tb=tb: v.tensor_tensor(
                        RT[:], RT[:], ACC[:, tb * 1024:(tb + 1) * 1024],
                        ALU.add),
                    reads=[f"RT{par}", "facc"], writes=[f"RT{par}"],
                    wait=[(f"RL{par}", tok_ld), ("DVE", p.tok("DVE"))],
                    inc=("DVE", 1))
                cm.rt_free[par] = [("DVE", tok_add)]
                tok_st = p.emit(
                    "sp", lambda s, RT=RT, ti=ti, rout_seq=rout_seq:
                    s.dma_start(
                        out=rout_seq[ti * 128:(ti + 1) * 128, :], in_=RT[:]),
                    dma=True, wait=[("DVE", tok_add)], inc=(f"ST{par}", 16))
                cm.rt_store[par] = [(f"ST{par}", tok_st)]
            state["acc_free"] = [("DVE", p.tok("DVE"))]
    p.barrier("sp", [("ST0", p.tok("ST0")), ("ST1", p.tok("ST1"))])


def build_l0_ffn():
    p = Prog()
    nc = p.nc
    rin = nc.dram_tensor("rin", [NSEQ * T, D], F32, kind="ExternalInput").ap()
    gate_w = nc.dram_tensor("gate_w", [8, D], F32, kind="ExternalInput").ap()
    w1 = nc.dram_tensor("w1", [8, 2048, D], F32, kind="ExternalInput").ap()
    w2 = nc.dram_tensor("w2", [8, D, 2048], F32, kind="ExternalInput").ap()
    w3 = nc.dram_tensor("w3", [8, 2048, D], F32, kind="ExternalInput").ap()
    cstf = nc.dram_tensor("cstf", [128 * 128], F32, kind="ExternalInput").ap()
    rout = nc.dram_tensor("rout", [NSEQ * T, D], F32,
                          kind="ExternalOutput").ap()
    cm = Common(p, xt_tokens=1024)
    cm.emit_setup(cstf)
    emit_l0_ffn(p, cm, rin, rout, gate_w,
                w1.rearrange("e f c -> (e f) c"),
                w2.rearrange("e d f -> (e d) f"),
                w3.rearrange("e f c -> (e f) c"))
    p.finish()
    return p



def emit_l1_ffn(p: Prog, cm: Common, rin, rout, wg, bg, wu, bu, wd, bd,
                wqp, keys, down_embed, up_embed):
    def _setph(c):
        p.pe_drain_on = (c == "C")
    XT = cm.XT
    _setph("A")
    DCD, RET, NEXP, NKEYS, H, KPH = 4096, 128, 4096, 64, 4, 8
    NFC = DCD // 128
    wgb, _ = cast_weight(p, "pwgb", wg, [D, DCD])
    wub, _ = cast_weight(p, "pwub", wu, [D, DCD])
    wdb, _ = cast_weight(p, "pwdb", wd, [DCD, D])
    wqb, _ = cast_weight(p, "pwqb", wqp, [D, 512])
    keyb, _ = cast_weight(p, "pkeyb", keys, [H * NKEYS, 2 * 64])
    deb, _ = cast_weight(p, "pdeb", down_embed, [NEXP, D])
    ueb, _ = cast_weight(p, "pueb", up_embed, [NEXP, D])
    tok_cast = p.tok("CAST")

    BGF = p.abuf("pBGF", [128, NFC], F32)
    BUF = p.abuf("pBUF", [128, NFC], F32)
    BDR = p.abuf("pBDR", [1, D], BF16)
    p.emit("sp", lambda s: s.dma_start(
        out=BGF[:], in_=bg.rearrange("(a p) -> p a", p=128),
        allow_slow_non_contiguous=True), dma=True, inc=("WO", 16))
    p.emit("sp", lambda s: s.dma_start(
        out=BUF[:], in_=bu.rearrange("(a p) -> p a", p=128),
        allow_slow_non_contiguous=True), dma=True, inc=("WO", 16))
    tok_bias = p.tok("WO")
    p.emit("gp", lambda g: g.dma_start(
        out=BDR[:], in_=bd.rearrange("(a b) -> a b", a=1)),
        dma=True, inc=("CAST", 16))
    WQP = p.abuf("pWQP", [128, NC * 512], BF16)
    p.emit("sp", lambda s: s.dma_start(
        out=WQP[:].rearrange("p (c n) -> p c n", c=NC),
        in_=wqb.ap().rearrange("(c p) n -> p c n", p=128)),
        dma=True, wait=[("CAST", tok_cast)], inc=("WO", 16))
    tok_wqp = p.tok("WO")
    KEY0 = p.abuf("pKEY0", [128, 2 * 128], BF16)
    KEYST = p.abuf("pKEYST", [128, 8 * 64], BF16)
    p.emit("sp", lambda s: s.dma_start(
        out=KEY0[:].rearrange("p (a x) -> p a x", a=2),
        in_=keyb.ap().rearrange("(a p) x -> p a x", p=128)),
        dma=True, wait=[("CAST", tok_cast)], inc=("WO", 16))
    tok_keys = p.tok("WO")
    for h in range(H):
        for pp in range(2):
            src_pb = (h % 2) * 64
            grp = pp * 4 + h
            for dst_pb in (0, 64):
                tok_tp = p.emit(
                    "pe", lambda t, h=h, pp=pp, src_pb=src_pb,
                    dst_pb=dst_pb: t.transpose(
                        cm.PST[0][dst_pb:dst_pb + 64, 0:64],
                        KEY0[src_pb:src_pb + 64,
                             (h // 2) * 128 + pp * 64:
                             (h // 2) * 128 + (pp + 1) * 64],
                        cm.IDENT[src_pb:src_pb + 64, src_pb:src_pb + 64]),
                    reads=["pkey0", "ident"], writes=["pst0"],
                    wait=[("WO", tok_keys), ("GPC", cm.tok_setup_gp)] +
                         cm.pst_free[0],
                    inc=("PE", 1))
                tok_ev = p.emit(
                    "act", lambda a, grp=grp, dst_pb=dst_pb: a.copy(
                        KEYST[dst_pb:dst_pb + 64, grp * 64:(grp + 1) * 64],
                        cm.PST[0][dst_pb:dst_pb + 64, 0:64]),
                    reads=["pst0"], writes=["pkeyst"],
                    wait=[("PE", tok_tp)], inc=("ACT", 1))
                cm.pst_free[0] = [("ACT", tok_ev)]
    tok_keyst = p.tok("ACT")

    WGF = [p.abuf(f"pWGF{i}", [128, NC * 128], BF16) for i in range(2)]
    WUF = [p.abuf(f"pWUF{i}", [128, NC * 128], BF16) for i in range(2)]
    WDF = [p.abuf(f"pWDF{i}", [128, D], BF16) for i in range(4)]
    XLNS = p.abuf("pXLNS", [128, 8 * 1024], BF16)
    GATED = p.abuf("pGATED", [128, NFC * 1024], BF16)
    S1 = p.abuf("pS1", [128, 1024], BF16)
    QP = p.abuf("pQP", [128, 4 * 1024], BF16)
    SIMS = p.abuf("pSIMS", [128, 512], F32)
    SC1 = p.abuf("pSC1", [128, 64], F32)
    IX1 = p.abuf("pIX1", [128, 64], F32)
    IXU = p.abuf("pIXU", [128, 8], U32)
    CMB = p.abuf("pCMB", [128, 4 * 64], F32)
    SC2 = p.abuf("pSC2", [128, 32], F32)
    PKU = p.abuf("pPKU", [128, 8], U32)
    PKF = p.abuf("pPKF", [128, 32], F32)
    AB2 = p.abuf("pAB2", [128, 64], F32)
    SEL = p.abuf("pSEL", [128, 64], F32)
    MSK = p.abuf("pMSK", [128, 64], F32)
    TMP = p.abuf("pTMP", [128, 64], F32)
    EID = p.abuf("pEID", [128, 32], I32)
    ABI = p.abuf("pABI", [128, 32], I32)
    EIDF = p.abuf("pEIDF", [128, 32], F32)
    PP = p.abuf("pPP", [128, 48], F32)
    WCOL = p.abuf("pWCOL", [128, 32], F32)
    WPR = p.abuf("pWPR", [128, 32], F32)
    DEG = [p.abuf(f"pDEG{i}", [128, 1024], BF16) for i in range(8)]
    UEG = [p.abuf(f"pUEG{i}", [128, 1024], BF16) for i in range(8)]
    JNK = p.abuf("pJNK", [128, 1024], BF16)
    EST = p.abuf("pEST", [128, 1024], F32)
    DELTA = [p.abuf(f"pDELTA{i}", [128, 1024], F32) for i in range(2)]

    state = {"wgf_free": [[], []], "wuf_free": [[], []],
             "wdf_free": [[], [], [], []],
             "ueg_free": [[] for _ in range(8)],
             "deg_free": [[] for _ in range(8)], "est_free": [],
             "eid_free": [],
             "delta_free": [[], []]}

    def load_w_fc(W, wb, fc, key, freelist):
        buf = fc % 2
        tok = p.emit(
            "sp", lambda s, W=W, wb=wb, fc=fc, buf=buf: s.dma_start(
                out=W[buf][:].rearrange("p (c n) -> p c n", c=NC),
                in_=wb.ap().rearrange("(c p) n -> p c n", p=128)[
                    :, :, fc * 128:(fc + 1) * 128],
                allow_slow_non_contiguous=True),
            dma=True, wait=([("CAST", tok_cast)] + freelist[buf]),
            inc=(f"WGU{buf}", 16))
        return ("WGU%d" % buf, p.tok("WGU%d" % buf))

    def load_wd_fc(fc):
        buf = fc % 4
        tok = p.emit(
            "sp", lambda s, fc=fc, buf=buf: s.dma_start(
                out=WDF[buf][:],
                in_=wdb.ap()[fc * 128:(fc + 1) * 128, :]),
            dma=True, wait=([("CAST", tok_cast)] +
                            state["wdf_free"][buf]),
            inc=(f"WD{buf}", 16))
        return ("WD%d" % buf, p.tok("WD%d" % buf))

    for seq in range(NSEQ):
        rin_seq = rin[seq * T:(seq + 1) * T, :]
        rout_seq = rout[seq * T:(seq + 1) * T, :]
        for th in range(2):
            XTK = 1024
            xlns_cb_tok = []

            def xlns_cb(ti, XLN, tok_xln, th=th):
                lt = ti - th * 8
                p.emit("dve", lambda v, XLN=XLN, lt=lt: v.tensor_copy(
                    XLNS[:, lt * 1024:(lt + 1) * 1024], XLN[:]),
                    reads=[f"XLN{ti % 2}", "pxlns"], writes=["pxlns"],
                    inc=("DVE", 1))

            cm.emit_ln_seq(rin_seq, extra_tile_cb=xlns_cb, tile0=th * 8,
                           ntiles=8, xt_tokens=XTK)
            tok_xt = p.tok("ACT")
            tok_xlns = p.tok("DVE")

            _setph("Q")
            for qb in range(4):
                last = None
                for n2 in range(2):
                    for ci in range(NC):
                        last = p.emit(
                            "pe", lambda t, qb=qb, ci=ci, n2=n2: t.matmul(
                                cm.PSM[:, n2 * 512:(n2 + 1) * 512],
                                WQP[:, ci * 512 + qb * 128:
                                    ci * 512 + (qb + 1) * 128],
                                XT[:, ci * XTK + n2 * 512:
                                      ci * XTK + (n2 + 1) * 512],
                                start=(ci == 0), stop=(ci == NC - 1)),
                            reads=["xt", "pwqp"], writes=["psm"],
                            wait=[("ACT", tok_xt), ("WO", tok_wqp)] +
                                 cm.psm_free,
                            inc=("PE", 1) if (ci == NC - 1 and n2 == 1)
                            else None)
                tok_e = p.emit("act", lambda a, qb=qb: a.copy(
                    QP[:, qb * 1024:(qb + 1) * 1024], cm.PSM[:, 0:1024]),
                    reads=["psm"], writes=["pqp"], wait=[("PE", last)],
                    inc=("ACT", 1))
                cm.psm_free = [("ACT", tok_e)]
            tok_qp = p.tok("ACT")

            _setph("B")
            tok_wg = load_w_fc(WGF, wgb, 0, "pwgf", state["wgf_free"])
            tok_wu = load_w_fc(WUF, wub, 0, "pwuf", state["wuf_free"])
            wtoks = {0: (tok_wg, tok_wu)}
            for fc in range(NFC):
                buf = fc % 2
                if fc + 1 < NFC:
                    wtoks[fc + 1] = (
                        load_w_fc(WGF, wgb, fc + 1, "pwgf",
                                  state["wgf_free"]),
                        load_w_fc(WUF, wub, fc + 1, "pwuf",
                                  state["wuf_free"]))
                tok_wg, tok_wu = wtoks.pop(fc)
                lastg = None
                for ci in range(NC):
                    for n2 in range(2):
                        lastg = p.emit(
                            "pe", lambda t, ci=ci, n2=n2, buf=buf: t.matmul(
                                cm.PSM[:, n2 * 512:(n2 + 1) * 512],
                                WGF[buf][:, ci * 128:(ci + 1) * 128],
                                XT[:, ci * XTK + n2 * 512:
                                      ci * XTK + (n2 + 1) * 512],
                                start=(ci == 0), stop=(ci == NC - 1)),
                            reads=["xt", f"pwgf{buf}"], writes=["psm"],
                            wait=[tok_wu, ("ACT", tok_xt)] +
                                 cm.psm_free,
                            inc=("PE", 1) if (ci == NC - 1 and n2 == 1)
                            else None)
                lastu = None
                for ci in range(NC):
                    for n2 in range(2):
                        lastu = p.emit(
                            "pe", lambda t, ci=ci, n2=n2, buf=buf: t.matmul(
                                cm.PSM[:, 1024 + n2 * 512:
                                       1024 + (n2 + 1) * 512],
                                WUF[buf][:, ci * 128:(ci + 1) * 128],
                                XT[:, ci * XTK + n2 * 512:
                                      ci * XTK + (n2 + 1) * 512],
                                start=(ci == 0), stop=(ci == NC - 1)),
                            reads=["xt", f"pwuf{buf}"], writes=["psm"],
                            wait=[tok_wu],
                            inc=("PE", 1) if (ci == NC - 1 and n2 == 1)
                            else None)
                state["wgf_free"][buf] = [("PE", lastg)]
                state["wuf_free"][buf] = [("PE", lastu)]
                tok_s1 = p.emit("act", lambda a, fc=fc: a.activation(
                    out=S1[:], in_=cm.PSM[:, 0:1024], func=AF.Silu,
                    bias=BGF[:, fc:fc + 1]),
                    reads=["psm", "ps1", "pbgf"], writes=["ps1"],
                    wait=[("PE", lastg), ("WO", tok_bias)], inc=("ACT", 1))
                tok_gt = p.emit("dve", lambda v, fc=fc: v.scalar_tensor_tensor(
                    out=GATED[:, fc * 1024:(fc + 1) * 1024],
                    in0=cm.PSM[:, 1024:2048], scalar=BUF[:, fc:fc + 1],
                    in1=S1[:], op0=ALU.add, op1=ALU.mult),
                    reads=["psm", "ps1", "pbuf", "pgated"],
                    writes=["pgated"],
                    wait=[("ACT", tok_s1), ("PE", lastu)], inc=("DVE", 1))
                cm.psm_free = [("DVE", tok_gt)]
            tok_gated = p.tok("DVE")

            tok_wd = load_wd_fc(0)
            wdtoks = {0: tok_wd}
            for lt in range(8):
                ti = th * 8 + lt
                _setph("C")
                last = None
                for pp in range(2):
                    for h in range(H):
                        grp = pp * 4 + h
                        base = ((4 * pp + h) % 2) * 64
                        qb = (4 * pp + h) // 2
                        last = p.emit(
                            "pe", lambda t, grp=grp, base=base, qb=qb,
                            lt=lt: t.matmul(
                                cm.PSM[:, 1536 + grp * 64:
                                       1536 + (grp + 1) * 64],
                                QP[base:base + 64,
                                   qb * 1024 + lt * 128:
                                   qb * 1024 + (lt + 1) * 128],
                                KEYST[base:base + 64,
                                      grp * 64:(grp + 1) * 64],
                                start=(grp == 0), stop=(grp == 7)),
                            reads=["pqp", "pkeyst"], writes=["psm"],
                            wait=[("ACT", tok_qp), ("ACT", tok_keyst)] +
                                 (cm.psm_free if grp == 0 else []),
                            inc=("PE", 1) if grp == 7 else None)
                tok_sims = p.emit("dve", lambda v: v.tensor_copy(
                    SIMS[:], cm.PSM[:, 1536:2048]),
                    reads=["psm", "psims"], writes=["psims"],
                    wait=[("PE", last)], inc=("DVE", 1))
                cm.psm_free = [("DVE", tok_sims)]
                for grp in range(8):
                    p.emit("dve", lambda v, grp=grp: v.max(
                        SC1[:, grp * 8:(grp + 1) * 8],
                        SIMS[:, grp * 64:(grp + 1) * 64]),
                        reads=["psims", "psc1"], writes=["psc1"])
                    p.emit("dve", lambda v, grp=grp: v.max_index(
                        IXU[:], SC1[:, grp * 8:(grp + 1) * 8],
                        SIMS[:, grp * 64:(grp + 1) * 64]),
                        reads=["psims", "psc1", "pixu"], writes=["pixu"])
                    p.emit("dve", lambda v, grp=grp: v.tensor_copy(
                        IX1[:, grp * 8:(grp + 1) * 8], IXU[:]),
                        reads=["pixu", "pix1"], writes=["pix1"])
                for h in range(H):
                    sx = SC1[:, h * 8:(h + 1) * 8]
                    sy = SC1[:, (4 + h) * 8:(5 + h) * 8]
                    sx_b = bass.AP(sx.tensor, sx.offset,
                                   [sx.ap[0], [sx.ap[1][0], 8], [0, 8]])
                    sy_b = bass.AP(sy.tensor, sy.offset,
                                   [sy.ap[0], [0, 8], [sy.ap[1][0], 8]])
                    p.emit("dve", lambda v, h=h, sx_b=sx_b, sy_b=sy_b:
                           v.tensor_tensor(
                        CMB[:, h * 64:(h + 1) * 64].rearrange(
                            "p (a b) -> p a b", a=8),
                        sx_b, sy_b, ALU.add),
                        reads=["psc1", "pcmb"], writes=["pcmb"])
                for h in range(H):
                    p.emit("dve", lambda v, h=h: v.max(
                        SC2[:, h * 8:(h + 1) * 8],
                        CMB[:, h * 64:(h + 1) * 64]),
                        reads=["pcmb", "psc2"], writes=["psc2"])
                    p.emit("dve", lambda v, h=h: v.max_index(
                        PKU[:], SC2[:, h * 8:(h + 1) * 8],
                        CMB[:, h * 64:(h + 1) * 64]),
                        reads=["pcmb", "psc2", "ppku"], writes=["ppku"])
                    p.emit("dve", lambda v, h=h: v.tensor_copy(
                        PKF[:, h * 8:(h + 1) * 8], PKU[:]),
                        reads=["ppku", "ppkf"], writes=["ppkf"])
                p.emit("dve", lambda v: v.tensor_scalar(
                    out=ABI[:], in0=PKF[:], scalar1=0.125, scalar2=-0.4375,
                    op0=ALU.mult, op1=ALU.add),
                    reads=["ppkf", "pabi"], writes=["pabi"])
                p.emit("dve", lambda v: v.tensor_copy(AB2[:, 0:32], ABI[:]),
                    reads=["pabi", "pab2"], writes=["pab2"])
                p.emit("dve", lambda v: v.scalar_tensor_tensor(
                    out=AB2[:, 32:64], in0=AB2[:, 0:32], scalar=-8.0,
                    in1=PKF[:], op0=ALU.mult, op1=ALU.add),
                    reads=["ppkf", "pab2"], writes=["pab2"])
                p.emit("dve", lambda v: v.memset(SEL[:], 0.0),
                       reads=["psel"], writes=["psel"])
                ix_b = bass.AP(IX1[:].tensor, IX1[:].offset,
                               [IX1[:].ap[0], [8, 8], [0, 8]])
                for j in range(8):
                    ix_j = bass.AP(IX1[:].tensor, IX1[:].offset + j,
                                   [IX1[:].ap[0], [8, 8], [0, 8]])
                    p.emit("dve", lambda v, j=j: v.tensor_single_scalar(
                        MSK[:], AB2[:], float(j), ALU.is_equal),
                        reads=["pab2", "pmsk"], writes=["pmsk"])
                    p.emit("dve", lambda v, ix_j=ix_j: v.tensor_tensor(
                        TMP[:].rearrange("p (a b) -> p a b", a=8),
                        MSK[:].rearrange("p (a b) -> p a b", a=8),
                        ix_j, ALU.mult),
                        reads=["pmsk", "pix1", "ptmp"], writes=["ptmp"])
                    p.emit("dve", lambda v: v.tensor_tensor(
                        SEL[:], SEL[:], TMP[:], ALU.add),
                        reads=["ptmp", "psel"], writes=["psel"])
                p.emit("dve", lambda v: v.scalar_tensor_tensor(
                    out=EIDF[:], in0=SEL[:, 0:32], scalar=64.0,
                    in1=SEL[:, 32:64], op0=ALU.mult, op1=ALU.add),
                    reads=["psel", "peidf"], writes=["peidf"])
                tok_eid = p.emit("dve", lambda v: v.tensor_copy(
                    EID[:], EIDF[:]), reads=["peidf", "peid"],
                    writes=["peid"], wait=state["eid_free"],
                    inc=("DVE", 1))
                tok_ex = p.emit("act", lambda a: a.activation(
                    out=PP[:, 0:32], in_=SC2[:], func=AF.Exp),
                    reads=["psc2", "ppp"], writes=["ppp"],
                    wait=[("DVE", tok_eid)], inc=("ACT", 1))
                for h in range(H):
                    p.emit("dve", lambda v, h=h: v.reduce_sum(
                        PP[:, 32 + h:33 + h], PP[:, h * 8:(h + 1) * 8],
                        axis=AX.X),
                        reads=["ppp"], writes=["ppp2"],
                        wait=[("ACT", tok_ex)])
                p.emit("dve", lambda v: v.reciprocal(
                    PP[:, 40:44], PP[:, 32:36]), reads=["ppp2"],
                    writes=["ppp2"])
                for h in range(H):
                    p.emit("dve", lambda v, h=h: v.tensor_scalar_mul(
                        PP[:, h * 8:(h + 1) * 8], PP[:, h * 8:(h + 1) * 8],
                        PP[:, 40 + h:41 + h]),
                        reads=["ppp", "ppp2"], writes=["ppp"])
                for sl in range(32):
                    buf = sl % 8
                    tok_g = p.emit(
                        "gp", lambda g, sl=sl, buf=buf: g.indirect_dma_start(
                            out=DEG[buf][:], out_offset=None,
                            in_=deb.ap()[:, :],
                            in_offset=IndirectOffsetOnAxis(
                                EID[:, sl:sl + 1], 0)),
                        dma=True,
                        wait=[("DVE", tok_eid)] + state["deg_free"][buf],
                        inc=(f"DE{buf}", 16))
                    tok_w = p.emit(
                        "dve", lambda v, sl=sl, lt=lt, buf=buf:
                        v.scalar_tensor_tensor(
                            out=JNK[:], in0=XLNS[:, lt * 1024:
                                                 (lt + 1) * 1024],
                            scalar=1.0, in1=DEG[buf][:], op0=ALU.mult,
                            op1=ALU.mult,
                            accum_out=WCOL[:, sl:sl + 1]),
                        reads=["pxlns", f"pdeg{buf}", "pjnk", "pwcol"],
                        writes=["pjnk", "pwcol"],
                        wait=[(f"DE{buf}", tok_g), ("DVE", tok_xlns)],
                        inc=("DVE", 1))
                    state["deg_free"][buf] = [("DVE", tok_w)]
                tok_wdots = p.tok("DVE")
                tok_sw = p.emit("act", lambda a: a.activation(
                    out=WPR[:], in_=WCOL[:], func=AF.Silu),
                    reads=["pwcol", "pwpr"], writes=["pwpr"],
                    wait=[("DVE", tok_wdots)], inc=("ACT", 1))
                tok_wp = p.emit("dve", lambda v: v.tensor_tensor(
                    WPR[:], WPR[:], PP[:, 0:32], ALU.mult),
                    reads=["pwpr", "ppp"], writes=["pwpr"],
                    wait=[("ACT", tok_sw)], inc=("DVE", 1))
                p.emit("dve", lambda v: v.memset(EST[:], 0.0),
                       reads=["pest"], writes=["pest"],
                       wait=state["est_free"])
                for sl in range(32):
                    buf = sl % 8
                    tok_g = p.emit(
                        "gp", lambda g, sl=sl, buf=buf:
                        g.indirect_dma_start(
                            out=UEG[buf][:], out_offset=None,
                            in_=ueb.ap()[:, :],
                            in_offset=IndirectOffsetOnAxis(
                                EID[:, sl:sl + 1], 0)),
                        dma=True,
                        wait=[("DVE", tok_eid)] + state["ueg_free"][buf],
                        inc=(f"UE{buf}", 16))
                    tok_es = p.emit(
                        "dve", lambda v, sl=sl, buf=buf:
                        v.scalar_tensor_tensor(
                            out=EST[:], in0=UEG[buf][:],
                            scalar=WPR[:, sl:sl + 1], in1=EST[:],
                            op0=ALU.mult, op1=ALU.add),
                        reads=[f"pueg{buf}", "pwpr", "pest"],
                        writes=["pest"],
                        wait=[(f"UE{buf}", tok_g), ("DVE", tok_wp)],
                        inc=("DVE", 1))
                    state["ueg_free"][buf] = [("DVE", tok_es)]
                state["eid_free"] = [(f"UE{b}", p.tok(f"UE{b}"))
                                     for b in range(8)]
                tok_est = p.tok("DVE")
                _setph("D")
                last = None
                for fc in range(NFC):
                    for pf in range(4 if fc == 0 else 1):
                        nxt = fc + pf + (0 if fc == 0 else 3)
                        if 0 < nxt < NFC and nxt not in wdtoks:
                            wdtoks[nxt] = load_wd_fc(nxt)
                    tok_wdc = wdtoks.pop(fc)
                    buf = fc % 4
                    for n2 in range(2):
                        last = p.emit(
                            "pe", lambda t, fc=fc, n2=n2, buf=buf, lt=lt:
                            t.matmul(
                                cm.PSM[:, n2 * 512:(n2 + 1) * 512],
                                GATED[:, fc * 1024 + lt * 128:
                                      fc * 1024 + (lt + 1) * 128],
                                WDF[buf][:, n2 * 512:(n2 + 1) * 512],
                                start=(fc == 0), stop=False),
                            reads=["pgated", f"pwdf{buf}"], writes=["psm"],
                            wait=[tok_wdc, ("DVE", tok_gated)] +
                                 cm.psm_free,
                            inc=("PE", 1) if n2 == 1 else None)
                    state["wdf_free"][buf] = [("PE", p.tok("PE"))]
                wdtoks = {0: load_wd_fc(0)} if lt + 1 < 8 else {}
                state["wdf_free"][0] = [("PE", p.tok("PE"))]
                for n2 in range(2):
                    last = p.emit(
                        "pe", lambda t, n2=n2, lt=lt: t.matmul(
                            cm.PSM[:, n2 * 512:(n2 + 1) * 512],
                            cm.ONES[0:1, lt * 128:(lt + 1) * 128],
                            BDR[0:1, n2 * 512:(n2 + 1) * 512],
                            start=False, stop=(n2 == 1)),
                        reads=["pbdr", "ones"], writes=["psm"],
                        wait=[("CAST", tok_cast), ("GPC",
                                                   cm.tok_setup_gp)],
                        inc=("PE", 1) if n2 == 1 else None)
                par = ti % 2
                RT = cm.RT[par]
                DL = DELTA[par]
                tok_dl = p.emit("act", lambda a, DL=DL: a.copy(
                    DL[:], cm.PSM[:, 0:1024]),
                    reads=["psm", f"pdelta{par}"], writes=[f"pdelta{par}"],
                    wait=[("PE", last)] + state["delta_free"][par],
                    inc=("ACT", 1))
                cm.psm_free = [("ACT", tok_dl)]
                tok_ld = p.emit(
                    "sp", lambda s, RT=RT, ti=ti, rin_seq=rin_seq:
                    s.dma_start(
                        out=RT[:], in_=rin_seq[ti * 128:(ti + 1) * 128, :]),
                    dma=True, wait=cm.rt_free[par] + cm.rt_store[par],
                    inc=(f"RL{par}", 16))
                tok_a1 = p.emit("dve", lambda v, RT=RT, DL=DL: v.tensor_tensor(
                    RT[:], RT[:], DL[:], ALU.add),
                    reads=[f"RT{par}", f"pdelta{par}"], writes=[f"RT{par}"],
                    wait=[("ACT", tok_dl), (f"RL{par}", tok_ld)],
                    inc=("DVE", 1))
                state["delta_free"][par] = [("DVE", tok_a1)]
                tok_a2 = p.emit("dve", lambda v, RT=RT: v.tensor_tensor(
                    RT[:], RT[:], EST[:], ALU.add),
                    reads=[f"RT{par}", "pest"], writes=[f"RT{par}"],
                    wait=[("DVE", tok_est)], inc=("DVE", 1))
                state["est_free"] = [("DVE", tok_a2)]
                cm.rt_free[par] = [("DVE", tok_a2)]
                tok_st = p.emit(
                    "sp", lambda s, RT=RT, ti=ti, rout_seq=rout_seq:
                    s.dma_start(
                        out=rout_seq[ti * 128:(ti + 1) * 128, :], in_=RT[:]),
                    dma=True, wait=[("DVE", tok_a2)], inc=(f"ST{par}", 16))
                cm.rt_store[par] = [(f"ST{par}", tok_st)]
    p.barrier("sp", [("ST0", p.tok("ST0")), ("ST1", p.tok("ST1"))])


def build_l1_ffn():
    p = Prog()
    nc = p.nc
    rin = nc.dram_tensor("rin", [NSEQ * T, D], F32, kind="ExternalInput").ap()
    wg = nc.dram_tensor("wg", [D, 4096], F32, kind="ExternalInput").ap()
    bg = nc.dram_tensor("bg", [4096], F32, kind="ExternalInput").ap()
    wu = nc.dram_tensor("wu", [D, 4096], F32, kind="ExternalInput").ap()
    bu = nc.dram_tensor("bu", [4096], F32, kind="ExternalInput").ap()
    wd = nc.dram_tensor("wd", [4096, D], F32, kind="ExternalInput").ap()
    bd = nc.dram_tensor("bd", [D], F32, kind="ExternalInput").ap()
    wqp = nc.dram_tensor("wq", [D, 512], F32, kind="ExternalInput").ap()
    keys = nc.dram_tensor("keys", [4, 64, 2, 64], F32,
                          kind="ExternalInput").ap()
    de = nc.dram_tensor("down_embed", [4096, D], F32,
                        kind="ExternalInput").ap()
    ue = nc.dram_tensor("up_embed", [4096, D], F32,
                        kind="ExternalInput").ap()
    cstf = nc.dram_tensor("cstf", [128 * 128], F32, kind="ExternalInput").ap()
    rout = nc.dram_tensor("rout", [NSEQ * T, D], F32,
                          kind="ExternalOutput").ap()
    cm = Common(p, xt_tokens=1024)
    cm.emit_setup(cstf)
    emit_l1_ffn(p, cm, rin, rout, wg, bg,
                wu, bu, wd, bd, wqp,
                keys.rearrange("h k t n -> (h k) (t n)"), de, ue)
    p.finish()
    return p



def emit_l1_attn_a(p: Prog, cm: Common, rin, rout, wq, wk, wv):
    XT = cm.XT
    DH = 64
    wqb, _ = cast_weight(p, "aawqb", wq, [DH, DH])
    wkb, _ = cast_weight(p, "aawkb", wk, [DH, DH])
    wvb, _ = cast_weight(p, "aawvb", wv, [DH, DH])
    tok_cast = p.tok("CAST")

    scl = p.dram("aa_scl", [NSEQ, 16 * T], F32)
    lmax = p.dram("aa_lmax", [128, 256], F32)
    gmax = p.dram("aa_gmax", [128, 256], F32)
    lsum = p.dram("aa_lsum", [128, 256], F32)
    gsum = p.dram("aa_gsum", [128, 256], F32)
    prb = p.dram("aa_prb", [NSEQ, 16 * T], F32)

    WS = p.abuf("aaWS", [64, 3 * DH], BF16)
    WT = [p.abuf(f"aaWT{i}", [128, 3 * DH], BF16) for i in range(2)]
    QA = p.abuf("aaQA", [64, T], F32)
    PROD = p.abuf("aaPROD", [64, T], BF16)
    ONEC = p.abuf("aaONEC", [64, 1], BF16)
    SR = p.abuf("aaSR", [1, T], F32)
    VS = p.abuf("aaVS", [64, 2], F32)
    VSB = p.abuf("aaVSB", [64, 1], BF16)
    VSR = p.abuf("aaVSR", [1, 16 * DH * NSEQ], BF16)
    PRB = p.abuf("aaPRB", [1, 16 * T], BF16)
    SA = p.abuf("aaSA", [128, 256], F32)
    SB = p.abuf("aaSB", [128, 256 * 5], F32)

    p.emit("sp", lambda s: s.dma_start(out=WS[:, 0:64], in_=wqb.ap()),
           dma=True, wait=[("CAST", tok_cast)], inc=("WO", 16))
    p.emit("sp", lambda s: s.dma_start(out=WS[:, 64:128], in_=wkb.ap()),
           dma=True, wait=[("CAST", tok_cast)], inc=("WO", 16))
    p.emit("sp", lambda s: s.dma_start(out=WS[:, 128:192], in_=wvb.ap()),
           dma=True, wait=[("CAST", tok_cast)], inc=("WO", 16))
    tok_ws = p.tok("WO")
    for i in range(2):
        p.emit("dve", lambda v, i=i: v.memset(WT[i][:], 0.0),
               writes=[f"aawt{i}"], inc=("DVE", 1))
    p.emit("gp", lambda g: g.memset(ONEC[:], 1.0), writes=["aaonec"],
           inc=("GPC", 1))
    tok_one = p.tok("GPC")
    tok_wz = p.tok("DVE")
    for m in range(3):
        for half in range(2):
            pb = half * 64
            tok_tp = p.emit("pe", lambda t, m=m, pb=pb: t.transpose(
                cm.PST[0][pb:pb + 64, 0:64],
                WS[0:64, m * 64:(m + 1) * 64], cm.IDENT[0:64, 0:64]),
                reads=["aaws", "ident"], writes=["pst0"],
                wait=[("WO", tok_ws), ("GPC", cm.tok_setup_gp)] +
                     cm.pst_free[0],
                inc=("PE", 1))
            tok_ev = p.emit("act", lambda a, m=m, pb=pb, half=half: a.copy(
                WT[half][pb:pb + 64, m * 64:(m + 1) * 64],
                cm.PST[0][pb:pb + 64, 0:64]),
                reads=["pst0"], writes=[f"aawt{half}"],
                wait=[("PE", tok_tp), ("DVE", tok_wz)], inc=("ACT", 1))
            cm.pst_free[0] = [("ACT", tok_ev)]
    tok_wt = p.tok("ACT")

    sc_store = []
    sr_free = []
    for c in range(NSEQ):
        rin_seq = rin[c * T:(c + 1) * T, :]
        cm.emit_ln_seq(rin_seq)
        tok_xt = p.tok("ACT")
        for h in range(16):
            par = h % 2
            chunk = h // 2
            last = None
            for n4 in range(4):
                last = p.emit("pe", lambda t, n4=n4, par=par, chunk=chunk:
                              t.matmul(
                    cm.PSM[0:64, n4 * 512:(n4 + 1) * 512],
                    WT[par][:, 0:64],
                    XT[:, chunk * T + n4 * 512:chunk * T + (n4 + 1) * 512],
                    start=True, stop=True),
                    reads=["xt", f"aawt{par}"], writes=["psm"],
                    wait=[("ACT", tok_wt), ("ACT", tok_xt)] + cm.psm_free,
                    inc=("PE", 1) if n4 == 3 else None)
            tok_q = p.emit("dve", lambda v: v.tensor_copy(
                QA[:], cm.PSM[0:64, 0:T]), reads=["psm", "aaqa"],
                writes=["aaqa"], wait=[("PE", last)], inc=("DVE", 1))
            cm.psm_free = [("DVE", tok_q)]
            last = None
            for n4 in range(4):
                last = p.emit("pe", lambda t, n4=n4, par=par, chunk=chunk:
                              t.matmul(
                    cm.PSM[0:64, n4 * 512:(n4 + 1) * 512],
                    WT[par][:, 64:128],
                    XT[:, chunk * T + n4 * 512:chunk * T + (n4 + 1) * 512],
                    start=True, stop=True),
                    reads=["xt", f"aawt{par}"], writes=["psm"],
                    wait=cm.psm_free, inc=("PE", 1) if n4 == 3 else None)
            tok_pr = p.emit("dve", lambda v: v.tensor_tensor(
                PROD[:], QA[:], cm.PSM[0:64, 0:T], ALU.mult),
                reads=["psm", "aaqa", "aaprod"], writes=["aaprod"],
                wait=[("PE", last)], inc=("DVE", 1))
            cm.psm_free = [("DVE", tok_pr)]
            last = None
            for n4 in range(4):
                last = p.emit("pe", lambda t, n4=n4: t.matmul(
                    cm.PSM[0:1, n4 * 512:(n4 + 1) * 512],
                    ONEC[:], PROD[:, n4 * 512:(n4 + 1) * 512],
                    start=True, stop=True),
                    reads=["aaprod", "aaonec"], writes=["psm"],
                    wait=[("GPC", tok_one), ("DVE", tok_pr)] + cm.psm_free,
                    inc=("PE", 1) if n4 == 3 else None)
            tok_sr = p.emit("act", lambda a: a.activation(
                out=SR[:], in_=cm.PSM[0:1, 0:T], func=AF.Copy,
                scale=1.0 / 32.0),
                reads=["psm", "aasr"], writes=["aasr"],
                wait=[("PE", last)] + sr_free, inc=("ACT", 1))
            cm.psm_free = [("ACT", tok_sr)]
            tok_scst = p.emit("sp", lambda s, c=c, h=h: s.dma_start(
                out=scl.ap()[c, h * T:(h + 1) * T].rearrange(
                    "(a b) -> a b", a=1),
                in_=SR[:]), dma=True, wait=[("ACT", tok_sr)],
                inc=("SC", 16))
            sc_store.append(tok_scst)
            sr_free = [("SC", tok_scst)]
            last = None
            for n4 in range(4):
                last = p.emit("pe", lambda t, n4=n4, par=par, chunk=chunk:
                              t.matmul(
                    cm.PSM[0:64, n4 * 512:(n4 + 1) * 512],
                    WT[par][:, 128:192],
                    XT[:, chunk * T + n4 * 512:chunk * T + (n4 + 1) * 512],
                    start=True, stop=True),
                    reads=["xt", f"aawt{par}"], writes=["psm"],
                    wait=cm.psm_free, inc=("PE", 1) if n4 == 3 else None)
            p.emit("dve", lambda v: v.reduce_sum(
                VS[:, 0:1], cm.PSM[0:64, 0:T], axis=AX.X),
                reads=["psm", "aavs"], writes=["aavs"],
                wait=[("PE", last)], inc=("DVE", 1))
            tok_vsb = p.emit("dve", lambda v: v.tensor_copy(
                VSB[:], VS[:, 0:1]), reads=["aavs", "aavsb"],
                writes=["aavsb"], inc=("DVE", 1))
            cm.psm_free = [("DVE", tok_vsb)]
            tok_vt = p.emit("pe", lambda t: t.transpose(
                cm.PST[1][0:1, 0:64], VSB[:], cm.IDENT[0:64, 0:64]),
                reads=["aavsb", "ident"], writes=["pst1"],
                wait=[("DVE", tok_vsb)] + cm.pst_free[1], inc=("PE", 1))
            tok_vr = p.emit("act", lambda a, c=c, h=h: a.copy(
                VSR[:, (c * 16 + h) * DH:(c * 16 + h + 1) * DH],
                cm.PST[1][0:1, 0:64]),
                reads=["pst1"], writes=["aavsr"],
                wait=[("PE", tok_vt)], inc=("ACT", 1))
            cm.pst_free[1] = [("ACT", tok_vr)]
    tok_ph1_dma = p.tok("SC")
    tok_ph1_act = p.tok("ACT")

    p.emit("sp", lambda s: s.dma_start(
        out=SA[:], in_=scl.ap()[0, :].rearrange("(a b) -> a b", a=128)),
        dma=True, wait=[("SC", tok_ph1_dma)], inc=("WO", 16))
    p.emit("sp", lambda s: s.dma_start(
        out=SB[:, 0:256], in_=scl.ap()[1, :].rearrange("(a b) -> a b",
                                                       a=128)),
        dma=True, wait=[("SC", tok_ph1_dma)], inc=("WO", 16))
    tok_s2 = p.tok("WO")
    tok_m = p.emit("dve", lambda v: v.tensor_max(
        SB[:, 256:512], SA[:], SB[:, 0:256]),
        reads=["aasa", "aasb"], writes=["aasb"],
        wait=[("WO", tok_s2)], inc=("DVE", 1))
    p.emit("sp", lambda s: s.dma_start(
        out=lmax.ap(), in_=SB[:, 256:512]),
        dma=True, wait=[("DVE", tok_m)], inc=("SC", 16))
    tok_lm = p.tok("SC")
    tok_cc1 = p.emit("gp", lambda g: g.collective_compute(
        "AllReduce", ALU.max, replica_groups=[list(range(NCORES))],
        ins=[lmax.ap().opt()], outs=[gmax.ap().opt()]),
        wait=[("SC", tok_lm)], inc=("CC", 1))
    p.emit("sp", lambda s: s.dma_start(
        out=SB[:, 256:512], in_=gmax.ap()),
        dma=True, wait=[("CC", tok_cc1)], inc=("WO", 16))
    tok_gm = p.tok("WO")
    for c in range(NSEQ):
        src_t = SA if c == 0 else None
        p.emit("dve", lambda v, c=c: v.tensor_tensor(
            SB[:, (3 + c) * 256:(4 + c) * 256],
            (SA[:] if c == 0 else SB[:, 0:256]), SB[:, 256:512],
            ALU.subtract),
            reads=["aasa", "aasb"], writes=["aasb"],
            wait=[("WO", tok_gm)], inc=("DVE", 1))
        p.emit("act", lambda a, c=c: a.activation(
            out=SB[:, (3 + c) * 256:(4 + c) * 256],
            in_=SB[:, (3 + c) * 256:(4 + c) * 256], func=AF.Exp),
            reads=["aasb"], writes=["aasb2"],
            wait=[("DVE", p.tok("DVE"))], inc=("ACT", 1))
    tok_exps = p.tok("ACT")
    tok_ls = p.emit("dve", lambda v: v.tensor_add(
        SB[:, 512:768], SB[:, 768:1024], SB[:, 1024:1280]),
        reads=["aasb", "aasb2"], writes=["aasb"],
        wait=[("ACT", tok_exps)], inc=("DVE", 1))
    p.emit("sp", lambda s: s.dma_start(
        out=lsum.ap(), in_=SB[:, 512:768]),
        dma=True, wait=[("DVE", tok_ls)], inc=("SC", 16))
    tok_ls2 = p.tok("SC")
    tok_cc2 = p.emit("gp", lambda g: g.collective_compute(
        "AllReduce", ALU.add, replica_groups=[list(range(NCORES))],
        ins=[lsum.ap().opt()], outs=[gsum.ap().opt()]),
        wait=[("SC", tok_ls2)], inc=("CC", 1))
    p.emit("sp", lambda s: s.dma_start(
        out=SB[:, 512:768], in_=gsum.ap()),
        dma=True, wait=[("CC", tok_cc2)], inc=("WO", 16))
    tok_gs = p.tok("WO")
    tok_rc = p.emit("dve", lambda v: v.reciprocal(
        SA[:], SB[:, 512:768]), reads=["aasb"], writes=["aasa"],
        wait=[("WO", tok_gs)], inc=("DVE", 1))
    for c in range(NSEQ):
        tok_pc = p.emit("dve", lambda v, c=c: v.tensor_mul(
            SB[:, (3 + c) * 256:(4 + c) * 256],
            SB[:, (3 + c) * 256:(4 + c) * 256], SA[:]),
            reads=["aasa", "aasb", "aasb2"], writes=["aasb"],
            wait=[("DVE", tok_rc)], inc=("DVE", 1))
        p.emit("sp", lambda s, c=c: s.dma_start(
            out=prb.ap()[c, :].rearrange("(a b) -> a b", a=128),
            in_=SB[:, (3 + c) * 256:(4 + c) * 256]),
            dma=True, wait=[("DVE", tok_pc)], inc=("SC", 16))
    tok_prb_st = p.tok("SC")

    for c in range(NSEQ):
        rin_seq = rin[c * T:(c + 1) * T, :]
        rout_seq = rout[c * T:(c + 1) * T, :]
        for h in range(16):
            tok_l = p.emit("sp", lambda s, c=c, h=h: s.dma_start(
                out=SR[:], in_=prb.ap()[c, h * T:(h + 1) * T].rearrange(
                    "(a b) -> a b", a=1)),
                dma=True, wait=[("SC", tok_prb_st), ("ACT", p.tok("ACT"))],
                inc=("WO", 16))
            p.emit("act", lambda a, h=h: a.copy(
                PRB[:, h * T:(h + 1) * T], SR[:]),
                reads=["aasr"], writes=["aaprb"],
                wait=[("WO", tok_l)], inc=("ACT", 1))
        tok_prb_all = p.tok("ACT")
        for tb in range(NT):
            par = tb % 2
            RT = cm.RT[par]
            tok_ld = p.emit(
                "sp", lambda s, RT=RT, tb=tb, rin_seq=rin_seq: s.dma_start(
                    out=RT[:], in_=rin_seq[tb * 128:(tb + 1) * 128, :]),
                dma=True, wait=cm.rt_free[par] + cm.rt_store[par],
                inc=(f"RL{par}", 16))
            lasts = [None, None]
            for h in range(16):
                bx = h // 8
                px = cm.PSX[bx]
                lasts[bx] = p.emit(
                    "pe", lambda t, h=h, tb=tb, c=c, px=px, bx=bx: t.matmul(
                        px[:, (h % 8) * 64:(h % 8 + 1) * 64],
                        PRB[0:1, h * T + tb * 128:h * T + (tb + 1) * 128],
                        VSR[0:1, (c * 16 + h) * DH:(c * 16 + h + 1) * DH],
                        start=(h % 8 == 0), stop=(h % 8 == 7)),
                    reads=["aaprb", "aavsr"], writes=[f"psx{bx}"],
                    wait=[("ACT", tok_prb_all), ("ACT", tok_ph1_act)] +
                         (cm.psx_free[bx] if h % 8 == 0 else []),
                    inc=("PE", 1) if h % 8 == 7 else None)
            tok_a1 = p.emit("dve", lambda v, RT=RT: v.tensor_tensor(
                RT[:, 0:512], RT[:, 0:512], cm.PSX[0][:], ALU.add),
                reads=[f"RT{par}", "psx0"], writes=[f"RT{par}"],
                wait=[("PE", lasts[0]), (f"RL{par}", tok_ld)],
                inc=("DVE", 1))
            tok_a2 = p.emit("dve", lambda v, RT=RT: v.tensor_tensor(
                RT[:, 512:1024], RT[:, 512:1024], cm.PSX[1][:], ALU.add),
                reads=[f"RT{par}", "psx1"], writes=[f"RT{par}"],
                wait=[("PE", lasts[1])], inc=("DVE", 1))
            cm.psx_free[0] = [("DVE", tok_a1)]
            cm.psx_free[1] = [("DVE", tok_a2)]
            cm.rt_free[par] = [("DVE", tok_a2)]
            tok_st = p.emit(
                "sp", lambda s, RT=RT, tb=tb, rout_seq=rout_seq: s.dma_start(
                    out=rout_seq[tb * 128:(tb + 1) * 128, :], in_=RT[:]),
                dma=True, wait=[("DVE", tok_a2)], inc=(f"ST{par}", 16))
            cm.rt_store[par] = [(f"ST{par}", tok_st)]
    p.barrier("sp", [("ST0", p.tok("ST0")), ("ST1", p.tok("ST1"))])


def build_l1_attn_a():
    p = Prog()
    nc = p.nc
    rin = nc.dram_tensor("rin", [NSEQ * T, D], F32, kind="ExternalInput").ap()
    wq = nc.dram_tensor("wq_a", [64, 64], F32, kind="ExternalInput").ap()
    wk = nc.dram_tensor("wk_a", [64, 64], F32, kind="ExternalInput").ap()
    wv = nc.dram_tensor("wv_a", [64, 64], F32, kind="ExternalInput").ap()
    cstf = nc.dram_tensor("cstf", [128 * 128], F32, kind="ExternalInput").ap()
    rout = nc.dram_tensor("rout", [NSEQ * T, D], F32,
                          kind="ExternalOutput").ap()
    cm = Common(p)
    cm.emit_setup(cstf)
    emit_l1_attn_a(p, cm, rin, rout, wq, wk, wv)
    p.finish()
    return p


def build_l0_attn_a():
    p = Prog()
    nc = p.nc
    rin = nc.dram_tensor("rin", [NSEQ * T, D], F32, kind="ExternalInput").ap()
    watt = nc.dram_tensor("w_attn", [3 * D, D], F32,
                          kind="ExternalInput").ap()
    wproj = nc.dram_tensor("w_proj", [D, D], F32, kind="ExternalInput").ap()
    cstf = nc.dram_tensor("cstf", [128 * 128], F32, kind="ExternalInput").ap()
    cstb = nc.dram_tensor("cstb", [CSTB_LEN], BF16, kind="ExternalInput").ap()
    rout = nc.dram_tensor("rout", [NSEQ * T, D], F32,
                          kind="ExternalOutput").ap()
    cm = Common(p)
    cm.emit_setup(cstf)
    emit_mha16(p, cm, rin, rout, watt, wproj, True, cstb, "aa")
    p.finish()
    return p


def build_l1_attn_b():
    p = Prog()
    nc = p.nc
    rin = nc.dram_tensor("rin", [NSEQ * T, D], F32, kind="ExternalInput").ap()
    wqkv = nc.dram_tensor("w_qkv", [3, 16, 64, D], F32,
                          kind="ExternalInput").ap()
    wo = nc.dram_tensor("w_o", [D, D], F32, kind="ExternalInput").ap()
    cstf = nc.dram_tensor("cstf", [128 * 128], F32, kind="ExternalInput").ap()
    rout = nc.dram_tensor("rout", [NSEQ * T, D], F32,
                          kind="ExternalOutput").ap()
    cm = Common(p)
    cm.emit_setup(cstf)
    emit_mha16(p, cm, rin, rout,
               wqkv.rearrange("q h d c -> (q h d) c"), wo, False, None, "bb")
    p.finish()
    return p


def build_fused():
    p = Prog()
    p.init_arena(212736)
    nc = p.nc
    A = lambda n, s: nc.dram_tensor(n, list(s), F32,
                                    kind="ExternalInput").ap()
    rin = A("rin", [NSEQ * T, D])
    w_attn = A("w_attn", [3 * D, D])
    w_proj = A("w_proj", [D, D])
    gate_w = A("gate_w", [8, D])
    w1 = A("w1", [8, 2048, D])
    w2 = A("w2", [8, D, 2048])
    w3 = A("w3", [8, 2048, D])
    wq_b = A("wq", [D, 128])
    wk_b = A("wk", [D, 128])
    wv_b = A("wv", [D, 128])
    wo_b = A("wo", [128, D])
    wq_a = A("wq_a", [64, 64])
    wk_a = A("wk_a", [64, 64])
    wv_a = A("wv_a", [64, 64])
    w_qkv = A("w_qkv", [3, 16, 64, D])
    w_o = A("w_o", [D, D])
    wg = A("wg", [D, 4096])
    bg = A("bg", [4096])
    wu = A("wu", [D, 4096])
    bu = A("bu", [4096])
    wd = A("wd", [4096, D])
    bd = A("bd", [D])
    wqp = A("wq_p", [D, 512])
    keys = A("keys", [4, 64, 2, 64])
    de = A("down_embed", [4096, D])
    ue = A("up_embed", [4096, D])
    cstf = nc.dram_tensor("cstf", [128 * 128], F32, kind="ExternalInput").ap()
    cstb = nc.dram_tensor("cstb", [CSTB_LEN], BF16, kind="ExternalInput").ap()
    rout = nc.dram_tensor("rout", [NSEQ * T, D], F32,
                          kind="ExternalOutput").ap()
    RA = nc.dram_tensor("resid_a", [NSEQ * T, D], F32).ap()
    RB = nc.dram_tensor("resid_b", [NSEQ * T, D], F32).ap()

    cm = Common(p)
    cm.emit_setup(cstf)

    steps = [
        (T, lambda i, o: emit_mha16(p, cm, i, o, w_attn, w_proj, True,
                                    cstb, "aa")),
        (1024, lambda i, o: emit_l0_ffn(
            p, cm, i, o, gate_w, w1.rearrange("e f c -> (e f) c"),
            w2.rearrange("e d f -> (e d) f"),
            w3.rearrange("e f c -> (e f) c"))),
        (T, lambda i, o: emit_l0_attn_b(p, cm, i, o, wq_b, wk_b, wv_b,
                                        wo_b)),
        (1024, lambda i, o: emit_l0_ffn(
            p, cm, i, o, gate_w, w1.rearrange("e f c -> (e f) c"),
            w2.rearrange("e d f -> (e d) f"),
            w3.rearrange("e f c -> (e f) c"))),
        (T, lambda i, o: emit_l1_attn_a(p, cm, i, o, wq_a, wk_a, wv_a)),
        (1024, lambda i, o: emit_l1_ffn(
            p, cm, i, o, wg, bg, wu, bu, wd, bd, wqp,
            keys.rearrange("h k t n -> (h k) (t n)"), de, ue)),
        (T, lambda i, o: emit_mha16(p, cm, i, o,
                                    w_qkv.rearrange("q h d c -> (q h d) c"),
                                    w_o, False, None, "bb")),
        (1024, lambda i, o: emit_l1_ffn(
            p, cm, i, o, wg, bg, wu, bu, wd, bd, wqp,
            keys.rearrange("h k t n -> (h k) (t n)"), de, ue)),
    ]
    chain = [rin, RA, RB, RA, RB, RA, RB, RA, rout]
    for i, (xtk, fn) in enumerate(steps):
        p.ascope()
        p.name_prefix = f"s{i}_"
        p.full_fence()
        cm.set_xt(xtk)
        fn(chain[i], chain[i + 1])
        p.pe_drain_on = False
        p.aend()
    p.name_prefix = ""
    n_ops = sum(len(v) for v in p.ops.values())
    p.finish()
    p._n_ops = n_ops
    return p


_PROG_CACHE = {}


def _get_prog(name):
    if name not in _PROG_CACHE:
        _PROG_CACHE[name] = {
            "l0aa": build_l0_attn_a,
            "l0f": build_l0_ffn,
            "l0ab": build_l0_attn_b,
            "l1aa": build_l1_attn_a,
            "l1f": build_l1_ffn,
            "l1ab": build_l1_attn_b,
        }[name]()
    return _PROG_CACHE[name]


def kernel(**inputs):
    f32 = lambda k: np.ascontiguousarray(np.asarray(inputs[k], np.float32))
    x = f32("L0_attn_a_x").reshape(16 * T, D)
    cstf, cstb = make_consts()
    if "fused" not in _PROG_CACHE:
        _PROG_CACHE["fused"] = build_fused()
    p = _PROG_CACHE["fused"]
    wmap = {
        "w_attn": f32("L0_attn_a_w_attn"), "w_proj": f32("L0_attn_a_w_proj"),
        "gate_w": f32("L0_ffn_gate_w"), "w1": f32("L0_ffn_w1"),
        "w2": f32("L0_ffn_w2"), "w3": f32("L0_ffn_w3"),
        "wq": f32("L0_attn_b_Wq"), "wk": f32("L0_attn_b_Wk"),
        "wv": f32("L0_attn_b_Wv"), "wo": f32("L0_attn_b_Wo"),
        "wq_a": f32("L1_attn_a_Wq"), "wk_a": f32("L1_attn_a_Wk"),
        "wv_a": f32("L1_attn_a_Wv"),
        "w_qkv": f32("L1_attn_b_W_qkv"), "w_o": f32("L1_attn_b_W_o"),
        "wg": f32("L1_ffn_Wg"), "bg": f32("L1_ffn_bg"),
        "wu": f32("L1_ffn_Wu"), "bu": f32("L1_ffn_bu"),
        "wd": f32("L1_ffn_Wd"), "bd": f32("L1_ffn_bd"),
        "wq_p": f32("L1_ffn_Wq"), "keys": f32("L1_ffn_keys"),
        "down_embed": f32("L1_ffn_down_embed"),
        "up_embed": f32("L1_ffn_up_embed"),
        "cstf": cstf, "cstb": cstb,
    }
    in_maps = []
    for c in range(NCORES):
        m = dict(wmap)
        m["rin"] = x[c * NSEQ * T:(c + 1) * NSEQ * T]
        in_maps.append(m)
    res = run_bass_kernel_spmd(p.nc, in_maps, core_ids=list(range(NCORES)))
    out = np.concatenate([r["rout"] for r in res.results], axis=0)
    return out.reshape(16, T, D).astype(np.float32)


# version 09
# speedup vs baseline: 1.0445x; 1.0445x over previous
import math
import numpy as np
import ml_dtypes
import concourse.bass as bass
import concourse.mybir as mybir
from concourse.bass import IndirectOffsetOnAxis
from concourse.bass_utils import run_bass_kernel_spmd

F32 = mybir.dt.float32
BF16 = mybir.dt.bfloat16
I32 = mybir.dt.int32
U32 = mybir.dt.uint32
AF = mybir.ActivationFunctionType
ALU = mybir.AluOpType
AX = mybir.AxisListType

NCORES = 8
T = 2048
D = 1024
NSEQ = 2
NT = T // 128
NC = D // 128

ENGINES = ("sp", "pe", "dve", "act", "gp")


class Prog:

    def __init__(self):
        self.nc = bass.Bass("TRN2", target_bir_lowering=False)
        self.ops = {e: [] for e in ENGINES}
        self.sems = {}
        self.counts = {}
        self.waited = {e: {} for e in ENGINES}
        self.dirty_w = {e: set() for e in ENGINES}
        self.dirty_r = {e: set() for e in ENGINES}
        self.AR = None
        self.ar_off = 0
        self.ar_mark = 0
        self.ar_peak = 0
        self.name_prefix = ""
        self.fence = []

    def init_arena(self, nbytes=204800):
        assert nbytes % 4 == 0
        self.AR = self.nc.alloc_sbuf_tensor("ARENA", [128, nbytes // 4], F32)
        self.ar_bytes = nbytes

    def abuf(self, name, shape, dtype):
        if self.AR is None:
            return self.nc.alloc_sbuf_tensor(
                self.name_prefix + name, list(shape), dtype)[:]
        P, n = shape
        esz = mybir.dt.size(dtype)
        off = (self.ar_off + 63) & ~63
        self.ar_off = off + n * esz
        self.ar_peak = max(self.ar_peak, self.ar_off)
        assert self.ar_off <= self.ar_bytes, (
            f"arena overflow at {name}: {self.ar_off} > {self.ar_bytes}")
        view = self.AR[:].bitcast(dtype)[0:P, off // esz:off // esz + n]
        return view

    def ascope(self):
        self.ar_mark = self.ar_off

    def aend(self):
        self.ar_off = self.ar_mark

    def full_fence(self):
        toks = [(s, self.counts[s]) for s in list(self.sems)
                if self.counts.get(s, 0) > 0]
        self.fence = list(toks)
        for eng in ENGINES:
            self.barrier(eng, toks)

    def sem(self, name):
        if name not in self.sems:
            self.sems[name] = self.nc.alloc_semaphore(name)
            self.counts[name] = 0
        return self.sems[name]

    def sbuf(self, name, shape, dtype):
        return self.nc.alloc_sbuf_tensor(name, list(shape), dtype)

    def psum(self, name, shape, dtype=F32):
        return self.nc.alloc_psum_tensor(name, list(shape), dtype)

    def dram(self, name, shape, dtype, kind="Internal"):
        return self.nc.dram_tensor(self.name_prefix + name, list(shape),
                                   dtype, kind=kind)

    def tok(self, semname):
        self.sem(semname)
        return self.counts[semname]

    def now(self, *semnames):
        return [(s, self.tok(s)) for s in semnames]

    def emit(self, eng, fn, reads=(), writes=(), wait=(), inc=None,
             dma=False, nodrain=False):
        waits = []
        for semname, val in wait:
            if val <= 0:
                continue
            self.sem(semname)
            if self.waited[eng].get(semname, 0) < val:
                waits.append((semname, val))
                self.waited[eng][semname] = val
        need_drain = False
        if eng == "pe" and not getattr(self, "pe_drain_on", False):
            nodrain = True
        if not nodrain:
            rset, wset = set(reads), set(writes)
            if dma:
                if rset & self.dirty_w[eng]:
                    need_drain = True
            elif (rset & self.dirty_w[eng]) or (
                    wset & (self.dirty_w[eng] | self.dirty_r[eng])):
                need_drain = True
        token = None
        if inc is not None:
            semname, k = inc
            self.sem(semname)
            self.counts[semname] += k
            token = self.counts[semname]
        sems = self.sems

        def closure(engine, waits=tuple(waits), need_drain=need_drain,
                    fn=fn, inc=inc):
            for semname, val in waits:
                engine.wait_ge(sems[semname], val)
            if need_drain:
                engine.drain()
            instr = fn(engine)
            if inc is not None:
                assert instr is not None, "inc requires fn returning instr"
                instr.then_inc(sems[inc[0]], inc[1])

        self.ops[eng].append(closure)
        if need_drain:
            self.dirty_w[eng].clear()
            self.dirty_r[eng].clear()
        if not dma:
            self.dirty_w[eng].update(writes)
            self.dirty_r[eng].update(reads)
        return token

    def barrier(self, eng, wait):
        self.emit(eng, lambda e: None, wait=wait, nodrain=True)

    def finish(self):
        nc, ops = self.nc, self.ops
        with nc.Block() as block:
            @block.sync
            def _(sync):
                for fn in ops["sp"]:
                    fn(sync)

            @block.tensor
            def _(tensor):
                for fn in ops["pe"]:
                    fn(tensor)

            @block.vector
            def _(vector):
                for fn in ops["dve"]:
                    fn(vector)

            @block.scalar
            def _(scalar):
                for fn in ops["act"]:
                    fn(scalar)

            @block.gpsimd
            def _(gpsimd):
                for fn in ops["gp"]:
                    fn(gpsimd)
        return nc



def alibi_slopes(n_heads):
    start = 2.0 ** (-2.0 ** (-(math.log2(n_heads) - 3)))
    return np.array([start * start ** i for i in range(n_heads)],
                    dtype=np.float32)


CSTB_QAUG = 0
CSTB_KAUG = 16 * 2 * T
CSTB_LEN = CSTB_KAUG + 2 * T


def make_consts():
    i = np.arange(128)
    mask = np.where(i[None, :] <= i[:, None], 0.0, -1e30).astype(np.float32)
    cstf = mask.reshape(-1)
    sl = alibi_slopes(16)
    qaug = np.zeros((16, 2, T), np.float32)
    qaug[:, 0, :] = (sl * 16.0)[:, None]
    qaug[:, 1, :] = sl[:, None]
    j = np.arange(T, dtype=np.float32)
    kaug = np.stack([np.floor(j / 16.0), j % 16.0], axis=0)
    cstb = np.concatenate([qaug.reshape(-1), kaug.reshape(-1)])
    return cstf, cstb.astype(ml_dtypes.bfloat16)



class Common:
    def __init__(self, p: Prog, xt_tokens=T):
        self.p = p
        self.xt_tokens = xt_tokens
        self.RT = [p.abuf(f"RT{i}", [128, 1024], F32) for i in range(2)]
        self.XLN = [p.abuf(f"XLN{i}", [128, 1024], BF16) for i in range(2)]
        self.ST = p.abuf("STAT", [128, 16], F32)
        self.XT = None
        if p.AR is None:
            self.set_xt(xt_tokens)
        self.CF = p.abuf("CMASK", [128, 128], F32)
        self.ONES = p.abuf("ONESR", [1, 2048], BF16)
        self.IDENT = p.abuf("IDENT", [128, 128], BF16)
        self.PSM = p.psum("PSM", [128, 2048], F32)
        self.PST = [p.psum(f"PST{i}", [128, 1024], BF16) for i in range(2)]
        self.PSX = [p.psum(f"PSX{i}", [128, 512], F32) for i in range(2)]
        self.psm_free = []
        self.pst_free = [[], []]
        self.psx_free = [[], []]
        self.rt_free = [[], []]
        self.rt_store = [[], []]
        self.xln_free = [[], []]
        self.st_free = []

    def set_xt(self, xt_tokens):
        self.xt_tokens = xt_tokens
        self.XT = self.p.abuf("XT", [128, NC * xt_tokens], BF16)

    def emit_setup(self, cstf_ap):
        p = self.p
        ID, ONES, CF = self.IDENT, self.ONES, self.CF
        p.emit("gp", lambda g: g.memset(ID[:], 1.0), writes=["ident"])
        p.emit("gp", lambda g: g.affine_select(
            out=ID[:], in_=ID[:], pattern=[[-1, 128]],
            compare_op=ALU.is_equal, fill=0.0, base=0, channel_multiplier=1),
            reads=["ident"], writes=["ident"], inc=("GPC", 1))
        p.emit("gp", lambda g: g.memset(ONES[:], 1.0), writes=["ones"],
               inc=("GPC", 1))
        self.tok_setup_gp = p.tok("GPC")
        p.emit("sp", lambda s: s.dma_start(
            out=CF[:], in_=cstf_ap.rearrange("(a b) -> a b", a=128)),
            dma=True, inc=("WO", 16))
        self.tok_setup_ld = p.tok("WO")

    def emit_ln_seq(self, rin_seq, extra_tile_cb=None, tile0=0,
                    ntiles=NT, xt_tokens=None):
        p = self.p
        ST = self.ST
        if xt_tokens is None:
            xt_tokens = self.xt_tokens
        for ti in range(tile0, tile0 + ntiles):
            par = ti % 2
            RT, XLN = self.RT[par], self.XLN[par]
            rkey, xkey = f"RT{par}", f"XLN{par}"
            tok_ld = p.emit(
                "sp", lambda s, RT=RT, ti=ti, rin_seq=rin_seq: s.dma_start(
                    out=RT[:], in_=rin_seq[ti * 128:(ti + 1) * 128, :]),
                dma=True, wait=self.rt_free[par] + self.rt_store[par],
                inc=(f"RL{par}", 16))
            p.emit("act", lambda a, RT=RT, XLN=XLN: a.activation(
                out=XLN[:], in_=RT[:], func=AF.Square,
                accum_out=ST[:, 0:1]),
                reads=[rkey], writes=[xkey, "st"],
                wait=[(f"RL{par}", tok_ld)] + self.xln_free[par] +
                     self.st_free)
            tok_sums = p.emit("act", lambda a, RT=RT, XLN=XLN: a.activation(
                out=XLN[:], in_=RT[:], func=AF.Identity,
                accum_out=ST[:, 1:2]),
                reads=[rkey], writes=[xkey, "st"], inc=("ACT", 1))
            p.emit("dve", lambda v: v.tensor_scalar_mul(
                ST[:, 2:3], ST[:, 1:2], 1.0 / 1024.0),
                reads=["st"], writes=["st"], wait=[("ACT", tok_sums)])
            p.emit("dve", lambda v: v.tensor_scalar(
                out=ST[:, 3:4], in0=ST[:, 2:3], scalar1=ST[:, 2:3],
                scalar2=-1e-5, op0=ALU.mult, op1=ALU.add),
                reads=["st"], writes=["st"])
            tok_var = p.emit("dve", lambda v: v.scalar_tensor_tensor(
                out=ST[:, 4:5], in0=ST[:, 0:1], scalar=1.0 / 1024.0,
                in1=ST[:, 3:4], op0=ALU.mult, op1=ALU.subtract),
                reads=["st"], writes=["st"], inc=("DVE", 1))
            p.emit("act", lambda a: a.activation(
                out=ST[:, 5:6], in_=ST[:, 4:5], func=AF.Ln),
                reads=["st"], writes=["st2"],
                wait=[("DVE", tok_var)])
            tok_rstd = p.emit("act", lambda a: a.activation(
                out=ST[:, 6:7], in_=ST[:, 5:6], func=AF.Exp,
                scale=-0.5), reads=["st2"], writes=["st2"], inc=("ACT", 1))
            tok_xln = p.emit("dve", lambda v, RT=RT, XLN=XLN: v.tensor_scalar(
                out=XLN[:], in0=RT[:], scalar1=ST[:, 2:3],
                scalar2=ST[:, 6:7], op0=ALU.subtract, op1=ALU.mult),
                reads=[rkey, "st", "st2"], writes=[xkey],
                wait=[("ACT", tok_rstd)], inc=("DVE", 1))
            self.st_free = [("DVE", tok_xln)]
            self.rt_free[par] = [("DVE", tok_xln)]
            if extra_tile_cb is not None:
                extra_tile_cb(ti, XLN, tok_xln)
            pst = self.PST[par]
            last = None
            for ci in range(NC):
                last = p.emit(
                    "pe", lambda t, ci=ci, XLN=XLN, pst=pst: t.transpose(
                        pst[:, ci * 128:(ci + 1) * 128],
                        XLN[:, ci * 128:(ci + 1) * 128], self.IDENT[:]),
                    reads=[xkey, "ident"], writes=[f"pst{par}"],
                    wait=([("DVE", tok_xln), ("GPC", self.tok_setup_gp)] +
                          self.pst_free[par]),
                    inc=("PE", 1) if ci == NC - 1 else None)
            xt_dst = self.XT[:, 0:NC * xt_tokens].rearrange(
                "p (c t) -> p c t", c=NC)[
                :, :, (ti - tile0) * 128:(ti - tile0 + 1) * 128]
            pst_src = pst[:].rearrange("p (c t) -> p c t", c=NC)[
                :, :, 0:128]
            tok_evac = p.emit("act", lambda a, d=xt_dst, s=pst_src: a.copy(
                d, s), reads=[f"pst{par}"], writes=["xt"],
                wait=[("PE", last)], inc=("ACT", 1))
            self.pst_free[par] = [("ACT", tok_evac)]
            self.xln_free[par] = [("ACT", tok_evac)]
        return ("ACT", p.tok("ACT"))

    def emit_residual_update(self, rin_seq, rout_seq, delta_cb):
        p = self.p
        for ti in range(NT):
            par = ti % 2
            RT = self.RT[par]
            tok_ld = p.emit(
                "sp", lambda s, RT=RT, ti=ti, rin_seq=rin_seq: s.dma_start(
                    out=RT[:], in_=rin_seq[ti * 128:(ti + 1) * 128, :]),
                dma=True, wait=self.rt_free[par] + self.rt_store[par],
                inc=(f"RL{par}", 16))
            dwait = delta_cb(ti)
            tok_add = p.emit("dve", lambda v, RT=RT: v.tensor_tensor(
                RT[:], RT[:], self.PSM[:, 0:1024], ALU.add),
                reads=[f"RT{par}", "psm"], writes=[f"RT{par}"],
                wait=dwait + [(f"RL{par}", tok_ld)], inc=("DVE", 1))
            self.psm_free = [("DVE", tok_add)]
            self.rt_free[par] = [("DVE", tok_add)]
            tok_st = p.emit(
                "sp", lambda s, RT=RT, ti=ti, rout_seq=rout_seq: s.dma_start(
                    out=rout_seq[ti * 128:(ti + 1) * 128, :], in_=RT[:]),
                dma=True, wait=[("DVE", tok_add)], inc=(f"ST{par}", 16))
            self.rt_store[par] = [(f"ST{par}", tok_st)]


def cast_weight(p, name, src_ap, shape):
    wb = p.dram(name, shape, BF16)
    tok = p.emit("gp", lambda g: g.dma_start(out=wb.ap(), in_=src_ap),
                 dma=True, inc=("CAST", 16))
    return wb, tok



def emit_mha16(p: Prog, cm: Common, rin, rout, watt, wproj, use_alibi,
               cstb, tag):
    XT = cm.XT
    NH, DH = 16, 64
    wattb, _ = cast_weight(p, f"{tag}_wattb", watt, [3 * D, D])
    wprob, _ = cast_weight(p, f"{tag}_wprob", wproj, [D, D])
    tok_cast = p.tok("CAST")

    WAT = p.abuf(f"{tag}WAT", [128, NC * 3 * D], BF16)
    for ci in range(NC):
        p.emit("sp", lambda s, ci=ci: s.dma_start_transpose(
            WAT[:, ci * 3 * D:(ci + 1) * 3 * D],
            wattb.ap()[:, ci * 128:(ci + 1) * 128]),
            dma=True, wait=[("CAST", tok_cast)], inc=("WO", 16))
    WPT = p.abuf(f"{tag}WPT", [128, NC * D], BF16)
    for ci in range(NC):
        p.emit("sp", lambda s, ci=ci: s.dma_start_transpose(
            WPT[:, ci * D:(ci + 1) * D],
            wprob.ap()[:, ci * 128:(ci + 1) * 128]),
            dma=True, wait=[("CAST", tok_cast)], inc=("WO", 16))
    tok_w = p.tok("WO")

    QH = p.abuf(f"{tag}QH", [128, T], BF16)
    KH = p.abuf(f"{tag}KH", [128, T], BF16)
    VB = p.abuf(f"{tag}VB", [128, NT * D], BF16)
    EXP = [p.abuf(f"{tag}EXP{i}", [128, T], BF16) for i in range(2)]
    ATTS = [p.abuf(f"{tag}ATTS{i}", [128, T], BF16) for i in range(2)]
    CTXS = p.abuf(f"{tag}CTXS", [128, NC * T], BF16)
    RS = p.abuf(f"{tag}RS", [128, 8], F32)
    kdim = 66 if use_alibi else 64
    scale = 1.0 / math.sqrt(DH)
    qh_free = []
    exp_free = [[], []]
    atts_free = [[], []]
    ctxs_free = []

    for seq in range(NSEQ):
        rin_seq = rin[seq * T:(seq + 1) * T, :]
        rout_seq = rout[seq * T:(seq + 1) * T, :]
        cm.emit_ln_seq(rin_seq)
        tok_xt = p.tok("ACT")

        for ti in range(NT):
            last = None
            for ci in range(NC):
                for nh2 in range(2):
                    last = p.emit(
                        "pe", lambda t, ci=ci, ti=ti, nh2=nh2: t.matmul(
                            cm.PSM[:, nh2 * 512:(nh2 + 1) * 512],
                            XT[:, ci * T + ti * 128:
                                  ci * T + (ti + 1) * 128],
                            WAT[:, ci * 3 * D + 2 * D + nh2 * 512:
                                ci * 3 * D + 2 * D + (nh2 + 1) * 512],
                            start=(ci == 0), stop=(ci == NC - 1)),
                        reads=["xt", "wat"], writes=["psm"],
                        wait=[("ACT", tok_xt), ("WO", tok_w)] + cm.psm_free,
                        inc=("PE", 1) if (ci == NC - 1 and nh2 == 1) else None)
            tok_e = p.emit("act", lambda a, ti=ti: a.copy(
                VB[:, ti * D:(ti + 1) * D], cm.PSM[:, 0:1024]),
                reads=["psm"], writes=["vb"], wait=[("PE", last)],
                inc=("ACT", 1))
            cm.psm_free = [("ACT", tok_e)]
        tok_vb = p.tok("ACT")

        proj_free = [cm.psm_free, cm.psm_free]
        for h in range(NH):
            for QK, sec, OUT, okey, sc in ((0, 0, QH, f"{tag}qh", scale),
                                           (1, 1, KH, f"{tag}kh", 1.0)):
                for nt4 in range(T // 512):
                    ppar = nt4 % 2
                    poff = ppar * 1024
                    last = None
                    for ci in range(NC):
                        roff = ci * 3 * D + sec * D + h * DH
                        last = p.emit(
                            "pe",
                            lambda t, ci=ci, nt4=nt4, roff=roff, poff=poff:
                            t.matmul(
                                cm.PSM[0:64, poff + (nt4 % 2) * 512:
                                       poff + (nt4 % 2) * 512 + 512],
                                WAT[:, roff:roff + DH],
                                XT[:, ci * T + nt4 * 512:
                                      ci * T + (nt4 + 1) * 512],
                                start=(ci == 0), stop=(ci == NC - 1)),
                            reads=["wat", "xt"], writes=["psm"],
                            wait=proj_free[ppar] + cm.psm_free,
                            inc=("PE", 1) if ci == NC - 1 else None)
                    tok_e = p.emit(
                        "act", lambda a, OUT=OUT, nt4=nt4, sc=sc, poff=poff:
                        a.activation(
                            out=OUT[0:64, nt4 * 512:(nt4 + 1) * 512],
                            in_=cm.PSM[0:64, poff + (nt4 % 2) * 512:
                                       poff + (nt4 % 2) * 512 + 512],
                            func=AF.Copy, scale=sc),
                        reads=["psm"], writes=[okey],
                        wait=[("PE", last)] + qh_free, inc=("ACT", 1))
                    proj_free[ppar] = [("ACT", tok_e)]
            cm.psm_free = proj_free[0] + proj_free[1]
            if use_alibi:
                p.emit("sp", lambda s, h=h: s.dma_start(
                    out=QH[64:66, :],
                    in_=cstb[CSTB_QAUG + h * 2 * T:
                             CSTB_QAUG + (h + 1) * 2 * T].rearrange(
                        "(a b) -> a b", a=2)),
                    dma=True, wait=qh_free, inc=("WO", 16))
                if h == 0 and seq == 0:
                    p.emit("sp", lambda s: s.dma_start(
                        out=KH[64:66, :],
                        in_=cstb[CSTB_KAUG:CSTB_KAUG + 2 * T].rearrange(
                            "(a b) -> a b", a=2)),
                        dma=True, inc=("WO", 16))
            tok_qk = p.tok("ACT")
            tok_qk_ld = p.tok("WO")

            def ph1(qi):
                par = qi % 2
                klen = (qi + 1) * 128
                off = par * 1024 if qi < 8 else 0
                if qi < 8:
                    pwait = half_free[par] + (cm.psm_free if qi < 2 else [])
                else:
                    pwait = cm.psm_free + half_free[0] + half_free[1]
                nch = (klen + 511) // 512
                last = None
                for ch in range(nch):
                    w = min(512, klen - ch * 512)
                    last = p.emit("pe", lambda t, ch=ch, w=w, qi=qi,
                                  off=off, kdim=kdim: t.matmul(
                        cm.PSM[:, off + ch * 512:off + ch * 512 + w],
                        QH[0:kdim, qi * 128:(qi + 1) * 128],
                        KH[0:kdim, ch * 512:ch * 512 + w],
                        start=True, stop=True),
                        reads=[f"{tag}qh", f"{tag}kh"], writes=["psm"],
                        wait=[("ACT", tok_qk), ("WO", tok_qk_ld)] + pwait,
                        inc=("PE", 1) if ch == nch - 1 else None)
                p.emit("dve", lambda v, qi=qi, off=off: v.tensor_tensor(
                    cm.PSM[:, off + qi * 128:off + (qi + 1) * 128],
                    cm.PSM[:, off + qi * 128:off + (qi + 1) * 128],
                    cm.CF[:], ALU.add),
                    reads=["psm", "cf"], writes=["psm"],
                    wait=[("PE", last), ("WO", cm.tok_setup_ld)],
                    inc=("DVE", 1))
                tok_max = p.emit(
                    "dve", lambda v, klen=klen, par=par, off=off:
                    v.reduce_max(
                        RS[:, par * 4:par * 4 + 1],
                        cm.PSM[:, off:off + klen],
                        axis=AX.X, negate=True),
                    reads=["psm", f"rs{par}"], writes=[f"rs{par}"],
                    inc=("DVE", 1))
                tok_exp = p.emit(
                    "act", lambda a, klen=klen, par=par, off=off:
                    a.activation(
                        out=EXP[par][:, 0:klen],
                        in_=cm.PSM[:, off:off + klen],
                        func=AF.Exp, bias=RS[:, par * 4:par * 4 + 1],
                        accum_out=RS[:, par * 4 + 1:par * 4 + 2]),
                    reads=["psm", f"rs{par}"],
                    writes=[f"{tag}exp{par}", f"rs{par}"],
                    wait=[("DVE", tok_max)] + exp_free[par],
                    inc=("ACT", 1))
                if qi < 8:
                    half_free[par] = [("ACT", tok_exp)]
                else:
                    cm.psm_free = [("ACT", tok_exp)]
                return tok_exp

            def ph2(qi, h, tok_exp):
                par = qi % 2
                klen = (qi + 1) * 128
                p.emit("dve", lambda v, par=par: v.reciprocal(
                    RS[:, par * 4 + 2:par * 4 + 3],
                    RS[:, par * 4 + 1:par * 4 + 2]),
                    reads=[f"rs{par}"], writes=[f"rs{par}"],
                    wait=[("ACT", tok_exp)], inc=("DVE", 1))
                tok_nrm = p.emit(
                    "dve", lambda v, klen=klen, par=par: v.tensor_scalar_mul(
                        EXP[par][:, 0:klen], EXP[par][:, 0:klen],
                        RS[:, par * 4 + 2:par * 4 + 3]),
                    reads=[f"{tag}exp{par}", f"rs{par}"],
                    writes=[f"{tag}exp{par}"], inc=("DVE", 1))
                nbk = qi + 1
                ev_toks = {}
                tp_last = {}
                for b in range((nbk + 7) // 8):
                    lo, hi = b * 8, min(nbk, b * 8 + 8)
                    for kb in range(lo, hi):
                        tp_last[b] = p.emit(
                            "pe", lambda t, kb=kb, b=b, par=par:
                            t.transpose(
                                cm.PST[b][:, (kb % 8) * 128:
                                         (kb % 8 + 1) * 128],
                                EXP[par][:, kb * 128:(kb + 1) * 128],
                                cm.IDENT[:]),
                            reads=[f"{tag}exp{par}", "ident"],
                            writes=[f"pst{b}"],
                            wait=[("DVE", tok_nrm)] + cm.pst_free[b],
                            inc=("PE", 1) if kb == hi - 1 else None)
                    ev_toks[b] = p.emit(
                        "act", lambda a, b=b, lo=lo, hi=hi, par=par: a.copy(
                            ATTS[par][:, lo * 128:hi * 128],
                            cm.PST[b][:, 0:(hi - lo) * 128]),
                        reads=[f"pst{b}"], writes=[f"{tag}atts{par}"],
                        wait=[("PE", tp_last[b])] + atts_free[par],
                        inc=("ACT", 1))
                    cm.pst_free[b] = [("ACT", ev_toks[b])]
                ctx_last = None
                for kb in range(nbk):
                    ctx_last = p.emit(
                        "pe", lambda t, kb=kb, qi=qi, h=h, par=par: t.matmul(
                            cm.PSX[par][0:64, 0:128],
                            VB[:, kb * D + h * DH:kb * D + (h + 1) * DH],
                            ATTS[par][:, kb * 128:(kb + 1) * 128],
                            start=(kb == 0), stop=(kb == nbk - 1)),
                        reads=["vb", f"{tag}atts{par}"],
                        writes=[f"psx{par}"],
                        wait=[("ACT", ev_toks[kb // 8]), ("ACT", tok_vb)] +
                             (cm.psx_free[par] if kb == 0 else []),
                        inc=("PE", 1) if kb == nbk - 1 else None)
                atts_free[par] = [("PE", ctx_last)]
                exp_free[par] = [("PE", ctx_last)]
                pb = (h % 2) * 64
                coff = (h // 2) * T + qi * 128
                tok_cv = p.emit(
                    "act", lambda a, pb=pb, coff=coff, par=par: a.copy(
                        CTXS[pb:pb + 64, coff:coff + 128],
                        cm.PSX[par][0:64, 0:128]),
                    reads=[f"psx{par}"], writes=[f"{tag}ctxs"],
                    wait=[("PE", ctx_last)] + ctxs_free, inc=("ACT", 1))
                cm.psx_free[par] = [("ACT", tok_cv)]

            half_free = [cm.psm_free, cm.psm_free]
            pend = None
            for qi in range(NT):
                te = ph1(qi)
                if pend is not None:
                    ph2(pend[0], h, pend[1])
                pend = (qi, te)
            ph2(pend[0], h, pend[1])
            cm.psm_free = cm.psm_free + half_free[0] + half_free[1]
            qh_free = [("PE", p.tok("PE"))]
        tok_ctx_done = p.tok("ACT")

        def delta_cb(ti, tok_ctx_done=tok_ctx_done):
            last = None
            for nh2 in range(2):
                for ci in range(NC):
                    last = p.emit(
                        "pe", lambda t, ti=ti, nh2=nh2, ci=ci: t.matmul(
                            cm.PSM[:, nh2 * 512:(nh2 + 1) * 512],
                            CTXS[:, ci * T + ti * 128:ci * T + (ti + 1) * 128],
                            WPT[:, ci * D + nh2 * 512:
                                ci * D + (nh2 + 1) * 512],
                            start=(ci == 0), stop=(ci == NC - 1)),
                        reads=[f"{tag}ctxs", "wpt"], writes=["psm"],
                        wait=[("ACT", tok_ctx_done)] + cm.psm_free,
                        inc=("PE", 1) if (ci == NC - 1 and nh2 == 1) else None)
            return [("PE", last)]

        cm.emit_residual_update(rin_seq, rout_seq, delta_cb)
        ctxs_free = [("PE", p.tok("PE"))]

    p.barrier("sp", [("ST0", p.tok("ST0")), ("ST1", p.tok("ST1"))])



def emit_l0_attn_b(p: Prog, cm: Common, rin, rout, wq, wk, wv, wo):
    XT = cm.XT
    DK = 128
    wqb, _ = cast_weight(p, "ab_wqb", wq, [D, DK])
    wkb, _ = cast_weight(p, "ab_wkb", wk, [D, DK])
    wvb, _ = cast_weight(p, "ab_wvb", wv, [D, DK])
    wob, _ = cast_weight(p, "ab_wob", wo, [DK, D])
    tok_cast = p.tok("CAST")

    WQ = p.abuf("abWQ", [128, NC * DK], BF16)
    WK = p.abuf("abWK", [128, NC * DK], BF16)
    WV = p.abuf("abWV", [128, NC * DK], BF16)
    WO = p.abuf("abWO", [128, D], BF16)
    for Ws, wb in ((WQ, wqb), (WK, wkb), (WV, wvb)):
        p.emit("sp", lambda s, Ws=Ws, wb=wb: s.dma_start(
            out=Ws[:].rearrange("p (c n) -> p c n", c=NC),
            in_=wb.ap().rearrange("(c p) n -> p c n", p=128)),
            dma=True, wait=[("CAST", tok_cast)], inc=("WO", 16))
    p.emit("sp", lambda s: s.dma_start(out=WO[:], in_=wob.ap()),
           dma=True, wait=[("CAST", tok_cast)], inc=("WO", 16))
    tok_w = p.tok("WO")

    QF = p.abuf("abQF", [128, T], BF16)
    KF = p.abuf("abKF", [128, T], BF16)
    VB = p.abuf("abVB", [128, NT * DK], BF16)
    EXP = p.abuf("abEXP", [128, T], BF16)
    ATTS = p.abuf("abATTS", [128, T], BF16)
    CTXS = p.abuf("abCTXS", [128, NT * DK], BF16)
    RS = p.abuf("abRS", [128, 4], F32)

    scale = 1.0 / math.sqrt(DK)
    exp_free = []
    atts_free = []

    for seq in range(NSEQ):
        rin_seq = rin[seq * T:(seq + 1) * T, :]
        rout_seq = rout[seq * T:(seq + 1) * T, :]
        cm.emit_ln_seq(rin_seq)
        tok_xt = p.tok("ACT")

        for OUT, Ws, okey, sc in ((QF, WQ, "abqf", scale),
                                  (KF, WK, "abkf", 1.0)):
            for nt4 in range(T // 512):
                last = None
                for ci in range(NC):
                    last = p.emit(
                        "pe", lambda t, ci=ci, nt4=nt4, Ws=Ws: t.matmul(
                            cm.PSM[:, nt4 * 512:(nt4 + 1) * 512],
                            Ws[:, ci * DK:(ci + 1) * DK],
                            XT[:, ci * T + nt4 * 512:
                                  ci * T + (nt4 + 1) * 512],
                            start=(ci == 0), stop=(ci == NC - 1)),
                        reads=["xt"], writes=["psm"],
                        wait=[("ACT", tok_xt), ("WO", tok_w)] + cm.psm_free,
                        inc=("PE", 1) if ci == NC - 1 else None)
                tok_e = p.emit(
                    "act", lambda a, OUT=OUT, nt4=nt4, sc=sc: a.activation(
                        out=OUT[:, nt4 * 512:(nt4 + 1) * 512],
                        in_=cm.PSM[:, nt4 * 512:(nt4 + 1) * 512],
                        func=AF.Copy, scale=sc),
                    reads=["psm"], writes=[okey], wait=[("PE", last)],
                    inc=("ACT", 1))
                cm.psm_free = [("ACT", tok_e)]

        for ti in range(NT):
            par = ti % 2
            px = cm.PSX[par]
            last = None
            for ci in range(NC):
                last = p.emit("pe", lambda t, ci=ci, ti=ti, px=px: t.matmul(
                    px[:, 0:DK],
                    XT[:, ci * T + ti * 128: ci * T + (ti + 1) * 128],
                    WV[:, ci * DK:(ci + 1) * DK],
                    start=(ci == 0), stop=(ci == NC - 1)),
                    reads=["xt"], writes=[f"psx{par}"],
                    wait=cm.psx_free[par],
                    inc=("PE", 1) if ci == NC - 1 else None)
            tok_e = p.emit("dve", lambda v, ti=ti, px=px: v.tensor_copy(
                VB[:, ti * DK:(ti + 1) * DK], px[:, 0:DK]),
                reads=[f"psx{par}"], writes=["abvb"],
                wait=[("PE", last)], inc=("DVE", 1))
            cm.psx_free[par] = [("DVE", tok_e)]
        tok_v = p.tok("DVE")

        for qi in range(NT):
            klen = (qi + 1) * 128
            nch = (klen + 511) // 512
            last = None
            for ch in range(nch):
                w = min(512, klen - ch * 512)
                last = p.emit("pe", lambda t, ch=ch, w=w, qi=qi: t.matmul(
                    cm.PSM[:, ch * 512:ch * 512 + w],
                    QF[:, qi * 128:(qi + 1) * 128],
                    KF[:, ch * 512:ch * 512 + w],
                    start=True, stop=True),
                    reads=["abqf", "abkf"], writes=["psm"],
                    wait=cm.psm_free,
                    inc=("PE", 1) if ch == nch - 1 else None)
            p.emit("dve", lambda v, qi=qi: v.tensor_tensor(
                cm.PSM[:, qi * 128:(qi + 1) * 128],
                cm.PSM[:, qi * 128:(qi + 1) * 128],
                cm.CF[:], ALU.add),
                reads=["psm", "cf"], writes=["psm"],
                wait=[("PE", last), ("WO", cm.tok_setup_ld)], inc=("DVE", 1))
            tok_max = p.emit("dve", lambda v, klen=klen: v.reduce_max(
                RS[:, 0:1], cm.PSM[:, 0:klen], axis=AX.X, negate=True),
                reads=["psm", "rs"], writes=["rs"], inc=("DVE", 1))
            tok_exp = p.emit("act", lambda a, klen=klen: a.activation(
                out=EXP[:, 0:klen], in_=cm.PSM[:, 0:klen], func=AF.Exp,
                bias=RS[:, 0:1], accum_out=RS[:, 1:2]),
                reads=["psm", "rs"], writes=["abexp", "rs"],
                wait=[("DVE", tok_max)] + exp_free, inc=("ACT", 1))
            cm.psm_free = [("ACT", tok_exp)]
            tok_rec = p.emit("dve", lambda v: v.reciprocal(
                RS[:, 2:3], RS[:, 1:2]), reads=["rs"], writes=["rs"],
                wait=[("ACT", tok_exp)], inc=("DVE", 1))
            tok_nrm = p.emit("dve", lambda v, klen=klen: v.tensor_scalar_mul(
                EXP[:, 0:klen], EXP[:, 0:klen], RS[:, 2:3]),
                reads=["abexp", "rs"], writes=["abexp"], inc=("DVE", 1))
            ctx_last = None
            for kb in range(qi + 1):
                bpar = kb % 2
                tok_tp = p.emit("pe", lambda t, kb=kb, bpar=bpar: t.transpose(
                    cm.PST[bpar][:, 0:128],
                    EXP[:, kb * 128:(kb + 1) * 128], cm.IDENT[:]),
                    reads=["abexp", "ident"], writes=[f"pst{bpar}"],
                    wait=[("DVE", tok_nrm)] + cm.pst_free[bpar],
                    inc=("PE", 1))
                tok_ev = p.emit("act", lambda a, kb=kb, bpar=bpar: a.copy(
                    ATTS[:, kb * 128:(kb + 1) * 128],
                    cm.PST[bpar][:, 0:128]),
                    reads=[f"pst{bpar}"], writes=["abatts"],
                    wait=[("PE", tok_tp)] + atts_free, inc=("ACT", 1))
                cm.pst_free[bpar] = [("ACT", tok_ev)]
                ctx_last = p.emit("pe", lambda t, kb=kb, qi=qi: t.matmul(
                    cm.PSX[0][:, 0:128],
                    VB[:, kb * DK:(kb + 1) * DK],
                    ATTS[:, kb * 128:(kb + 1) * 128],
                    start=(kb == 0), stop=(kb == qi)),
                    reads=["abvb", "abatts"], writes=["psx0"],
                    wait=[("ACT", tok_ev), ("DVE", tok_v)] +
                         (cm.psx_free[0] if kb == 0 else []),
                    inc=("PE", 1) if kb == qi else None)
            atts_free = [("PE", ctx_last)]
            exp_free = [("PE", ctx_last)]
            tok_cv = p.emit("dve", lambda v, qi=qi: v.tensor_copy(
                CTXS[:, qi * 128:(qi + 1) * 128], cm.PSX[0][:, 0:128]),
                reads=["psx0"], writes=["abctxs"],
                wait=[("PE", ctx_last)], inc=("DVE", 1))
            cm.psx_free[0] = [("DVE", tok_cv)]
        tok_ctx = p.tok("DVE")

        def delta_cb(ti, tok_ctx=tok_ctx):
            last = None
            for nh in range(2):
                last = p.emit("pe", lambda t, ti=ti, nh=nh: t.matmul(
                    cm.PSM[:, nh * 512:(nh + 1) * 512],
                    CTXS[:, ti * 128:(ti + 1) * 128],
                    WO[:, nh * 512:(nh + 1) * 512],
                    start=True, stop=True),
                    reads=["abctxs"], writes=["psm"],
                    wait=[("DVE", tok_ctx)] + cm.psm_free,
                    inc=("PE", 1) if nh == 1 else None)
            return [("PE", last)]

        cm.emit_residual_update(rin_seq, rout_seq, delta_cb)

    p.barrier("sp", [("ST0", p.tok("ST0")), ("ST1", p.tok("ST1"))])



def build_l0_attn_b():
    p = Prog()
    nc = p.nc
    rin = nc.dram_tensor("rin", [NSEQ * T, D], F32, kind="ExternalInput").ap()
    wq = nc.dram_tensor("wq", [D, 128], F32, kind="ExternalInput").ap()
    wk = nc.dram_tensor("wk", [D, 128], F32, kind="ExternalInput").ap()
    wv = nc.dram_tensor("wv", [D, 128], F32, kind="ExternalInput").ap()
    wo = nc.dram_tensor("wo", [128, D], F32, kind="ExternalInput").ap()
    cstf = nc.dram_tensor("cstf", [128 * 128], F32, kind="ExternalInput").ap()
    rout = nc.dram_tensor("rout", [NSEQ * T, D], F32,
                          kind="ExternalOutput").ap()
    cm = Common(p)
    cm.emit_setup(cstf)
    emit_l0_attn_b(p, cm, rin, rout, wq, wk, wv, wo)
    p.finish()
    return p



def emit_l0_ffn(p: Prog, cm: Common, rin, rout, gate_w, w1, w2, w3):
    XT = cm.XT
    NE, DF = 8, 2048
    gwb, _ = cast_weight(p, "fgwb", gate_w, [NE, D])
    w1b, _ = cast_weight(p, "fw1b", w1, [NE * DF, D])
    w3b, _ = cast_weight(p, "fw3b", w3, [NE * DF, D])
    w2b, _ = cast_weight(p, "fw2b", w2, [NE * D, DF])
    tok_cast = p.tok("CAST")

    GWS = p.abuf("fGWS", [8, D], BF16)
    GWT = p.abuf("fGWT", [128, NC * NE], BF16)
    p.emit("sp", lambda s: s.dma_start(out=GWS[:], in_=gwb.ap()),
           dma=True, wait=[("CAST", tok_cast)], inc=("WO", 16))
    tok_gw = p.tok("WO")
    for ci in range(NC):
        bpar = ci % 2
        tok_tp = p.emit("pe", lambda t, ci=ci, bpar=bpar: t.transpose(
            cm.PST[bpar][:, 0:8], GWS[0:8, ci * 128:(ci + 1) * 128],
            cm.IDENT[0:8, 0:8]),
            reads=["fgws", "ident"], writes=[f"pst{bpar}"],
            wait=[("WO", tok_gw), ("GPC", cm.tok_setup_gp)] +
                 cm.pst_free[bpar],
            inc=("PE", 1))
        tok_ev = p.emit("act", lambda a, ci=ci, bpar=bpar: a.copy(
            GWT[:, ci * NE:(ci + 1) * NE], cm.PST[bpar][:, 0:8]),
            reads=[f"pst{bpar}"], writes=["fgwt"],
            wait=[("PE", tok_tp)], inc=("ACT", 1))
        cm.pst_free[bpar] = [("ACT", tok_ev)]
    tok_gwt = p.tok("ACT")

    IOTA8 = p.abuf("fIOTA8", [128, 8], F32)
    p.emit("gp", lambda g: g.iota(
        IOTA8[:], pattern=[[1, 8]], base=0, channel_multiplier=0,
        allow_small_or_imprecise_dtypes=True), writes=["fiota"],
        inc=("GPC", 1))
    tok_iota = p.tok("GPC")

    W1T = p.abuf("fW1T", [128, NC * DF], BF16)
    W3T = p.abuf("fW3T", [128, NC * DF], BF16)
    W2TF = p.abuf("fW2TF", [128, NT * D], BF16)
    S1 = p.abuf("fS1", [128, 1024], BF16)
    GATED = p.abuf("fGATED", [128, NT * 1024], BF16)
    ACC = p.abuf("fACC", [128, 8 * 1024], F32)
    GT = p.abuf("fGT", [128, NT * NE], F32)
    GS = p.abuf("fGS", [128, 32], F32)
    GIX = p.abuf("fGIX", [128, 8], U32)

    state = {"w13_free": [], "w2t_free": [], "acc_free": [],
             "gated_free": []}

    def load_expert_w13(e):
        sem = f"W13{e % 2}"
        for (W, wb) in ((W1T, w1b), (W3T, w3b)):
            for ci in range(NC):
                p.emit(
                    "sp", lambda s, W=W, wb=wb, ci=ci, e=e:
                    s.dma_start_transpose(
                        W[:, ci * DF:(ci + 1) * DF],
                        wb.ap()[e * DF:(e + 1) * DF,
                                ci * 128:(ci + 1) * 128]),
                    dma=True, wait=([("CAST", tok_cast)] + state["w13_free"]),
                    inc=(sem, 16))
        return (sem, p.tok(sem))

    def load_expert_w2(e):
        sem = f"W2{e % 2}"
        for fc in range(NT):
            p.emit(
                "sp", lambda s, e=e, fc=fc: s.dma_start_transpose(
                    W2TF[:, fc * D:(fc + 1) * D],
                    w2b.ap()[e * D:(e + 1) * D, fc * 128:(fc + 1) * 128]),
                dma=True, wait=([("CAST", tok_cast)] + state["w2t_free"]),
                inc=(sem, 16))
        return (sem, p.tok(sem))

    for seq in range(NSEQ):
        rin_seq = rin[seq * T:(seq + 1) * T, :]
        rout_seq = rout[seq * T:(seq + 1) * T, :]
        for th in range(2):
            XTK = 1024
            cm.emit_ln_seq(rin_seq, tile0=th * 8, ntiles=8, xt_tokens=XTK)
            tok_xt = p.tok("ACT")

            for lt in range(8):
                px = cm.PSX[1]
                last = None
                for ci in range(NC):
                    last = p.emit(
                        "pe", lambda t, ci=ci, lt=lt, px=px: t.matmul(
                            px[:, 0:NE],
                            XT[:, ci * XTK + lt * 128:
                                  ci * XTK + (lt + 1) * 128],
                            GWT[:, ci * NE:(ci + 1) * NE],
                            start=(ci == 0), stop=(ci == NC - 1)),
                        reads=["xt", "fgwt"], writes=["psx1"],
                        wait=[("ACT", tok_xt), ("ACT", tok_gwt)] +
                             cm.psx_free[1],
                        inc=("PE", 1) if ci == NC - 1 else None)
                tok_sc = p.emit("dve", lambda v, px=px: v.tensor_copy(
                    GS[:, 0:8], px[:, 0:NE]), reads=["psx1", "fgs"],
                    writes=["fgs"], wait=[("PE", last)], inc=("DVE", 1))
                cm.psx_free[1] = [("DVE", tok_sc)]
                p.emit("dve", lambda v: v.max(GS[:, 8:16], GS[:, 0:8]),
                       reads=["fgs"], writes=["fgs"])
                p.emit("dve", lambda v: v.max_index(
                    GIX[:], GS[:, 8:16], GS[:, 0:8]),
                    reads=["fgs", "fgix"], writes=["fgix"])
                p.emit("dve", lambda v: v.tensor_copy(
                    GS[:, 16:18], GIX[:, 0:2]),
                    reads=["fgix", "fgs"], writes=["fgs"])
                tok_d = p.emit("dve", lambda v: v.tensor_tensor(
                    GS[:, 18:19], GS[:, 8:9], GS[:, 9:10], ALU.subtract),
                    reads=["fgs"], writes=["fgs"], inc=("DVE", 1))
                tok_sig = p.emit("act", lambda a: a.activation(
                    out=GS[:, 19:20], in_=GS[:, 18:19], func=AF.Sigmoid),
                    reads=["fgs"], writes=["fgs2"],
                    wait=[("DVE", tok_d)], inc=("ACT", 1))
                p.emit("dve", lambda v: v.tensor_scalar(
                    out=GS[:, 20:21], in0=GS[:, 19:20], scalar1=-1.0,
                    scalar2=1.0, op0=ALU.mult, op1=ALU.add),
                    reads=["fgs2", "fgs"], writes=["fgs"],
                    wait=[("ACT", tok_sig)])
                p.emit("dve", lambda v: v.tensor_scalar(
                    out=GS[:, 24:32], in0=IOTA8[:], scalar1=GS[:, 16:17],
                    scalar2=None, op0=ALU.is_equal),
                    reads=["fiota", "fgs"], writes=["fgs"],
                    wait=[("GPC", tok_iota)])
                p.emit("dve", lambda v, lt=lt: v.tensor_scalar_mul(
                    GT[:, lt * NE:(lt + 1) * NE], GS[:, 24:32],
                    GS[:, 19:20]),
                    reads=["fgs", "fgs2", "fgt"], writes=["fgt"])
                p.emit("dve", lambda v: v.tensor_scalar(
                    out=GS[:, 24:32], in0=IOTA8[:], scalar1=GS[:, 17:18],
                    scalar2=None, op0=ALU.is_equal),
                    reads=["fiota", "fgs"], writes=["fgs"])
                p.emit("dve", lambda v, lt=lt: v.scalar_tensor_tensor(
                    out=GT[:, lt * NE:(lt + 1) * NE], in0=GS[:, 24:32],
                    scalar=GS[:, 20:21], in1=GT[:, lt * NE:(lt + 1) * NE],
                    op0=ALU.mult, op1=ALU.add),
                    reads=["fgs", "fgt"], writes=["fgt"], inc=("DVE", 1))
            tok_gate = p.tok("DVE")

            tok_z = p.emit("dve", lambda v: v.memset(ACC[:], 0.0),
                           reads=[], writes=["facc"],
                           wait=state["acc_free"], inc=("DVE", 1))
            tok_w13 = load_expert_w13(0)
            tok_w2 = load_expert_w2(0)
            for e in range(NE):
                pst_h1 = [cm.PST[0][:].bitcast(F32), cm.PST[1][:].bitcast(F32)]
                alt_free = state.get("alt_free", [])
                for fc in range(NT):
                    par = fc % 2
                    if par == 0:
                        h1v = [cm.PSM[:, 0:512], cm.PSM[:, 512:1024]]
                        h3v = [cm.PSM[:, 1024:1536], cm.PSM[:, 1536:2048]]
                        pkey = "psm"
                        pw = cm.psm_free
                    else:
                        h1v = pst_h1
                        h3v = [cm.PSX[0][:], cm.PSX[1][:]]
                        pkey = "psalt"
                        pw = alt_free + cm.pst_free[0] + cm.pst_free[1] + \
                            cm.psx_free[0] + cm.psx_free[1]
                    lastpair = None
                    for HW, hvv, hk in ((W1T, h1v, "fw1t"),
                                        (W3T, h3v, "fw3t")):
                        for ci in range(NC):
                            for n2 in range(2):
                                lastpair = p.emit(
                                    "pe", lambda t, HW=HW, hv=hvv[n2],
                                    ci=ci, n2=n2, fc=fc: t.matmul(
                                        hv,
                                        HW[:, ci * DF + fc * 128:
                                           ci * DF + (fc + 1) * 128],
                                        XT[:, ci * XTK + n2 * 512:
                                              ci * XTK + (n2 + 1) * 512],
                                        start=(ci == 0),
                                        stop=(ci == NC - 1)),
                                    reads=["xt", hk], writes=[pkey],
                                    wait=[tok_w13, ("ACT", tok_xt)] + pw,
                                    inc=("PE", 1) if (
                                        ci == NC - 1 and n2 == 1 and
                                        hk == "fw3t") else None)
                    tok_s1a = p.emit("act", lambda a, h1v=h1v: a.activation(
                        out=S1[:, 0:512], in_=h1v[0], func=AF.Silu),
                        reads=[pkey, "fs1"], writes=["fs1"],
                        wait=[("PE", lastpair)], inc=("ACT", 1))
                    tok_s1 = p.emit("act", lambda a, h1v=h1v: a.activation(
                        out=S1[:, 512:1024], in_=h1v[1], func=AF.Silu),
                        reads=[pkey, "fs1"], writes=["fs1"],
                        inc=("ACT", 1))
                    p.emit(
                        "dve", lambda v, fc=fc, h3v=h3v: v.tensor_tensor(
                            GATED[:, fc * 1024:fc * 1024 + 512],
                            S1[:, 0:512], h3v[0], ALU.mult),
                        reads=["fs1", pkey, "fgated"], writes=["fgated"],
                        wait=[("ACT", tok_s1)] + state["gated_free"],
                        inc=("DVE", 1))
                    tok_gt = p.emit(
                        "dve", lambda v, fc=fc, h3v=h3v: v.tensor_tensor(
                            GATED[:, fc * 1024 + 512:(fc + 1) * 1024],
                            S1[:, 512:1024], h3v[1], ALU.mult),
                        reads=["fs1", pkey, "fgated"], writes=["fgated"],
                        inc=("DVE", 1))
                    if par == 0:
                        cm.psm_free = [("DVE", tok_gt)]
                    else:
                        alt_free = [("DVE", tok_gt)]
                        state["alt_free"] = alt_free
                        cm.pst_free[0] = [("DVE", tok_gt)]
                        cm.pst_free[1] = [("DVE", tok_gt)]
                        cm.psx_free[0] = [("DVE", tok_gt)]
                        cm.psx_free[1] = [("DVE", tok_gt)]
                tok_gated_all = p.tok("DVE")
                state["w13_free"] = [("PE", p.tok("PE"))]
                if e + 1 < NE:
                    tok_w13_next = load_expert_w13(e + 1)
                w2banks = {
                    0: [cm.PSM[:, 0:512], cm.PSM[:, 512:1024],
                        cm.PSM[:, 1024:1536], cm.PSM[:, 1536:2048]],
                    1: [pst_h1[0], pst_h1[1],
                        cm.PSX[0][:], cm.PSX[1][:]],
                }
                w2free = {0: cm.psm_free,
                          1: state.get("alt_free", []) +
                          cm.pst_free[0] + cm.pst_free[1] +
                          cm.psx_free[0] + cm.psx_free[1]}
                for tbp in range(4):
                    wp = tbp % 2
                    pk = "psm" if wp == 0 else "psalt"
                    last = None
                    for fc in range(NT):
                        for half in range(4):
                            tb = tbp * 2 + half // 2
                            o512 = (half % 2) * 512 + (half // 2) * 1024
                            last = p.emit(
                                "pe", lambda t, fc=fc, tb=tb, o512=o512,
                                bk=w2banks[wp][half]:
                                t.matmul(
                                    bk,
                                    GATED[:, fc * 1024 + tb * 128:
                                          fc * 1024 + (tb + 1) * 128],
                                    W2TF[:, fc * D + (o512 % 1024):
                                         fc * D + (o512 % 1024) + 512],
                                    start=(fc == 0), stop=(fc == NT - 1)),
                                reads=["fgated", "fw2tf"], writes=[pk],
                                wait=[tok_w2,
                                      ("DVE", tok_gated_all)] + w2free[wp],
                                inc=("PE", 1) if (
                                    fc == NT - 1 and half == 3) else None)
                    accs = []
                    for half in range(2):
                        tb = tbp * 2 + half
                        g_col = GT[:].rearrange(
                            "p (t e) -> p t e", e=NE)[:, tb, e:e + 1]
                        for qh2 in range(2):
                            tok_acc = p.emit(
                                "dve", lambda v, tb=tb, half=half,
                                g_col=g_col, qh2=qh2,
                                bk=w2banks[wp][half * 2 + qh2]:
                                v.scalar_tensor_tensor(
                                    out=ACC[:, tb * 1024 + qh2 * 512:
                                            tb * 1024 + qh2 * 512 + 512],
                                    in0=bk,
                                    scalar=g_col,
                                    in1=ACC[:, tb * 1024 + qh2 * 512:
                                            tb * 1024 + qh2 * 512 + 512],
                                    op0=ALU.mult, op1=ALU.add),
                                reads=[pk, "facc", "fgt"], writes=["facc"],
                                wait=[("PE", last), ("DVE", tok_gate),
                                      ("DVE", tok_z)], inc=("DVE", 1))
                            accs.append(tok_acc)
                    w2free[wp] = [("DVE", accs[-1])]
                    if wp == 0:
                        cm.psm_free = w2free[0]
                    else:
                        state["alt_free"] = w2free[1]
                        cm.pst_free[0] = w2free[1]
                        cm.pst_free[1] = w2free[1]
                        cm.psx_free[0] = w2free[1]
                        cm.psx_free[1] = w2free[1]
                state["gated_free"] = [("PE", p.tok("PE"))]
                state["w2t_free"] = [("PE", p.tok("PE"))]
                if e + 1 < NE:
                    tok_w13 = tok_w13_next
                    tok_w2 = load_expert_w2(e + 1)
            for tb in range(8):
                ti = th * 8 + tb
                par = ti % 2
                RT = cm.RT[par]
                tok_ld = p.emit(
                    "sp", lambda s, RT=RT, ti=ti, rin_seq=rin_seq:
                    s.dma_start(
                        out=RT[:], in_=rin_seq[ti * 128:(ti + 1) * 128, :]),
                    dma=True, wait=cm.rt_free[par] + cm.rt_store[par],
                    inc=(f"RL{par}", 16))
                tok_add = p.emit(
                    "dve", lambda v, RT=RT, tb=tb: v.tensor_tensor(
                        RT[:], RT[:], ACC[:, tb * 1024:(tb + 1) * 1024],
                        ALU.add),
                    reads=[f"RT{par}", "facc"], writes=[f"RT{par}"],
                    wait=[(f"RL{par}", tok_ld), ("DVE", p.tok("DVE"))],
                    inc=("DVE", 1))
                cm.rt_free[par] = [("DVE", tok_add)]
                tok_st = p.emit(
                    "sp", lambda s, RT=RT, ti=ti, rout_seq=rout_seq:
                    s.dma_start(
                        out=rout_seq[ti * 128:(ti + 1) * 128, :], in_=RT[:]),
                    dma=True, wait=[("DVE", tok_add)], inc=(f"ST{par}", 16))
                cm.rt_store[par] = [(f"ST{par}", tok_st)]
            state["acc_free"] = [("DVE", p.tok("DVE"))]
    p.barrier("sp", [("ST0", p.tok("ST0")), ("ST1", p.tok("ST1"))])


def build_l0_ffn():
    p = Prog()
    nc = p.nc
    rin = nc.dram_tensor("rin", [NSEQ * T, D], F32, kind="ExternalInput").ap()
    gate_w = nc.dram_tensor("gate_w", [8, D], F32, kind="ExternalInput").ap()
    w1 = nc.dram_tensor("w1", [8, 2048, D], F32, kind="ExternalInput").ap()
    w2 = nc.dram_tensor("w2", [8, D, 2048], F32, kind="ExternalInput").ap()
    w3 = nc.dram_tensor("w3", [8, 2048, D], F32, kind="ExternalInput").ap()
    cstf = nc.dram_tensor("cstf", [128 * 128], F32, kind="ExternalInput").ap()
    rout = nc.dram_tensor("rout", [NSEQ * T, D], F32,
                          kind="ExternalOutput").ap()
    cm = Common(p, xt_tokens=1024)
    cm.emit_setup(cstf)
    emit_l0_ffn(p, cm, rin, rout, gate_w,
                w1.rearrange("e f c -> (e f) c"),
                w2.rearrange("e d f -> (e d) f"),
                w3.rearrange("e f c -> (e f) c"))
    p.finish()
    return p



def emit_l1_ffn(p: Prog, cm: Common, rin, rout, wg, bg, wu, bu, wd, bd,
                wqp, keys, down_embed, up_embed):
    def _setph(c):
        p.pe_drain_on = (c == "C")
    XT = cm.XT
    _setph("A")
    DCD, RET, NEXP, NKEYS, H, KPH = 4096, 128, 4096, 64, 4, 8
    NFC = DCD // 128
    wgb, _ = cast_weight(p, "pwgb", wg, [D, DCD])
    wub, _ = cast_weight(p, "pwub", wu, [D, DCD])
    wdb, _ = cast_weight(p, "pwdb", wd, [DCD, D])
    wqb, _ = cast_weight(p, "pwqb", wqp, [D, 512])
    keyb, _ = cast_weight(p, "pkeyb", keys, [H * NKEYS, 2 * 64])
    deb, _ = cast_weight(p, "pdeb", down_embed, [NEXP, D])
    ueb, _ = cast_weight(p, "pueb", up_embed, [NEXP, D])
    tok_cast = p.tok("CAST")

    BGF = p.abuf("pBGF", [128, NFC], F32)
    BUF = p.abuf("pBUF", [128, NFC], F32)
    BDR = p.abuf("pBDR", [1, D], BF16)
    p.emit("sp", lambda s: s.dma_start(
        out=BGF[:], in_=bg.rearrange("(a p) -> p a", p=128),
        allow_slow_non_contiguous=True), dma=True, inc=("WO", 16))
    p.emit("sp", lambda s: s.dma_start(
        out=BUF[:], in_=bu.rearrange("(a p) -> p a", p=128),
        allow_slow_non_contiguous=True), dma=True, inc=("WO", 16))
    tok_bias = p.tok("WO")
    p.emit("gp", lambda g: g.dma_start(
        out=BDR[:], in_=bd.rearrange("(a b) -> a b", a=1)),
        dma=True, inc=("CAST", 16))
    WQP = p.abuf("pWQP", [128, NC * 512], BF16)
    p.emit("sp", lambda s: s.dma_start(
        out=WQP[:].rearrange("p (c n) -> p c n", c=NC),
        in_=wqb.ap().rearrange("(c p) n -> p c n", p=128)),
        dma=True, wait=[("CAST", tok_cast)], inc=("WO", 16))
    tok_wqp = p.tok("WO")
    KEY0 = p.abuf("pKEY0", [128, 2 * 128], BF16)
    KEYST = p.abuf("pKEYST", [128, 8 * 64], BF16)
    p.emit("sp", lambda s: s.dma_start(
        out=KEY0[:].rearrange("p (a x) -> p a x", a=2),
        in_=keyb.ap().rearrange("(a p) x -> p a x", p=128)),
        dma=True, wait=[("CAST", tok_cast)], inc=("WO", 16))
    tok_keys = p.tok("WO")
    for h in range(H):
        for pp in range(2):
            src_pb = (h % 2) * 64
            grp = pp * 4 + h
            for dst_pb in (0, 64):
                tok_tp = p.emit(
                    "pe", lambda t, h=h, pp=pp, src_pb=src_pb,
                    dst_pb=dst_pb: t.transpose(
                        cm.PST[0][dst_pb:dst_pb + 64, 0:64],
                        KEY0[src_pb:src_pb + 64,
                             (h // 2) * 128 + pp * 64:
                             (h // 2) * 128 + (pp + 1) * 64],
                        cm.IDENT[src_pb:src_pb + 64, src_pb:src_pb + 64]),
                    reads=["pkey0", "ident"], writes=["pst0"],
                    wait=[("WO", tok_keys), ("GPC", cm.tok_setup_gp)] +
                         cm.pst_free[0],
                    inc=("PE", 1))
                tok_ev = p.emit(
                    "act", lambda a, grp=grp, dst_pb=dst_pb: a.copy(
                        KEYST[dst_pb:dst_pb + 64, grp * 64:(grp + 1) * 64],
                        cm.PST[0][dst_pb:dst_pb + 64, 0:64]),
                    reads=["pst0"], writes=["pkeyst"],
                    wait=[("PE", tok_tp)], inc=("ACT", 1))
                cm.pst_free[0] = [("ACT", tok_ev)]
    tok_keyst = p.tok("ACT")

    WGF = [p.abuf(f"pWGF{i}", [128, NC * 128], BF16) for i in range(2)]
    WUF = [p.abuf(f"pWUF{i}", [128, NC * 128], BF16) for i in range(2)]
    WDF = [p.abuf(f"pWDF{i}", [128, D], BF16) for i in range(4)]
    XLNS = p.abuf("pXLNS", [128, 8 * 1024], BF16)
    GATED = p.abuf("pGATED", [128, NFC * 1024], BF16)
    S1 = p.abuf("pS1", [128, 1024], BF16)
    QP = p.abuf("pQP", [128, 4 * 1024], BF16)
    SIMS = p.abuf("pSIMS", [128, 512], F32)
    SC1 = p.abuf("pSC1", [128, 64], F32)
    IX1 = p.abuf("pIX1", [128, 64], F32)
    IXU = p.abuf("pIXU", [128, 8], U32)
    CMB = p.abuf("pCMB", [128, 4 * 64], F32)
    SC2 = p.abuf("pSC2", [128, 32], F32)
    PKU = p.abuf("pPKU", [128, 8], U32)
    PKF = p.abuf("pPKF", [128, 32], F32)
    AB2 = p.abuf("pAB2", [128, 64], F32)
    SEL = p.abuf("pSEL", [128, 64], F32)
    MSK = p.abuf("pMSK", [128, 64], F32)
    TMP = p.abuf("pTMP", [128, 64], F32)
    EID = p.abuf("pEID", [128, 32], I32)
    ABI = p.abuf("pABI", [128, 32], I32)
    EIDF = p.abuf("pEIDF", [128, 32], F32)
    PP = p.abuf("pPP", [128, 48], F32)
    WCOL = p.abuf("pWCOL", [128, 32], F32)
    WPR = p.abuf("pWPR", [128, 32], F32)
    DEG = [p.abuf(f"pDEG{i}", [128, 1024], BF16) for i in range(8)]
    UEG = [p.abuf(f"pUEG{i}", [128, 1024], BF16) for i in range(8)]
    JNK = p.abuf("pJNK", [128, 1024], BF16)
    EST = p.abuf("pEST", [128, 1024], F32)
    DELTA = [p.abuf(f"pDELTA{i}", [128, 1024], F32) for i in range(2)]

    state = {"wgf_free": [[], []], "wuf_free": [[], []],
             "wdf_free": [[], [], [], []],
             "ueg_free": [[] for _ in range(8)],
             "deg_free": [[] for _ in range(8)], "est_free": [],
             "eid_free": [],
             "delta_free": [[], []]}

    def load_w_fc(W, wb, fc, key, freelist):
        buf = fc % 2
        tok = p.emit(
            "sp", lambda s, W=W, wb=wb, fc=fc, buf=buf: s.dma_start(
                out=W[buf][:].rearrange("p (c n) -> p c n", c=NC),
                in_=wb.ap().rearrange("(c p) n -> p c n", p=128)[
                    :, :, fc * 128:(fc + 1) * 128],
                allow_slow_non_contiguous=True),
            dma=True, wait=([("CAST", tok_cast)] + freelist[buf]),
            inc=(f"WGU{buf}", 16))
        return ("WGU%d" % buf, p.tok("WGU%d" % buf))

    def load_wd_fc(fc):
        buf = fc % 4
        tok = p.emit(
            "sp", lambda s, fc=fc, buf=buf: s.dma_start(
                out=WDF[buf][:],
                in_=wdb.ap()[fc * 128:(fc + 1) * 128, :]),
            dma=True, wait=([("CAST", tok_cast)] +
                            state["wdf_free"][buf]),
            inc=(f"WD{buf}", 16))
        return ("WD%d" % buf, p.tok("WD%d" % buf))

    for seq in range(NSEQ):
        rin_seq = rin[seq * T:(seq + 1) * T, :]
        rout_seq = rout[seq * T:(seq + 1) * T, :]
        for th in range(2):
            XTK = 1024
            xlns_cb_tok = []

            def xlns_cb(ti, XLN, tok_xln, th=th):
                lt = ti - th * 8
                p.emit("dve", lambda v, XLN=XLN, lt=lt: v.tensor_copy(
                    XLNS[:, lt * 1024:(lt + 1) * 1024], XLN[:]),
                    reads=[f"XLN{ti % 2}", "pxlns"], writes=["pxlns"],
                    inc=("DVE", 1))

            cm.emit_ln_seq(rin_seq, extra_tile_cb=xlns_cb, tile0=th * 8,
                           ntiles=8, xt_tokens=XTK)
            tok_xt = p.tok("ACT")
            tok_xlns = p.tok("DVE")

            _setph("Q")
            for qb in range(4):
                last = None
                for n2 in range(2):
                    for ci in range(NC):
                        last = p.emit(
                            "pe", lambda t, qb=qb, ci=ci, n2=n2: t.matmul(
                                cm.PSM[:, n2 * 512:(n2 + 1) * 512],
                                WQP[:, ci * 512 + qb * 128:
                                    ci * 512 + (qb + 1) * 128],
                                XT[:, ci * XTK + n2 * 512:
                                      ci * XTK + (n2 + 1) * 512],
                                start=(ci == 0), stop=(ci == NC - 1)),
                            reads=["xt", "pwqp"], writes=["psm"],
                            wait=[("ACT", tok_xt), ("WO", tok_wqp)] +
                                 cm.psm_free,
                            inc=("PE", 1) if (ci == NC - 1 and n2 == 1)
                            else None)
                tok_e = p.emit("act", lambda a, qb=qb: a.copy(
                    QP[:, qb * 1024:(qb + 1) * 1024], cm.PSM[:, 0:1024]),
                    reads=["psm"], writes=["pqp"], wait=[("PE", last)],
                    inc=("ACT", 1))
                cm.psm_free = [("ACT", tok_e)]
            tok_qp = p.tok("ACT")

            _setph("B")
            tok_wg = load_w_fc(WGF, wgb, 0, "pwgf", state["wgf_free"])
            tok_wu = load_w_fc(WUF, wub, 0, "pwuf", state["wuf_free"])
            wtoks = {0: (tok_wg, tok_wu)}
            for fc in range(NFC):
                buf = fc % 2
                if fc + 1 < NFC:
                    wtoks[fc + 1] = (
                        load_w_fc(WGF, wgb, fc + 1, "pwgf",
                                  state["wgf_free"]),
                        load_w_fc(WUF, wub, fc + 1, "pwuf",
                                  state["wuf_free"]))
                tok_wg, tok_wu = wtoks.pop(fc)
                lastg = None
                for ci in range(NC):
                    for n2 in range(2):
                        lastg = p.emit(
                            "pe", lambda t, ci=ci, n2=n2, buf=buf: t.matmul(
                                cm.PSM[:, n2 * 512:(n2 + 1) * 512],
                                WGF[buf][:, ci * 128:(ci + 1) * 128],
                                XT[:, ci * XTK + n2 * 512:
                                      ci * XTK + (n2 + 1) * 512],
                                start=(ci == 0), stop=(ci == NC - 1)),
                            reads=["xt", f"pwgf{buf}"], writes=["psm"],
                            wait=[tok_wu, ("ACT", tok_xt)] +
                                 cm.psm_free,
                            inc=("PE", 1) if (ci == NC - 1 and n2 == 1)
                            else None)
                lastu = None
                for ci in range(NC):
                    for n2 in range(2):
                        lastu = p.emit(
                            "pe", lambda t, ci=ci, n2=n2, buf=buf: t.matmul(
                                cm.PSM[:, 1024 + n2 * 512:
                                       1024 + (n2 + 1) * 512],
                                WUF[buf][:, ci * 128:(ci + 1) * 128],
                                XT[:, ci * XTK + n2 * 512:
                                      ci * XTK + (n2 + 1) * 512],
                                start=(ci == 0), stop=(ci == NC - 1)),
                            reads=["xt", f"pwuf{buf}"], writes=["psm"],
                            wait=[tok_wu],
                            inc=("PE", 1) if (ci == NC - 1 and n2 == 1)
                            else None)
                state["wgf_free"][buf] = [("PE", lastg)]
                state["wuf_free"][buf] = [("PE", lastu)]
                tok_s1 = p.emit("act", lambda a, fc=fc: a.activation(
                    out=S1[:], in_=cm.PSM[:, 0:1024], func=AF.Silu,
                    bias=BGF[:, fc:fc + 1]),
                    reads=["psm", "ps1", "pbgf"], writes=["ps1"],
                    wait=[("PE", lastg), ("WO", tok_bias)], inc=("ACT", 1))
                tok_gt = p.emit("dve", lambda v, fc=fc: v.scalar_tensor_tensor(
                    out=GATED[:, fc * 1024:(fc + 1) * 1024],
                    in0=cm.PSM[:, 1024:2048], scalar=BUF[:, fc:fc + 1],
                    in1=S1[:], op0=ALU.add, op1=ALU.mult),
                    reads=["psm", "ps1", "pbuf", "pgated"],
                    writes=["pgated"],
                    wait=[("ACT", tok_s1), ("PE", lastu)], inc=("DVE", 1))
                cm.psm_free = [("DVE", tok_gt)]
            tok_gated = p.tok("DVE")

            tok_wd = load_wd_fc(0)
            wdtoks = {0: tok_wd}
            for lt in range(8):
                ti = th * 8 + lt
                _setph("C")
                last = None
                for pp in range(2):
                    for h in range(H):
                        grp = pp * 4 + h
                        base = ((4 * pp + h) % 2) * 64
                        qb = (4 * pp + h) // 2
                        last = p.emit(
                            "pe", lambda t, grp=grp, base=base, qb=qb,
                            lt=lt: t.matmul(
                                cm.PSM[:, 1536 + grp * 64:
                                       1536 + (grp + 1) * 64],
                                QP[base:base + 64,
                                   qb * 1024 + lt * 128:
                                   qb * 1024 + (lt + 1) * 128],
                                KEYST[base:base + 64,
                                      grp * 64:(grp + 1) * 64],
                                start=(grp == 0), stop=(grp == 7)),
                            reads=["pqp", "pkeyst"], writes=["psm"],
                            wait=[("ACT", tok_qp), ("ACT", tok_keyst)] +
                                 (cm.psm_free if grp == 0 else []),
                            inc=("PE", 1) if grp == 7 else None)
                tok_sims = p.emit("dve", lambda v: v.tensor_copy(
                    SIMS[:], cm.PSM[:, 1536:2048]),
                    reads=["psm", "psims"], writes=["psims"],
                    wait=[("PE", last)], inc=("DVE", 1))
                cm.psm_free = [("DVE", tok_sims)]
                for grp in range(8):
                    p.emit("dve", lambda v, grp=grp: v.max(
                        SC1[:, grp * 8:(grp + 1) * 8],
                        SIMS[:, grp * 64:(grp + 1) * 64]),
                        reads=["psims", "psc1"], writes=["psc1"])
                    p.emit("dve", lambda v, grp=grp: v.max_index(
                        IXU[:], SC1[:, grp * 8:(grp + 1) * 8],
                        SIMS[:, grp * 64:(grp + 1) * 64]),
                        reads=["psims", "psc1", "pixu"], writes=["pixu"])
                    p.emit("dve", lambda v, grp=grp: v.tensor_copy(
                        IX1[:, grp * 8:(grp + 1) * 8], IXU[:]),
                        reads=["pixu", "pix1"], writes=["pix1"])
                for h in range(H):
                    sx = SC1[:, h * 8:(h + 1) * 8]
                    sy = SC1[:, (4 + h) * 8:(5 + h) * 8]
                    sx_b = bass.AP(sx.tensor, sx.offset,
                                   [sx.ap[0], [sx.ap[1][0], 8], [0, 8]])
                    sy_b = bass.AP(sy.tensor, sy.offset,
                                   [sy.ap[0], [0, 8], [sy.ap[1][0], 8]])
                    p.emit("dve", lambda v, h=h, sx_b=sx_b, sy_b=sy_b:
                           v.tensor_tensor(
                        CMB[:, h * 64:(h + 1) * 64].rearrange(
                            "p (a b) -> p a b", a=8),
                        sx_b, sy_b, ALU.add),
                        reads=["psc1", "pcmb"], writes=["pcmb"])
                for h in range(H):
                    p.emit("dve", lambda v, h=h: v.max(
                        SC2[:, h * 8:(h + 1) * 8],
                        CMB[:, h * 64:(h + 1) * 64]),
                        reads=["pcmb", "psc2"], writes=["psc2"])
                    p.emit("dve", lambda v, h=h: v.max_index(
                        PKU[:], SC2[:, h * 8:(h + 1) * 8],
                        CMB[:, h * 64:(h + 1) * 64]),
                        reads=["pcmb", "psc2", "ppku"], writes=["ppku"])
                    p.emit("dve", lambda v, h=h: v.tensor_copy(
                        PKF[:, h * 8:(h + 1) * 8], PKU[:]),
                        reads=["ppku", "ppkf"], writes=["ppkf"])
                p.emit("dve", lambda v: v.tensor_scalar(
                    out=ABI[:], in0=PKF[:], scalar1=0.125, scalar2=-0.4375,
                    op0=ALU.mult, op1=ALU.add),
                    reads=["ppkf", "pabi"], writes=["pabi"])
                p.emit("dve", lambda v: v.tensor_copy(AB2[:, 0:32], ABI[:]),
                    reads=["pabi", "pab2"], writes=["pab2"])
                p.emit("dve", lambda v: v.scalar_tensor_tensor(
                    out=AB2[:, 32:64], in0=AB2[:, 0:32], scalar=-8.0,
                    in1=PKF[:], op0=ALU.mult, op1=ALU.add),
                    reads=["ppkf", "pab2"], writes=["pab2"])
                p.emit("dve", lambda v: v.memset(SEL[:], 0.0),
                       reads=["psel"], writes=["psel"])
                ix_b = bass.AP(IX1[:].tensor, IX1[:].offset,
                               [IX1[:].ap[0], [8, 8], [0, 8]])
                for j in range(8):
                    ix_j = bass.AP(IX1[:].tensor, IX1[:].offset + j,
                                   [IX1[:].ap[0], [8, 8], [0, 8]])
                    p.emit("dve", lambda v, j=j: v.tensor_single_scalar(
                        MSK[:], AB2[:], float(j), ALU.is_equal),
                        reads=["pab2", "pmsk"], writes=["pmsk"])
                    p.emit("dve", lambda v, ix_j=ix_j: v.tensor_tensor(
                        TMP[:].rearrange("p (a b) -> p a b", a=8),
                        MSK[:].rearrange("p (a b) -> p a b", a=8),
                        ix_j, ALU.mult),
                        reads=["pmsk", "pix1", "ptmp"], writes=["ptmp"])
                    p.emit("dve", lambda v: v.tensor_tensor(
                        SEL[:], SEL[:], TMP[:], ALU.add),
                        reads=["ptmp", "psel"], writes=["psel"])
                p.emit("dve", lambda v: v.scalar_tensor_tensor(
                    out=EIDF[:], in0=SEL[:, 0:32], scalar=64.0,
                    in1=SEL[:, 32:64], op0=ALU.mult, op1=ALU.add),
                    reads=["psel", "peidf"], writes=["peidf"])
                tok_eid = p.emit("dve", lambda v: v.tensor_copy(
                    EID[:], EIDF[:]), reads=["peidf", "peid"],
                    writes=["peid"], wait=state["eid_free"],
                    inc=("DVE", 1))
                tok_ex = p.emit("act", lambda a: a.activation(
                    out=PP[:, 0:32], in_=SC2[:], func=AF.Exp),
                    reads=["psc2", "ppp"], writes=["ppp"],
                    wait=[("DVE", tok_eid)], inc=("ACT", 1))
                for h in range(H):
                    p.emit("dve", lambda v, h=h: v.reduce_sum(
                        PP[:, 32 + h:33 + h], PP[:, h * 8:(h + 1) * 8],
                        axis=AX.X),
                        reads=["ppp"], writes=["ppp2"],
                        wait=[("ACT", tok_ex)])
                p.emit("dve", lambda v: v.reciprocal(
                    PP[:, 40:44], PP[:, 32:36]), reads=["ppp2"],
                    writes=["ppp2"])
                for h in range(H):
                    p.emit("dve", lambda v, h=h: v.tensor_scalar_mul(
                        PP[:, h * 8:(h + 1) * 8], PP[:, h * 8:(h + 1) * 8],
                        PP[:, 40 + h:41 + h]),
                        reads=["ppp", "ppp2"], writes=["ppp"])
                for sl in range(32):
                    buf = sl % 8
                    tok_g = p.emit(
                        "gp", lambda g, sl=sl, buf=buf: g.indirect_dma_start(
                            out=DEG[buf][:], out_offset=None,
                            in_=deb.ap()[:, :],
                            in_offset=IndirectOffsetOnAxis(
                                EID[:, sl:sl + 1], 0)),
                        dma=True,
                        wait=[("DVE", tok_eid)] + state["deg_free"][buf],
                        inc=(f"DE{buf}", 16))
                    tok_w = p.emit(
                        "dve", lambda v, sl=sl, lt=lt, buf=buf:
                        v.scalar_tensor_tensor(
                            out=JNK[:], in0=XLNS[:, lt * 1024:
                                                 (lt + 1) * 1024],
                            scalar=1.0, in1=DEG[buf][:], op0=ALU.mult,
                            op1=ALU.mult,
                            accum_out=WCOL[:, sl:sl + 1]),
                        reads=["pxlns", f"pdeg{buf}", "pjnk", "pwcol"],
                        writes=["pjnk", "pwcol"],
                        wait=[(f"DE{buf}", tok_g), ("DVE", tok_xlns)],
                        inc=("DVE", 1))
                    state["deg_free"][buf] = [("DVE", tok_w)]
                tok_wdots = p.tok("DVE")
                tok_sw = p.emit("act", lambda a: a.activation(
                    out=WPR[:], in_=WCOL[:], func=AF.Silu),
                    reads=["pwcol", "pwpr"], writes=["pwpr"],
                    wait=[("DVE", tok_wdots)], inc=("ACT", 1))
                tok_wp = p.emit("dve", lambda v: v.tensor_tensor(
                    WPR[:], WPR[:], PP[:, 0:32], ALU.mult),
                    reads=["pwpr", "ppp"], writes=["pwpr"],
                    wait=[("ACT", tok_sw)], inc=("DVE", 1))
                p.emit("dve", lambda v: v.memset(EST[:], 0.0),
                       reads=["pest"], writes=["pest"],
                       wait=state["est_free"])
                for sl in range(32):
                    buf = sl % 8
                    tok_g = p.emit(
                        "gp", lambda g, sl=sl, buf=buf:
                        g.indirect_dma_start(
                            out=UEG[buf][:], out_offset=None,
                            in_=ueb.ap()[:, :],
                            in_offset=IndirectOffsetOnAxis(
                                EID[:, sl:sl + 1], 0)),
                        dma=True,
                        wait=[("DVE", tok_eid)] + state["ueg_free"][buf],
                        inc=(f"UE{buf}", 16))
                    tok_es = p.emit(
                        "dve", lambda v, sl=sl, buf=buf:
                        v.scalar_tensor_tensor(
                            out=EST[:], in0=UEG[buf][:],
                            scalar=WPR[:, sl:sl + 1], in1=EST[:],
                            op0=ALU.mult, op1=ALU.add),
                        reads=[f"pueg{buf}", "pwpr", "pest"],
                        writes=["pest"],
                        wait=[(f"UE{buf}", tok_g), ("DVE", tok_wp)],
                        inc=("DVE", 1))
                    state["ueg_free"][buf] = [("DVE", tok_es)]
                state["eid_free"] = [(f"UE{b}", p.tok(f"UE{b}"))
                                     for b in range(8)]
                tok_est = p.tok("DVE")
                _setph("D")
                last = None
                for fc in range(NFC):
                    for pf in range(4 if fc == 0 else 1):
                        nxt = fc + pf + (0 if fc == 0 else 3)
                        if 0 < nxt < NFC and nxt not in wdtoks:
                            wdtoks[nxt] = load_wd_fc(nxt)
                    tok_wdc = wdtoks.pop(fc)
                    buf = fc % 4
                    for n2 in range(2):
                        last = p.emit(
                            "pe", lambda t, fc=fc, n2=n2, buf=buf, lt=lt:
                            t.matmul(
                                cm.PSM[:, n2 * 512:(n2 + 1) * 512],
                                GATED[:, fc * 1024 + lt * 128:
                                      fc * 1024 + (lt + 1) * 128],
                                WDF[buf][:, n2 * 512:(n2 + 1) * 512],
                                start=(fc == 0), stop=False),
                            reads=["pgated", f"pwdf{buf}"], writes=["psm"],
                            wait=[tok_wdc, ("DVE", tok_gated)] +
                                 cm.psm_free,
                            inc=("PE", 1) if n2 == 1 else None)
                    state["wdf_free"][buf] = [("PE", p.tok("PE"))]
                wdtoks = {0: load_wd_fc(0)} if lt + 1 < 8 else {}
                state["wdf_free"][0] = [("PE", p.tok("PE"))]
                for n2 in range(2):
                    last = p.emit(
                        "pe", lambda t, n2=n2, lt=lt: t.matmul(
                            cm.PSM[:, n2 * 512:(n2 + 1) * 512],
                            cm.ONES[0:1, lt * 128:(lt + 1) * 128],
                            BDR[0:1, n2 * 512:(n2 + 1) * 512],
                            start=False, stop=(n2 == 1)),
                        reads=["pbdr", "ones"], writes=["psm"],
                        wait=[("CAST", tok_cast), ("GPC",
                                                   cm.tok_setup_gp)],
                        inc=("PE", 1) if n2 == 1 else None)
                par = ti % 2
                RT = cm.RT[par]
                DL = DELTA[par]
                tok_dl = p.emit("act", lambda a, DL=DL: a.copy(
                    DL[:], cm.PSM[:, 0:1024]),
                    reads=["psm", f"pdelta{par}"], writes=[f"pdelta{par}"],
                    wait=[("PE", last)] + state["delta_free"][par],
                    inc=("ACT", 1))
                cm.psm_free = [("ACT", tok_dl)]
                tok_ld = p.emit(
                    "sp", lambda s, RT=RT, ti=ti, rin_seq=rin_seq:
                    s.dma_start(
                        out=RT[:], in_=rin_seq[ti * 128:(ti + 1) * 128, :]),
                    dma=True, wait=cm.rt_free[par] + cm.rt_store[par],
                    inc=(f"RL{par}", 16))
                tok_a1 = p.emit("dve", lambda v, RT=RT, DL=DL: v.tensor_tensor(
                    RT[:], RT[:], DL[:], ALU.add),
                    reads=[f"RT{par}", f"pdelta{par}"], writes=[f"RT{par}"],
                    wait=[("ACT", tok_dl), (f"RL{par}", tok_ld)],
                    inc=("DVE", 1))
                state["delta_free"][par] = [("DVE", tok_a1)]
                tok_a2 = p.emit("dve", lambda v, RT=RT: v.tensor_tensor(
                    RT[:], RT[:], EST[:], ALU.add),
                    reads=[f"RT{par}", "pest"], writes=[f"RT{par}"],
                    wait=[("DVE", tok_est)], inc=("DVE", 1))
                state["est_free"] = [("DVE", tok_a2)]
                cm.rt_free[par] = [("DVE", tok_a2)]
                tok_st = p.emit(
                    "sp", lambda s, RT=RT, ti=ti, rout_seq=rout_seq:
                    s.dma_start(
                        out=rout_seq[ti * 128:(ti + 1) * 128, :], in_=RT[:]),
                    dma=True, wait=[("DVE", tok_a2)], inc=(f"ST{par}", 16))
                cm.rt_store[par] = [(f"ST{par}", tok_st)]
    p.barrier("sp", [("ST0", p.tok("ST0")), ("ST1", p.tok("ST1"))])


def build_l1_ffn():
    p = Prog()
    nc = p.nc
    rin = nc.dram_tensor("rin", [NSEQ * T, D], F32, kind="ExternalInput").ap()
    wg = nc.dram_tensor("wg", [D, 4096], F32, kind="ExternalInput").ap()
    bg = nc.dram_tensor("bg", [4096], F32, kind="ExternalInput").ap()
    wu = nc.dram_tensor("wu", [D, 4096], F32, kind="ExternalInput").ap()
    bu = nc.dram_tensor("bu", [4096], F32, kind="ExternalInput").ap()
    wd = nc.dram_tensor("wd", [4096, D], F32, kind="ExternalInput").ap()
    bd = nc.dram_tensor("bd", [D], F32, kind="ExternalInput").ap()
    wqp = nc.dram_tensor("wq", [D, 512], F32, kind="ExternalInput").ap()
    keys = nc.dram_tensor("keys", [4, 64, 2, 64], F32,
                          kind="ExternalInput").ap()
    de = nc.dram_tensor("down_embed", [4096, D], F32,
                        kind="ExternalInput").ap()
    ue = nc.dram_tensor("up_embed", [4096, D], F32,
                        kind="ExternalInput").ap()
    cstf = nc.dram_tensor("cstf", [128 * 128], F32, kind="ExternalInput").ap()
    rout = nc.dram_tensor("rout", [NSEQ * T, D], F32,
                          kind="ExternalOutput").ap()
    cm = Common(p, xt_tokens=1024)
    cm.emit_setup(cstf)
    emit_l1_ffn(p, cm, rin, rout, wg, bg,
                wu, bu, wd, bd, wqp,
                keys.rearrange("h k t n -> (h k) (t n)"), de, ue)
    p.finish()
    return p



def emit_l1_attn_a(p: Prog, cm: Common, rin, rout, wq, wk, wv):
    XT = cm.XT
    DH = 64
    wqb, _ = cast_weight(p, "aawqb", wq, [DH, DH])
    wkb, _ = cast_weight(p, "aawkb", wk, [DH, DH])
    wvb, _ = cast_weight(p, "aawvb", wv, [DH, DH])
    tok_cast = p.tok("CAST")

    scl = p.dram("aa_scl", [NSEQ, 16 * T], F32)
    lmax = p.dram("aa_lmax", [128, 256], F32)
    gmax = p.dram("aa_gmax", [128, 256], F32)
    lsum = p.dram("aa_lsum", [128, 256], F32)
    gsum = p.dram("aa_gsum", [128, 256], F32)
    prb = p.dram("aa_prb", [NSEQ, 16 * T], F32)

    WS = p.abuf("aaWS", [64, 3 * DH], BF16)
    WT = [p.abuf(f"aaWT{i}", [128, 3 * DH], BF16) for i in range(2)]
    QA = p.abuf("aaQA", [64, T], F32)
    PROD = p.abuf("aaPROD", [64, T], BF16)
    ONEC = p.abuf("aaONEC", [64, 1], BF16)
    SR = p.abuf("aaSR", [1, T], F32)
    VS = p.abuf("aaVS", [64, 2], F32)
    VSB = p.abuf("aaVSB", [64, 1], BF16)
    VSR = p.abuf("aaVSR", [1, 16 * DH * NSEQ], BF16)
    PRB = p.abuf("aaPRB", [1, 16 * T], BF16)
    SA = p.abuf("aaSA", [128, 256], F32)
    SB = p.abuf("aaSB", [128, 256 * 5], F32)

    p.emit("sp", lambda s: s.dma_start(out=WS[:, 0:64], in_=wqb.ap()),
           dma=True, wait=[("CAST", tok_cast)], inc=("WO", 16))
    p.emit("sp", lambda s: s.dma_start(out=WS[:, 64:128], in_=wkb.ap()),
           dma=True, wait=[("CAST", tok_cast)], inc=("WO", 16))
    p.emit("sp", lambda s: s.dma_start(out=WS[:, 128:192], in_=wvb.ap()),
           dma=True, wait=[("CAST", tok_cast)], inc=("WO", 16))
    tok_ws = p.tok("WO")
    for i in range(2):
        p.emit("dve", lambda v, i=i: v.memset(WT[i][:], 0.0),
               writes=[f"aawt{i}"], inc=("DVE", 1))
    p.emit("gp", lambda g: g.memset(ONEC[:], 1.0), writes=["aaonec"],
           inc=("GPC", 1))
    tok_one = p.tok("GPC")
    tok_wz = p.tok("DVE")
    for m in range(3):
        for half in range(2):
            pb = half * 64
            tok_tp = p.emit("pe", lambda t, m=m, pb=pb: t.transpose(
                cm.PST[0][pb:pb + 64, 0:64],
                WS[0:64, m * 64:(m + 1) * 64], cm.IDENT[0:64, 0:64]),
                reads=["aaws", "ident"], writes=["pst0"],
                wait=[("WO", tok_ws), ("GPC", cm.tok_setup_gp)] +
                     cm.pst_free[0],
                inc=("PE", 1))
            tok_ev = p.emit("act", lambda a, m=m, pb=pb, half=half: a.copy(
                WT[half][pb:pb + 64, m * 64:(m + 1) * 64],
                cm.PST[0][pb:pb + 64, 0:64]),
                reads=["pst0"], writes=[f"aawt{half}"],
                wait=[("PE", tok_tp), ("DVE", tok_wz)], inc=("ACT", 1))
            cm.pst_free[0] = [("ACT", tok_ev)]
    tok_wt = p.tok("ACT")

    sc_store = []
    sr_free = []
    for c in range(NSEQ):
        rin_seq = rin[c * T:(c + 1) * T, :]
        cm.emit_ln_seq(rin_seq)
        tok_xt = p.tok("ACT")
        for h in range(16):
            par = h % 2
            chunk = h // 2
            last = None
            for n4 in range(4):
                last = p.emit("pe", lambda t, n4=n4, par=par, chunk=chunk:
                              t.matmul(
                    cm.PSM[0:64, n4 * 512:(n4 + 1) * 512],
                    WT[par][:, 0:64],
                    XT[:, chunk * T + n4 * 512:chunk * T + (n4 + 1) * 512],
                    start=True, stop=True),
                    reads=["xt", f"aawt{par}"], writes=["psm"],
                    wait=[("ACT", tok_wt), ("ACT", tok_xt)] + cm.psm_free,
                    inc=("PE", 1) if n4 == 3 else None)
            tok_q = p.emit("dve", lambda v: v.tensor_copy(
                QA[:], cm.PSM[0:64, 0:T]), reads=["psm", "aaqa"],
                writes=["aaqa"], wait=[("PE", last)], inc=("DVE", 1))
            cm.psm_free = [("DVE", tok_q)]
            last = None
            for n4 in range(4):
                last = p.emit("pe", lambda t, n4=n4, par=par, chunk=chunk:
                              t.matmul(
                    cm.PSM[0:64, n4 * 512:(n4 + 1) * 512],
                    WT[par][:, 64:128],
                    XT[:, chunk * T + n4 * 512:chunk * T + (n4 + 1) * 512],
                    start=True, stop=True),
                    reads=["xt", f"aawt{par}"], writes=["psm"],
                    wait=cm.psm_free, inc=("PE", 1) if n4 == 3 else None)
            tok_pr = p.emit("dve", lambda v: v.tensor_tensor(
                PROD[:], QA[:], cm.PSM[0:64, 0:T], ALU.mult),
                reads=["psm", "aaqa", "aaprod"], writes=["aaprod"],
                wait=[("PE", last)], inc=("DVE", 1))
            cm.psm_free = [("DVE", tok_pr)]
            last = None
            for n4 in range(4):
                last = p.emit("pe", lambda t, n4=n4: t.matmul(
                    cm.PSM[0:1, n4 * 512:(n4 + 1) * 512],
                    ONEC[:], PROD[:, n4 * 512:(n4 + 1) * 512],
                    start=True, stop=True),
                    reads=["aaprod", "aaonec"], writes=["psm"],
                    wait=[("GPC", tok_one), ("DVE", tok_pr)] + cm.psm_free,
                    inc=("PE", 1) if n4 == 3 else None)
            tok_sr = p.emit("act", lambda a: a.activation(
                out=SR[:], in_=cm.PSM[0:1, 0:T], func=AF.Copy,
                scale=1.0 / 32.0),
                reads=["psm", "aasr"], writes=["aasr"],
                wait=[("PE", last)] + sr_free, inc=("ACT", 1))
            cm.psm_free = [("ACT", tok_sr)]
            tok_scst = p.emit("sp", lambda s, c=c, h=h: s.dma_start(
                out=scl.ap()[c, h * T:(h + 1) * T].rearrange(
                    "(a b) -> a b", a=1),
                in_=SR[:]), dma=True, wait=[("ACT", tok_sr)],
                inc=("SC", 16))
            sc_store.append(tok_scst)
            sr_free = [("SC", tok_scst)]
            last = None
            for n4 in range(4):
                last = p.emit("pe", lambda t, n4=n4, par=par, chunk=chunk:
                              t.matmul(
                    cm.PSM[0:64, n4 * 512:(n4 + 1) * 512],
                    WT[par][:, 128:192],
                    XT[:, chunk * T + n4 * 512:chunk * T + (n4 + 1) * 512],
                    start=True, stop=True),
                    reads=["xt", f"aawt{par}"], writes=["psm"],
                    wait=cm.psm_free, inc=("PE", 1) if n4 == 3 else None)
            p.emit("dve", lambda v: v.reduce_sum(
                VS[:, 0:1], cm.PSM[0:64, 0:T], axis=AX.X),
                reads=["psm", "aavs"], writes=["aavs"],
                wait=[("PE", last)], inc=("DVE", 1))
            tok_vsb = p.emit("dve", lambda v: v.tensor_copy(
                VSB[:], VS[:, 0:1]), reads=["aavs", "aavsb"],
                writes=["aavsb"], inc=("DVE", 1))
            cm.psm_free = [("DVE", tok_vsb)]
            tok_vt = p.emit("pe", lambda t: t.transpose(
                cm.PST[1][0:1, 0:64], VSB[:], cm.IDENT[0:64, 0:64]),
                reads=["aavsb", "ident"], writes=["pst1"],
                wait=[("DVE", tok_vsb)] + cm.pst_free[1], inc=("PE", 1))
            tok_vr = p.emit("act", lambda a, c=c, h=h: a.copy(
                VSR[:, (c * 16 + h) * DH:(c * 16 + h + 1) * DH],
                cm.PST[1][0:1, 0:64]),
                reads=["pst1"], writes=["aavsr"],
                wait=[("PE", tok_vt)], inc=("ACT", 1))
            cm.pst_free[1] = [("ACT", tok_vr)]
    tok_ph1_dma = p.tok("SC")
    tok_ph1_act = p.tok("ACT")

    p.emit("sp", lambda s: s.dma_start(
        out=SA[:], in_=scl.ap()[0, :].rearrange("(a b) -> a b", a=128)),
        dma=True, wait=[("SC", tok_ph1_dma)], inc=("WO", 16))
    p.emit("sp", lambda s: s.dma_start(
        out=SB[:, 0:256], in_=scl.ap()[1, :].rearrange("(a b) -> a b",
                                                       a=128)),
        dma=True, wait=[("SC", tok_ph1_dma)], inc=("WO", 16))
    tok_s2 = p.tok("WO")
    tok_m = p.emit("dve", lambda v: v.tensor_max(
        SB[:, 256:512], SA[:], SB[:, 0:256]),
        reads=["aasa", "aasb"], writes=["aasb"],
        wait=[("WO", tok_s2)], inc=("DVE", 1))
    p.emit("sp", lambda s: s.dma_start(
        out=lmax.ap(), in_=SB[:, 256:512]),
        dma=True, wait=[("DVE", tok_m)], inc=("SC", 16))
    tok_lm = p.tok("SC")
    tok_cc1 = p.emit("gp", lambda g: g.collective_compute(
        "AllReduce", ALU.max, replica_groups=[list(range(NCORES))],
        ins=[lmax.ap().opt()], outs=[gmax.ap().opt()]),
        wait=[("SC", tok_lm)], inc=("CC", 1))
    p.emit("sp", lambda s: s.dma_start(
        out=SB[:, 256:512], in_=gmax.ap()),
        dma=True, wait=[("CC", tok_cc1)], inc=("WO", 16))
    tok_gm = p.tok("WO")
    for c in range(NSEQ):
        src_t = SA if c == 0 else None
        p.emit("dve", lambda v, c=c: v.tensor_tensor(
            SB[:, (3 + c) * 256:(4 + c) * 256],
            (SA[:] if c == 0 else SB[:, 0:256]), SB[:, 256:512],
            ALU.subtract),
            reads=["aasa", "aasb"], writes=["aasb"],
            wait=[("WO", tok_gm)], inc=("DVE", 1))
        p.emit("act", lambda a, c=c: a.activation(
            out=SB[:, (3 + c) * 256:(4 + c) * 256],
            in_=SB[:, (3 + c) * 256:(4 + c) * 256], func=AF.Exp),
            reads=["aasb"], writes=["aasb2"],
            wait=[("DVE", p.tok("DVE"))], inc=("ACT", 1))
    tok_exps = p.tok("ACT")
    tok_ls = p.emit("dve", lambda v: v.tensor_add(
        SB[:, 512:768], SB[:, 768:1024], SB[:, 1024:1280]),
        reads=["aasb", "aasb2"], writes=["aasb"],
        wait=[("ACT", tok_exps)], inc=("DVE", 1))
    p.emit("sp", lambda s: s.dma_start(
        out=lsum.ap(), in_=SB[:, 512:768]),
        dma=True, wait=[("DVE", tok_ls)], inc=("SC", 16))
    tok_ls2 = p.tok("SC")
    tok_cc2 = p.emit("gp", lambda g: g.collective_compute(
        "AllReduce", ALU.add, replica_groups=[list(range(NCORES))],
        ins=[lsum.ap().opt()], outs=[gsum.ap().opt()]),
        wait=[("SC", tok_ls2)], inc=("CC", 1))
    p.emit("sp", lambda s: s.dma_start(
        out=SB[:, 512:768], in_=gsum.ap()),
        dma=True, wait=[("CC", tok_cc2)], inc=("WO", 16))
    tok_gs = p.tok("WO")
    tok_rc = p.emit("dve", lambda v: v.reciprocal(
        SA[:], SB[:, 512:768]), reads=["aasb"], writes=["aasa"],
        wait=[("WO", tok_gs)], inc=("DVE", 1))
    for c in range(NSEQ):
        tok_pc = p.emit("dve", lambda v, c=c: v.tensor_mul(
            SB[:, (3 + c) * 256:(4 + c) * 256],
            SB[:, (3 + c) * 256:(4 + c) * 256], SA[:]),
            reads=["aasa", "aasb", "aasb2"], writes=["aasb"],
            wait=[("DVE", tok_rc)], inc=("DVE", 1))
        p.emit("sp", lambda s, c=c: s.dma_start(
            out=prb.ap()[c, :].rearrange("(a b) -> a b", a=128),
            in_=SB[:, (3 + c) * 256:(4 + c) * 256]),
            dma=True, wait=[("DVE", tok_pc)], inc=("SC", 16))
    tok_prb_st = p.tok("SC")

    for c in range(NSEQ):
        rin_seq = rin[c * T:(c + 1) * T, :]
        rout_seq = rout[c * T:(c + 1) * T, :]
        for h in range(16):
            tok_l = p.emit("sp", lambda s, c=c, h=h: s.dma_start(
                out=SR[:], in_=prb.ap()[c, h * T:(h + 1) * T].rearrange(
                    "(a b) -> a b", a=1)),
                dma=True, wait=[("SC", tok_prb_st), ("ACT", p.tok("ACT"))],
                inc=("WO", 16))
            p.emit("act", lambda a, h=h: a.copy(
                PRB[:, h * T:(h + 1) * T], SR[:]),
                reads=["aasr"], writes=["aaprb"],
                wait=[("WO", tok_l)], inc=("ACT", 1))
        tok_prb_all = p.tok("ACT")
        for tb in range(NT):
            par = tb % 2
            RT = cm.RT[par]
            tok_ld = p.emit(
                "sp", lambda s, RT=RT, tb=tb, rin_seq=rin_seq: s.dma_start(
                    out=RT[:], in_=rin_seq[tb * 128:(tb + 1) * 128, :]),
                dma=True, wait=cm.rt_free[par] + cm.rt_store[par],
                inc=(f"RL{par}", 16))
            lasts = [None, None]
            for h in range(16):
                bx = h // 8
                px = cm.PSX[bx]
                lasts[bx] = p.emit(
                    "pe", lambda t, h=h, tb=tb, c=c, px=px, bx=bx: t.matmul(
                        px[:, (h % 8) * 64:(h % 8 + 1) * 64],
                        PRB[0:1, h * T + tb * 128:h * T + (tb + 1) * 128],
                        VSR[0:1, (c * 16 + h) * DH:(c * 16 + h + 1) * DH],
                        start=(h % 8 == 0), stop=(h % 8 == 7)),
                    reads=["aaprb", "aavsr"], writes=[f"psx{bx}"],
                    wait=[("ACT", tok_prb_all), ("ACT", tok_ph1_act)] +
                         (cm.psx_free[bx] if h % 8 == 0 else []),
                    inc=("PE", 1) if h % 8 == 7 else None)
            tok_a1 = p.emit("dve", lambda v, RT=RT: v.tensor_tensor(
                RT[:, 0:512], RT[:, 0:512], cm.PSX[0][:], ALU.add),
                reads=[f"RT{par}", "psx0"], writes=[f"RT{par}"],
                wait=[("PE", lasts[0]), (f"RL{par}", tok_ld)],
                inc=("DVE", 1))
            tok_a2 = p.emit("dve", lambda v, RT=RT: v.tensor_tensor(
                RT[:, 512:1024], RT[:, 512:1024], cm.PSX[1][:], ALU.add),
                reads=[f"RT{par}", "psx1"], writes=[f"RT{par}"],
                wait=[("PE", lasts[1])], inc=("DVE", 1))
            cm.psx_free[0] = [("DVE", tok_a1)]
            cm.psx_free[1] = [("DVE", tok_a2)]
            cm.rt_free[par] = [("DVE", tok_a2)]
            tok_st = p.emit(
                "sp", lambda s, RT=RT, tb=tb, rout_seq=rout_seq: s.dma_start(
                    out=rout_seq[tb * 128:(tb + 1) * 128, :], in_=RT[:]),
                dma=True, wait=[("DVE", tok_a2)], inc=(f"ST{par}", 16))
            cm.rt_store[par] = [(f"ST{par}", tok_st)]
    p.barrier("sp", [("ST0", p.tok("ST0")), ("ST1", p.tok("ST1"))])


def build_l1_attn_a():
    p = Prog()
    nc = p.nc
    rin = nc.dram_tensor("rin", [NSEQ * T, D], F32, kind="ExternalInput").ap()
    wq = nc.dram_tensor("wq_a", [64, 64], F32, kind="ExternalInput").ap()
    wk = nc.dram_tensor("wk_a", [64, 64], F32, kind="ExternalInput").ap()
    wv = nc.dram_tensor("wv_a", [64, 64], F32, kind="ExternalInput").ap()
    cstf = nc.dram_tensor("cstf", [128 * 128], F32, kind="ExternalInput").ap()
    rout = nc.dram_tensor("rout", [NSEQ * T, D], F32,
                          kind="ExternalOutput").ap()
    cm = Common(p)
    cm.emit_setup(cstf)
    emit_l1_attn_a(p, cm, rin, rout, wq, wk, wv)
    p.finish()
    return p


def build_l0_attn_a():
    p = Prog()
    nc = p.nc
    rin = nc.dram_tensor("rin", [NSEQ * T, D], F32, kind="ExternalInput").ap()
    watt = nc.dram_tensor("w_attn", [3 * D, D], F32,
                          kind="ExternalInput").ap()
    wproj = nc.dram_tensor("w_proj", [D, D], F32, kind="ExternalInput").ap()
    cstf = nc.dram_tensor("cstf", [128 * 128], F32, kind="ExternalInput").ap()
    cstb = nc.dram_tensor("cstb", [CSTB_LEN], BF16, kind="ExternalInput").ap()
    rout = nc.dram_tensor("rout", [NSEQ * T, D], F32,
                          kind="ExternalOutput").ap()
    cm = Common(p)
    cm.emit_setup(cstf)
    emit_mha16(p, cm, rin, rout, watt, wproj, True, cstb, "aa")
    p.finish()
    return p


def build_l1_attn_b():
    p = Prog()
    nc = p.nc
    rin = nc.dram_tensor("rin", [NSEQ * T, D], F32, kind="ExternalInput").ap()
    wqkv = nc.dram_tensor("w_qkv", [3, 16, 64, D], F32,
                          kind="ExternalInput").ap()
    wo = nc.dram_tensor("w_o", [D, D], F32, kind="ExternalInput").ap()
    cstf = nc.dram_tensor("cstf", [128 * 128], F32, kind="ExternalInput").ap()
    rout = nc.dram_tensor("rout", [NSEQ * T, D], F32,
                          kind="ExternalOutput").ap()
    cm = Common(p)
    cm.emit_setup(cstf)
    emit_mha16(p, cm, rin, rout,
               wqkv.rearrange("q h d c -> (q h d) c"), wo, False, None, "bb")
    p.finish()
    return p


def build_fused():
    p = Prog()
    p.init_arena(212736)
    nc = p.nc
    A = lambda n, s: nc.dram_tensor(n, list(s), F32,
                                    kind="ExternalInput").ap()
    rin = A("rin", [NSEQ * T, D])
    w_attn = A("w_attn", [3 * D, D])
    w_proj = A("w_proj", [D, D])
    gate_w = A("gate_w", [8, D])
    w1 = A("w1", [8, 2048, D])
    w2 = A("w2", [8, D, 2048])
    w3 = A("w3", [8, 2048, D])
    wq_b = A("wq", [D, 128])
    wk_b = A("wk", [D, 128])
    wv_b = A("wv", [D, 128])
    wo_b = A("wo", [128, D])
    wq_a = A("wq_a", [64, 64])
    wk_a = A("wk_a", [64, 64])
    wv_a = A("wv_a", [64, 64])
    w_qkv = A("w_qkv", [3, 16, 64, D])
    w_o = A("w_o", [D, D])
    wg = A("wg", [D, 4096])
    bg = A("bg", [4096])
    wu = A("wu", [D, 4096])
    bu = A("bu", [4096])
    wd = A("wd", [4096, D])
    bd = A("bd", [D])
    wqp = A("wq_p", [D, 512])
    keys = A("keys", [4, 64, 2, 64])
    de = A("down_embed", [4096, D])
    ue = A("up_embed", [4096, D])
    cstf = nc.dram_tensor("cstf", [128 * 128], F32, kind="ExternalInput").ap()
    cstb = nc.dram_tensor("cstb", [CSTB_LEN], BF16, kind="ExternalInput").ap()
    rout = nc.dram_tensor("rout", [NSEQ * T, D], F32,
                          kind="ExternalOutput").ap()
    RA = nc.dram_tensor("resid_a", [NSEQ * T, D], F32).ap()
    RB = nc.dram_tensor("resid_b", [NSEQ * T, D], F32).ap()

    cm = Common(p)
    cm.emit_setup(cstf)

    steps = [
        (T, lambda i, o: emit_mha16(p, cm, i, o, w_attn, w_proj, True,
                                    cstb, "aa")),
        (1024, lambda i, o: emit_l0_ffn(
            p, cm, i, o, gate_w, w1.rearrange("e f c -> (e f) c"),
            w2.rearrange("e d f -> (e d) f"),
            w3.rearrange("e f c -> (e f) c"))),
        (T, lambda i, o: emit_l0_attn_b(p, cm, i, o, wq_b, wk_b, wv_b,
                                        wo_b)),
        (1024, lambda i, o: emit_l0_ffn(
            p, cm, i, o, gate_w, w1.rearrange("e f c -> (e f) c"),
            w2.rearrange("e d f -> (e d) f"),
            w3.rearrange("e f c -> (e f) c"))),
        (T, lambda i, o: emit_l1_attn_a(p, cm, i, o, wq_a, wk_a, wv_a)),
        (1024, lambda i, o: emit_l1_ffn(
            p, cm, i, o, wg, bg, wu, bu, wd, bd, wqp,
            keys.rearrange("h k t n -> (h k) (t n)"), de, ue)),
        (T, lambda i, o: emit_mha16(p, cm, i, o,
                                    w_qkv.rearrange("q h d c -> (q h d) c"),
                                    w_o, False, None, "bb")),
        (1024, lambda i, o: emit_l1_ffn(
            p, cm, i, o, wg, bg, wu, bu, wd, bd, wqp,
            keys.rearrange("h k t n -> (h k) (t n)"), de, ue)),
    ]
    chain = [rin, RA, RB, RA, RB, RA, RB, RA, rout]
    for i, (xtk, fn) in enumerate(steps):
        p.ascope()
        p.name_prefix = f"s{i}_"
        p.full_fence()
        cm.set_xt(xtk)
        fn(chain[i], chain[i + 1])
        p.pe_drain_on = False
        p.aend()
    p.name_prefix = ""
    n_ops = sum(len(v) for v in p.ops.values())
    p.finish()
    p._n_ops = n_ops
    return p


_PROG_CACHE = {}


def _get_prog(name):
    if name not in _PROG_CACHE:
        _PROG_CACHE[name] = {
            "l0aa": build_l0_attn_a,
            "l0f": build_l0_ffn,
            "l0ab": build_l0_attn_b,
            "l1aa": build_l1_attn_a,
            "l1f": build_l1_ffn,
            "l1ab": build_l1_attn_b,
        }[name]()
    return _PROG_CACHE[name]


def kernel(**inputs):
    f32 = lambda k: np.ascontiguousarray(np.asarray(inputs[k], np.float32))
    x = f32("L0_attn_a_x").reshape(16 * T, D)
    cstf, cstb = make_consts()
    if "fused" not in _PROG_CACHE:
        _PROG_CACHE["fused"] = build_fused()
    p = _PROG_CACHE["fused"]
    wmap = {
        "w_attn": f32("L0_attn_a_w_attn"), "w_proj": f32("L0_attn_a_w_proj"),
        "gate_w": f32("L0_ffn_gate_w"), "w1": f32("L0_ffn_w1"),
        "w2": f32("L0_ffn_w2"), "w3": f32("L0_ffn_w3"),
        "wq": f32("L0_attn_b_Wq"), "wk": f32("L0_attn_b_Wk"),
        "wv": f32("L0_attn_b_Wv"), "wo": f32("L0_attn_b_Wo"),
        "wq_a": f32("L1_attn_a_Wq"), "wk_a": f32("L1_attn_a_Wk"),
        "wv_a": f32("L1_attn_a_Wv"),
        "w_qkv": f32("L1_attn_b_W_qkv"), "w_o": f32("L1_attn_b_W_o"),
        "wg": f32("L1_ffn_Wg"), "bg": f32("L1_ffn_bg"),
        "wu": f32("L1_ffn_Wu"), "bu": f32("L1_ffn_bu"),
        "wd": f32("L1_ffn_Wd"), "bd": f32("L1_ffn_bd"),
        "wq_p": f32("L1_ffn_Wq"), "keys": f32("L1_ffn_keys"),
        "down_embed": f32("L1_ffn_down_embed"),
        "up_embed": f32("L1_ffn_up_embed"),
        "cstf": cstf, "cstb": cstb,
    }
    in_maps = []
    for c in range(NCORES):
        m = dict(wmap)
        m["rin"] = x[c * NSEQ * T:(c + 1) * NSEQ * T]
        in_maps.append(m)
    res = run_bass_kernel_spmd(p.nc, in_maps, core_ids=list(range(NCORES)))
    out = np.concatenate([r["rout"] for r in res.results], axis=0)
    return out.reshape(16, T, D).astype(np.float32)


# version 10 (final)
# speedup vs baseline: 1.0450x; 1.0004x over previous
import math
import numpy as np
import ml_dtypes
import concourse.bass as bass
import concourse.mybir as mybir
from concourse.bass import IndirectOffsetOnAxis
from concourse.bass_utils import run_bass_kernel_spmd

F32 = mybir.dt.float32
BF16 = mybir.dt.bfloat16
I32 = mybir.dt.int32
U32 = mybir.dt.uint32
AF = mybir.ActivationFunctionType
ALU = mybir.AluOpType
AX = mybir.AxisListType

NCORES = 8
T = 2048
D = 1024
NSEQ = 2
NT = T // 128
NC = D // 128

ENGINES = ("sp", "pe", "dve", "act", "gp")


class Prog:

    def __init__(self):
        self.nc = bass.Bass("TRN2", target_bir_lowering=False)
        self.ops = {e: [] for e in ENGINES}
        self.sems = {}
        self.counts = {}
        self.waited = {e: {} for e in ENGINES}
        self.dirty_w = {e: set() for e in ENGINES}
        self.dirty_r = {e: set() for e in ENGINES}
        self.AR = None
        self.ar_off = 0
        self.ar_mark = 0
        self.ar_peak = 0
        self.name_prefix = ""
        self.fence = []

    def init_arena(self, nbytes=204800):
        assert nbytes % 4 == 0
        self.AR = self.nc.alloc_sbuf_tensor("ARENA", [128, nbytes // 4], F32)
        self.ar_bytes = nbytes

    def abuf(self, name, shape, dtype):
        if self.AR is None:
            return self.nc.alloc_sbuf_tensor(
                self.name_prefix + name, list(shape), dtype)[:]
        P, n = shape
        esz = mybir.dt.size(dtype)
        off = (self.ar_off + 63) & ~63
        self.ar_off = off + n * esz
        self.ar_peak = max(self.ar_peak, self.ar_off)
        assert self.ar_off <= self.ar_bytes, (
            f"arena overflow at {name}: {self.ar_off} > {self.ar_bytes}")
        view = self.AR[:].bitcast(dtype)[0:P, off // esz:off // esz + n]
        return view

    def ascope(self):
        self.ar_mark = self.ar_off

    def aend(self):
        self.ar_off = self.ar_mark

    def full_fence(self):
        toks = [(s, self.counts[s]) for s in list(self.sems)
                if self.counts.get(s, 0) > 0]
        self.fence = list(toks)
        for eng in ENGINES:
            self.barrier(eng, toks)

    def sem(self, name):
        if name not in self.sems:
            self.sems[name] = self.nc.alloc_semaphore(name)
            self.counts[name] = 0
        return self.sems[name]

    def sbuf(self, name, shape, dtype):
        return self.nc.alloc_sbuf_tensor(name, list(shape), dtype)

    def psum(self, name, shape, dtype=F32):
        return self.nc.alloc_psum_tensor(name, list(shape), dtype)

    def dram(self, name, shape, dtype, kind="Internal"):
        return self.nc.dram_tensor(self.name_prefix + name, list(shape),
                                   dtype, kind=kind)

    def tok(self, semname):
        self.sem(semname)
        return self.counts[semname]

    def now(self, *semnames):
        return [(s, self.tok(s)) for s in semnames]

    def emit(self, eng, fn, reads=(), writes=(), wait=(), inc=None,
             dma=False, nodrain=False):
        waits = []
        for semname, val in wait:
            if val <= 0:
                continue
            self.sem(semname)
            if self.waited[eng].get(semname, 0) < val:
                waits.append((semname, val))
                self.waited[eng][semname] = val
        need_drain = False
        if eng == "pe" and not getattr(self, "pe_drain_on", False):
            nodrain = True
        if not nodrain:
            rset, wset = set(reads), set(writes)
            if dma:
                if rset & self.dirty_w[eng]:
                    need_drain = True
            elif (rset & self.dirty_w[eng]) or (
                    wset & (self.dirty_w[eng] | self.dirty_r[eng])):
                need_drain = True
        token = None
        if inc is not None:
            semname, k = inc
            self.sem(semname)
            self.counts[semname] += k
            token = self.counts[semname]
        sems = self.sems

        def closure(engine, waits=tuple(waits), need_drain=need_drain,
                    fn=fn, inc=inc):
            for semname, val in waits:
                engine.wait_ge(sems[semname], val)
            if need_drain:
                engine.drain()
            instr = fn(engine)
            if inc is not None:
                assert instr is not None, "inc requires fn returning instr"
                instr.then_inc(sems[inc[0]], inc[1])

        self.ops[eng].append(closure)
        if need_drain:
            self.dirty_w[eng].clear()
            self.dirty_r[eng].clear()
        if not dma:
            self.dirty_w[eng].update(writes)
            self.dirty_r[eng].update(reads)
        return token

    def barrier(self, eng, wait):
        self.emit(eng, lambda e: None, wait=wait, nodrain=True)

    def finish(self):
        nc, ops = self.nc, self.ops
        with nc.Block() as block:
            @block.sync
            def _(sync):
                for fn in ops["sp"]:
                    fn(sync)

            @block.tensor
            def _(tensor):
                for fn in ops["pe"]:
                    fn(tensor)

            @block.vector
            def _(vector):
                for fn in ops["dve"]:
                    fn(vector)

            @block.scalar
            def _(scalar):
                for fn in ops["act"]:
                    fn(scalar)

            @block.gpsimd
            def _(gpsimd):
                for fn in ops["gp"]:
                    fn(gpsimd)
        return nc



def alibi_slopes(n_heads):
    start = 2.0 ** (-2.0 ** (-(math.log2(n_heads) - 3)))
    return np.array([start * start ** i for i in range(n_heads)],
                    dtype=np.float32)


CSTB_QAUG = 0
CSTB_KAUG = 16 * 2 * T
CSTB_LEN = CSTB_KAUG + 2 * T


def make_consts():
    i = np.arange(128)
    mask = np.where(i[None, :] <= i[:, None], 0.0, -1e30).astype(np.float32)
    cstf = mask.reshape(-1)
    sl = alibi_slopes(16)
    qaug = np.zeros((16, 2, T), np.float32)
    qaug[:, 0, :] = (sl * 16.0)[:, None]
    qaug[:, 1, :] = sl[:, None]
    j = np.arange(T, dtype=np.float32)
    kaug = np.stack([np.floor(j / 16.0), j % 16.0], axis=0)
    cstb = np.concatenate([qaug.reshape(-1), kaug.reshape(-1)])
    return cstf, cstb.astype(ml_dtypes.bfloat16)



class Common:
    def __init__(self, p: Prog, xt_tokens=T):
        self.p = p
        self.xt_tokens = xt_tokens
        self.RT = [p.abuf(f"RT{i}", [128, 1024], F32) for i in range(2)]
        self.XLN = [p.abuf(f"XLN{i}", [128, 1024], BF16) for i in range(2)]
        self.ST = p.abuf("STAT", [128, 16], F32)
        self.XT = None
        if p.AR is None:
            self.set_xt(xt_tokens)
        self.CF = p.abuf("CMASK", [128, 128], F32)
        self.ONES = p.abuf("ONESR", [1, 2048], BF16)
        self.IDENT = p.abuf("IDENT", [128, 128], BF16)
        self.PSM = p.psum("PSM", [128, 2048], F32)
        self.PST = [p.psum(f"PST{i}", [128, 1024], BF16) for i in range(2)]
        self.PSX = [p.psum(f"PSX{i}", [128, 512], F32) for i in range(2)]
        self.psm_free = []
        self.pst_free = [[], []]
        self.psx_free = [[], []]
        self.rt_free = [[], []]
        self.rt_store = [[], []]
        self.xln_free = [[], []]
        self.st_free = []

    def set_xt(self, xt_tokens):
        self.xt_tokens = xt_tokens
        self.XT = self.p.abuf("XT", [128, NC * xt_tokens], BF16)

    def emit_setup(self, cstf_ap):
        p = self.p
        ID, ONES, CF = self.IDENT, self.ONES, self.CF
        p.emit("gp", lambda g: g.memset(ID[:], 1.0), writes=["ident"])
        p.emit("gp", lambda g: g.affine_select(
            out=ID[:], in_=ID[:], pattern=[[-1, 128]],
            compare_op=ALU.is_equal, fill=0.0, base=0, channel_multiplier=1),
            reads=["ident"], writes=["ident"], inc=("GPC", 1))
        p.emit("gp", lambda g: g.memset(ONES[:], 1.0), writes=["ones"],
               inc=("GPC", 1))
        self.tok_setup_gp = p.tok("GPC")
        p.emit("sp", lambda s: s.dma_start(
            out=CF[:], in_=cstf_ap.rearrange("(a b) -> a b", a=128)),
            dma=True, inc=("WO", 16))
        self.tok_setup_ld = p.tok("WO")

    def emit_ln_seq(self, rin_seq, extra_tile_cb=None, tile0=0,
                    ntiles=NT, xt_tokens=None):
        p = self.p
        ST = self.ST
        if xt_tokens is None:
            xt_tokens = self.xt_tokens
        for ti in range(tile0, tile0 + ntiles):
            par = ti % 2
            RT, XLN = self.RT[par], self.XLN[par]
            rkey, xkey = f"RT{par}", f"XLN{par}"
            tok_ld = p.emit(
                "sp", lambda s, RT=RT, ti=ti, rin_seq=rin_seq: s.dma_start(
                    out=RT[:], in_=rin_seq[ti * 128:(ti + 1) * 128, :]),
                dma=True, wait=self.rt_free[par] + self.rt_store[par],
                inc=(f"RL{par}", 16))
            p.emit("act", lambda a, RT=RT, XLN=XLN: a.activation(
                out=XLN[:], in_=RT[:], func=AF.Square,
                accum_out=ST[:, 0:1]),
                reads=[rkey], writes=[xkey, "st"],
                wait=[(f"RL{par}", tok_ld)] + self.xln_free[par] +
                     self.st_free)
            tok_sums = p.emit("act", lambda a, RT=RT, XLN=XLN: a.activation(
                out=XLN[:], in_=RT[:], func=AF.Identity,
                accum_out=ST[:, 1:2]),
                reads=[rkey], writes=[xkey, "st"], inc=("ACT", 1))
            p.emit("dve", lambda v: v.tensor_scalar_mul(
                ST[:, 2:3], ST[:, 1:2], 1.0 / 1024.0),
                reads=["st"], writes=["st"], wait=[("ACT", tok_sums)])
            p.emit("dve", lambda v: v.tensor_scalar(
                out=ST[:, 3:4], in0=ST[:, 2:3], scalar1=ST[:, 2:3],
                scalar2=-1e-5, op0=ALU.mult, op1=ALU.add),
                reads=["st"], writes=["st"])
            tok_var = p.emit("dve", lambda v: v.scalar_tensor_tensor(
                out=ST[:, 4:5], in0=ST[:, 0:1], scalar=1.0 / 1024.0,
                in1=ST[:, 3:4], op0=ALU.mult, op1=ALU.subtract),
                reads=["st"], writes=["st"], inc=("DVE", 1))
            p.emit("act", lambda a: a.activation(
                out=ST[:, 5:6], in_=ST[:, 4:5], func=AF.Ln),
                reads=["st"], writes=["st2"],
                wait=[("DVE", tok_var)])
            tok_rstd = p.emit("act", lambda a: a.activation(
                out=ST[:, 6:7], in_=ST[:, 5:6], func=AF.Exp,
                scale=-0.5), reads=["st2"], writes=["st2"], inc=("ACT", 1))
            tok_xln = p.emit("dve", lambda v, RT=RT, XLN=XLN: v.tensor_scalar(
                out=XLN[:], in0=RT[:], scalar1=ST[:, 2:3],
                scalar2=ST[:, 6:7], op0=ALU.subtract, op1=ALU.mult),
                reads=[rkey, "st", "st2"], writes=[xkey],
                wait=[("ACT", tok_rstd)], inc=("DVE", 1))
            self.st_free = [("DVE", tok_xln)]
            self.rt_free[par] = [("DVE", tok_xln)]
            if extra_tile_cb is not None:
                extra_tile_cb(ti, XLN, tok_xln)
            pst = self.PST[par]
            last = None
            for ci in range(NC):
                last = p.emit(
                    "pe", lambda t, ci=ci, XLN=XLN, pst=pst: t.transpose(
                        pst[:, ci * 128:(ci + 1) * 128],
                        XLN[:, ci * 128:(ci + 1) * 128], self.IDENT[:]),
                    reads=[xkey, "ident"], writes=[f"pst{par}"],
                    wait=([("DVE", tok_xln), ("GPC", self.tok_setup_gp)] +
                          self.pst_free[par]),
                    inc=("PE", 1) if ci == NC - 1 else None)
            xt_dst = self.XT[:, 0:NC * xt_tokens].rearrange(
                "p (c t) -> p c t", c=NC)[
                :, :, (ti - tile0) * 128:(ti - tile0 + 1) * 128]
            pst_src = pst[:].rearrange("p (c t) -> p c t", c=NC)[
                :, :, 0:128]
            tok_evac = p.emit("act", lambda a, d=xt_dst, s=pst_src: a.copy(
                d, s), reads=[f"pst{par}"], writes=["xt"],
                wait=[("PE", last)], inc=("ACT", 1))
            self.pst_free[par] = [("ACT", tok_evac)]
            self.xln_free[par] = [("ACT", tok_evac)]
        return ("ACT", p.tok("ACT"))

    def emit_residual_update(self, rin_seq, rout_seq, delta_cb):
        p = self.p
        for ti in range(NT):
            par = ti % 2
            RT = self.RT[par]
            tok_ld = p.emit(
                "sp", lambda s, RT=RT, ti=ti, rin_seq=rin_seq: s.dma_start(
                    out=RT[:], in_=rin_seq[ti * 128:(ti + 1) * 128, :]),
                dma=True, wait=self.rt_free[par] + self.rt_store[par],
                inc=(f"RL{par}", 16))
            dwait = delta_cb(ti)
            tok_add = p.emit("dve", lambda v, RT=RT: v.tensor_tensor(
                RT[:], RT[:], self.PSM[:, 0:1024], ALU.add),
                reads=[f"RT{par}", "psm"], writes=[f"RT{par}"],
                wait=dwait + [(f"RL{par}", tok_ld)], inc=("DVE", 1))
            self.psm_free = [("DVE", tok_add)]
            self.rt_free[par] = [("DVE", tok_add)]
            tok_st = p.emit(
                "sp", lambda s, RT=RT, ti=ti, rout_seq=rout_seq: s.dma_start(
                    out=rout_seq[ti * 128:(ti + 1) * 128, :], in_=RT[:]),
                dma=True, wait=[("DVE", tok_add)], inc=(f"ST{par}", 16))
            self.rt_store[par] = [(f"ST{par}", tok_st)]


def cast_weight(p, name, src_ap, shape):
    wb = p.dram(name, shape, BF16)
    tok = p.emit("gp", lambda g: g.dma_start(out=wb.ap(), in_=src_ap),
                 dma=True, inc=("CAST", 16))
    return wb, tok



def emit_mha16(p: Prog, cm: Common, rin, rout, watt, wproj, use_alibi,
               cstb, tag):
    XT = cm.XT
    NH, DH = 16, 64
    wattb, _ = cast_weight(p, f"{tag}_wattb", watt, [3 * D, D])
    wprob, _ = cast_weight(p, f"{tag}_wprob", wproj, [D, D])
    tok_cast = p.tok("CAST")

    WAT = p.abuf(f"{tag}WAT", [128, NC * 3 * D], BF16)
    for ci in range(NC):
        p.emit("sp", lambda s, ci=ci: s.dma_start_transpose(
            WAT[:, ci * 3 * D:(ci + 1) * 3 * D],
            wattb.ap()[:, ci * 128:(ci + 1) * 128]),
            dma=True, wait=[("CAST", tok_cast)], inc=("WO", 16))
    WPT = p.abuf(f"{tag}WPT", [128, NC * D], BF16)
    for ci in range(NC):
        p.emit("sp", lambda s, ci=ci: s.dma_start_transpose(
            WPT[:, ci * D:(ci + 1) * D],
            wprob.ap()[:, ci * 128:(ci + 1) * 128]),
            dma=True, wait=[("CAST", tok_cast)], inc=("WO", 16))
    tok_w = p.tok("WO")

    QH = p.abuf(f"{tag}QH", [128, T], BF16)
    KH = p.abuf(f"{tag}KH", [128, T], BF16)
    VB = p.abuf(f"{tag}VB", [128, NT * D], BF16)
    EXP = [p.abuf(f"{tag}EXP{i}", [128, T], BF16) for i in range(2)]
    ATTS = [p.abuf(f"{tag}ATTS{i}", [128, T], BF16) for i in range(2)]
    CTXS = p.abuf(f"{tag}CTXS", [128, NC * T], BF16)
    RS = p.abuf(f"{tag}RS", [128, 8], F32)
    kdim = 66 if use_alibi else 64
    scale = 1.0 / math.sqrt(DH)
    qh_free = []
    exp_free = [[], []]
    atts_free = [[], []]
    ctxs_free = []

    for seq in range(NSEQ):
        rin_seq = rin[seq * T:(seq + 1) * T, :]
        rout_seq = rout[seq * T:(seq + 1) * T, :]
        cm.emit_ln_seq(rin_seq)
        tok_xt = p.tok("ACT")

        for ti in range(NT):
            last = None
            for ci in range(NC):
                for nh2 in range(2):
                    last = p.emit(
                        "pe", lambda t, ci=ci, ti=ti, nh2=nh2: t.matmul(
                            cm.PSM[:, nh2 * 512:(nh2 + 1) * 512],
                            XT[:, ci * T + ti * 128:
                                  ci * T + (ti + 1) * 128],
                            WAT[:, ci * 3 * D + 2 * D + nh2 * 512:
                                ci * 3 * D + 2 * D + (nh2 + 1) * 512],
                            start=(ci == 0), stop=(ci == NC - 1)),
                        reads=["xt", "wat"], writes=["psm"],
                        wait=[("ACT", tok_xt), ("WO", tok_w)] + cm.psm_free,
                        inc=("PE", 1) if (ci == NC - 1 and nh2 == 1) else None)
            tok_e = p.emit("act", lambda a, ti=ti: a.copy(
                VB[:, ti * D:(ti + 1) * D], cm.PSM[:, 0:1024]),
                reads=["psm"], writes=["vb"], wait=[("PE", last)],
                inc=("ACT", 1))
            cm.psm_free = [("ACT", tok_e)]
        tok_vb = p.tok("ACT")

        proj_free = [cm.psm_free, cm.psm_free]
        for h in range(NH):
            for QK, sec, OUT, okey, sc in ((0, 0, QH, f"{tag}qh", scale),
                                           (1, 1, KH, f"{tag}kh", 1.0)):
                for nt4 in range(T // 512):
                    ppar = nt4 % 2
                    poff = ppar * 1024
                    last = None
                    for ci in range(NC):
                        roff = ci * 3 * D + sec * D + h * DH
                        last = p.emit(
                            "pe",
                            lambda t, ci=ci, nt4=nt4, roff=roff, poff=poff:
                            t.matmul(
                                cm.PSM[0:64, poff + (nt4 % 2) * 512:
                                       poff + (nt4 % 2) * 512 + 512],
                                WAT[:, roff:roff + DH],
                                XT[:, ci * T + nt4 * 512:
                                      ci * T + (nt4 + 1) * 512],
                                start=(ci == 0), stop=(ci == NC - 1)),
                            reads=["wat", "xt"], writes=["psm"],
                            wait=proj_free[ppar] + cm.psm_free,
                            inc=("PE", 1) if ci == NC - 1 else None)
                    tok_e = p.emit(
                        "act", lambda a, OUT=OUT, nt4=nt4, sc=sc, poff=poff:
                        a.activation(
                            out=OUT[0:64, nt4 * 512:(nt4 + 1) * 512],
                            in_=cm.PSM[0:64, poff + (nt4 % 2) * 512:
                                       poff + (nt4 % 2) * 512 + 512],
                            func=AF.Copy, scale=sc),
                        reads=["psm"], writes=[okey],
                        wait=[("PE", last)] + qh_free, inc=("ACT", 1))
                    proj_free[ppar] = [("ACT", tok_e)]
            cm.psm_free = proj_free[0] + proj_free[1]
            if use_alibi:
                p.emit("sp", lambda s, h=h: s.dma_start(
                    out=QH[64:66, :],
                    in_=cstb[CSTB_QAUG + h * 2 * T:
                             CSTB_QAUG + (h + 1) * 2 * T].rearrange(
                        "(a b) -> a b", a=2)),
                    dma=True, wait=qh_free, inc=("AUG", 16))
                if h == 0 and seq == 0:
                    p.emit("sp", lambda s: s.dma_start(
                        out=KH[64:66, :],
                        in_=cstb[CSTB_KAUG:CSTB_KAUG + 2 * T].rearrange(
                            "(a b) -> a b", a=2)),
                        dma=True, inc=("AUG", 16))
            tok_qk = p.tok("ACT")
            tok_qk_ld = p.tok("AUG")

            def ph1(qi):
                par = qi % 2
                klen = (qi + 1) * 128
                off = par * 1024 if qi < 8 else 0
                if qi < 8:
                    pwait = half_free[par] + (cm.psm_free if qi < 2 else [])
                else:
                    pwait = cm.psm_free + half_free[0] + half_free[1]
                nch = (klen + 511) // 512
                last = None
                for ch in range(nch):
                    w = min(512, klen - ch * 512)
                    last = p.emit("pe", lambda t, ch=ch, w=w, qi=qi,
                                  off=off, kdim=kdim: t.matmul(
                        cm.PSM[:, off + ch * 512:off + ch * 512 + w],
                        QH[0:kdim, qi * 128:(qi + 1) * 128],
                        KH[0:kdim, ch * 512:ch * 512 + w],
                        start=True, stop=True),
                        reads=[f"{tag}qh", f"{tag}kh"], writes=["psm"],
                        wait=[("ACT", tok_qk), ("AUG", tok_qk_ld)] + pwait,
                        inc=("PE", 1) if ch == nch - 1 else None)
                p.emit("dve", lambda v, qi=qi, off=off: v.tensor_tensor(
                    cm.PSM[:, off + qi * 128:off + (qi + 1) * 128],
                    cm.PSM[:, off + qi * 128:off + (qi + 1) * 128],
                    cm.CF[:], ALU.add),
                    reads=["psm", "cf"], writes=["psm"],
                    wait=[("PE", last), ("WO", cm.tok_setup_ld)],
                    inc=("DVE", 1))
                tok_max = p.emit(
                    "dve", lambda v, klen=klen, par=par, off=off:
                    v.reduce_max(
                        RS[:, par * 4:par * 4 + 1],
                        cm.PSM[:, off:off + klen],
                        axis=AX.X, negate=True),
                    reads=["psm", f"rs{par}"], writes=[f"rs{par}"],
                    inc=("DVE", 1))
                tok_exp = p.emit(
                    "act", lambda a, klen=klen, par=par, off=off:
                    a.activation(
                        out=EXP[par][:, 0:klen],
                        in_=cm.PSM[:, off:off + klen],
                        func=AF.Exp, bias=RS[:, par * 4:par * 4 + 1],
                        accum_out=RS[:, par * 4 + 1:par * 4 + 2]),
                    reads=["psm", f"rs{par}"],
                    writes=[f"{tag}exp{par}", f"rs{par}"],
                    wait=[("DVE", tok_max)] + exp_free[par],
                    inc=("ACT", 1))
                if qi < 8:
                    half_free[par] = [("ACT", tok_exp)]
                else:
                    cm.psm_free = [("ACT", tok_exp)]
                return tok_exp

            def ph2(qi, h, tok_exp):
                par = qi % 2
                klen = (qi + 1) * 128
                p.emit("dve", lambda v, par=par: v.reciprocal(
                    RS[:, par * 4 + 2:par * 4 + 3],
                    RS[:, par * 4 + 1:par * 4 + 2]),
                    reads=[f"rs{par}"], writes=[f"rs{par}"],
                    wait=[("ACT", tok_exp)], inc=("DVE", 1))
                tok_nrm = p.emit(
                    "dve", lambda v, klen=klen, par=par: v.tensor_scalar_mul(
                        EXP[par][:, 0:klen], EXP[par][:, 0:klen],
                        RS[:, par * 4 + 2:par * 4 + 3]),
                    reads=[f"{tag}exp{par}", f"rs{par}"],
                    writes=[f"{tag}exp{par}"], inc=("DVE", 1))
                nbk = qi + 1
                ev_toks = {}
                tp_last = {}
                for b in range((nbk + 7) // 8):
                    lo, hi = b * 8, min(nbk, b * 8 + 8)
                    for kb in range(lo, hi):
                        tp_last[b] = p.emit(
                            "pe", lambda t, kb=kb, b=b, par=par:
                            t.transpose(
                                cm.PST[b][:, (kb % 8) * 128:
                                         (kb % 8 + 1) * 128],
                                EXP[par][:, kb * 128:(kb + 1) * 128],
                                cm.IDENT[:]),
                            reads=[f"{tag}exp{par}", "ident"],
                            writes=[f"pst{b}"],
                            wait=[("DVE", tok_nrm)] + cm.pst_free[b],
                            inc=("PE", 1) if kb == hi - 1 else None)
                    ev_toks[b] = p.emit(
                        "act", lambda a, b=b, lo=lo, hi=hi, par=par: a.copy(
                            ATTS[par][:, lo * 128:hi * 128],
                            cm.PST[b][:, 0:(hi - lo) * 128]),
                        reads=[f"pst{b}"], writes=[f"{tag}atts{par}"],
                        wait=[("PE", tp_last[b])] + atts_free[par],
                        inc=("ACT", 1))
                    cm.pst_free[b] = [("ACT", ev_toks[b])]
                ctx_last = None
                for kb in range(nbk):
                    ctx_last = p.emit(
                        "pe", lambda t, kb=kb, qi=qi, h=h, par=par: t.matmul(
                            cm.PSX[par][0:64, 0:128],
                            VB[:, kb * D + h * DH:kb * D + (h + 1) * DH],
                            ATTS[par][:, kb * 128:(kb + 1) * 128],
                            start=(kb == 0), stop=(kb == nbk - 1)),
                        reads=["vb", f"{tag}atts{par}"],
                        writes=[f"psx{par}"],
                        wait=[("ACT", ev_toks[kb // 8]), ("ACT", tok_vb)] +
                             (cm.psx_free[par] if kb == 0 else []),
                        inc=("PE", 1) if kb == nbk - 1 else None)
                atts_free[par] = [("PE", ctx_last)]
                exp_free[par] = [("PE", ctx_last)]
                pb = (h % 2) * 64
                coff = (h // 2) * T + qi * 128
                tok_cv = p.emit(
                    "act", lambda a, pb=pb, coff=coff, par=par: a.copy(
                        CTXS[pb:pb + 64, coff:coff + 128],
                        cm.PSX[par][0:64, 0:128]),
                    reads=[f"psx{par}"], writes=[f"{tag}ctxs"],
                    wait=[("PE", ctx_last)] + ctxs_free, inc=("ACT", 1))
                cm.psx_free[par] = [("ACT", tok_cv)]

            half_free = [cm.psm_free, cm.psm_free]
            pend = None
            for qi in range(NT):
                te = ph1(qi)
                if pend is not None:
                    ph2(pend[0], h, pend[1])
                pend = (qi, te)
            ph2(pend[0], h, pend[1])
            cm.psm_free = cm.psm_free + half_free[0] + half_free[1]
            qh_free = [("PE", p.tok("PE"))]
        tok_ctx_done = p.tok("ACT")

        def delta_cb(ti, tok_ctx_done=tok_ctx_done):
            last = None
            for nh2 in range(2):
                for ci in range(NC):
                    last = p.emit(
                        "pe", lambda t, ti=ti, nh2=nh2, ci=ci: t.matmul(
                            cm.PSM[:, nh2 * 512:(nh2 + 1) * 512],
                            CTXS[:, ci * T + ti * 128:ci * T + (ti + 1) * 128],
                            WPT[:, ci * D + nh2 * 512:
                                ci * D + (nh2 + 1) * 512],
                            start=(ci == 0), stop=(ci == NC - 1)),
                        reads=[f"{tag}ctxs", "wpt"], writes=["psm"],
                        wait=[("ACT", tok_ctx_done)] + cm.psm_free,
                        inc=("PE", 1) if (ci == NC - 1 and nh2 == 1) else None)
            return [("PE", last)]

        cm.emit_residual_update(rin_seq, rout_seq, delta_cb)
        ctxs_free = [("PE", p.tok("PE"))]

    p.barrier("sp", [("ST0", p.tok("ST0")), ("ST1", p.tok("ST1"))])



def emit_l0_attn_b(p: Prog, cm: Common, rin, rout, wq, wk, wv, wo):
    XT = cm.XT
    DK = 128
    wqb, _ = cast_weight(p, "ab_wqb", wq, [D, DK])
    wkb, _ = cast_weight(p, "ab_wkb", wk, [D, DK])
    wvb, _ = cast_weight(p, "ab_wvb", wv, [D, DK])
    wob, _ = cast_weight(p, "ab_wob", wo, [DK, D])
    tok_cast = p.tok("CAST")

    WQ = p.abuf("abWQ", [128, NC * DK], BF16)
    WK = p.abuf("abWK", [128, NC * DK], BF16)
    WV = p.abuf("abWV", [128, NC * DK], BF16)
    WO = p.abuf("abWO", [128, D], BF16)
    for Ws, wb in ((WQ, wqb), (WK, wkb), (WV, wvb)):
        p.emit("sp", lambda s, Ws=Ws, wb=wb: s.dma_start(
            out=Ws[:].rearrange("p (c n) -> p c n", c=NC),
            in_=wb.ap().rearrange("(c p) n -> p c n", p=128)),
            dma=True, wait=[("CAST", tok_cast)], inc=("WO", 16))
    p.emit("sp", lambda s: s.dma_start(out=WO[:], in_=wob.ap()),
           dma=True, wait=[("CAST", tok_cast)], inc=("WO", 16))
    tok_w = p.tok("WO")

    QF = p.abuf("abQF", [128, T], BF16)
    KF = p.abuf("abKF", [128, T], BF16)
    VB = p.abuf("abVB", [128, NT * DK], BF16)
    EXP = p.abuf("abEXP", [128, T], BF16)
    ATTS = p.abuf("abATTS", [128, T], BF16)
    CTXS = p.abuf("abCTXS", [128, NT * DK], BF16)
    RS = p.abuf("abRS", [128, 4], F32)

    scale = 1.0 / math.sqrt(DK)
    exp_free = []
    atts_free = []

    for seq in range(NSEQ):
        rin_seq = rin[seq * T:(seq + 1) * T, :]
        rout_seq = rout[seq * T:(seq + 1) * T, :]
        cm.emit_ln_seq(rin_seq)
        tok_xt = p.tok("ACT")

        for OUT, Ws, okey, sc in ((QF, WQ, "abqf", scale),
                                  (KF, WK, "abkf", 1.0)):
            for nt4 in range(T // 512):
                last = None
                for ci in range(NC):
                    last = p.emit(
                        "pe", lambda t, ci=ci, nt4=nt4, Ws=Ws: t.matmul(
                            cm.PSM[:, nt4 * 512:(nt4 + 1) * 512],
                            Ws[:, ci * DK:(ci + 1) * DK],
                            XT[:, ci * T + nt4 * 512:
                                  ci * T + (nt4 + 1) * 512],
                            start=(ci == 0), stop=(ci == NC - 1)),
                        reads=["xt"], writes=["psm"],
                        wait=[("ACT", tok_xt), ("WO", tok_w)] + cm.psm_free,
                        inc=("PE", 1) if ci == NC - 1 else None)
                tok_e = p.emit(
                    "act", lambda a, OUT=OUT, nt4=nt4, sc=sc: a.activation(
                        out=OUT[:, nt4 * 512:(nt4 + 1) * 512],
                        in_=cm.PSM[:, nt4 * 512:(nt4 + 1) * 512],
                        func=AF.Copy, scale=sc),
                    reads=["psm"], writes=[okey], wait=[("PE", last)],
                    inc=("ACT", 1))
                cm.psm_free = [("ACT", tok_e)]

        for ti in range(NT):
            par = ti % 2
            px = cm.PSX[par]
            last = None
            for ci in range(NC):
                last = p.emit("pe", lambda t, ci=ci, ti=ti, px=px: t.matmul(
                    px[:, 0:DK],
                    XT[:, ci * T + ti * 128: ci * T + (ti + 1) * 128],
                    WV[:, ci * DK:(ci + 1) * DK],
                    start=(ci == 0), stop=(ci == NC - 1)),
                    reads=["xt"], writes=[f"psx{par}"],
                    wait=cm.psx_free[par],
                    inc=("PE", 1) if ci == NC - 1 else None)
            tok_e = p.emit("dve", lambda v, ti=ti, px=px: v.tensor_copy(
                VB[:, ti * DK:(ti + 1) * DK], px[:, 0:DK]),
                reads=[f"psx{par}"], writes=["abvb"],
                wait=[("PE", last)], inc=("DVE", 1))
            cm.psx_free[par] = [("DVE", tok_e)]
        tok_v = p.tok("DVE")

        for qi in range(NT):
            klen = (qi + 1) * 128
            nch = (klen + 511) // 512
            last = None
            for ch in range(nch):
                w = min(512, klen - ch * 512)
                last = p.emit("pe", lambda t, ch=ch, w=w, qi=qi: t.matmul(
                    cm.PSM[:, ch * 512:ch * 512 + w],
                    QF[:, qi * 128:(qi + 1) * 128],
                    KF[:, ch * 512:ch * 512 + w],
                    start=True, stop=True),
                    reads=["abqf", "abkf"], writes=["psm"],
                    wait=cm.psm_free,
                    inc=("PE", 1) if ch == nch - 1 else None)
            p.emit("dve", lambda v, qi=qi: v.tensor_tensor(
                cm.PSM[:, qi * 128:(qi + 1) * 128],
                cm.PSM[:, qi * 128:(qi + 1) * 128],
                cm.CF[:], ALU.add),
                reads=["psm", "cf"], writes=["psm"],
                wait=[("PE", last), ("WO", cm.tok_setup_ld)], inc=("DVE", 1))
            tok_max = p.emit("dve", lambda v, klen=klen: v.reduce_max(
                RS[:, 0:1], cm.PSM[:, 0:klen], axis=AX.X, negate=True),
                reads=["psm", "rs"], writes=["rs"], inc=("DVE", 1))
            tok_exp = p.emit("act", lambda a, klen=klen: a.activation(
                out=EXP[:, 0:klen], in_=cm.PSM[:, 0:klen], func=AF.Exp,
                bias=RS[:, 0:1], accum_out=RS[:, 1:2]),
                reads=["psm", "rs"], writes=["abexp", "rs"],
                wait=[("DVE", tok_max)] + exp_free, inc=("ACT", 1))
            cm.psm_free = [("ACT", tok_exp)]
            tok_rec = p.emit("dve", lambda v: v.reciprocal(
                RS[:, 2:3], RS[:, 1:2]), reads=["rs"], writes=["rs"],
                wait=[("ACT", tok_exp)], inc=("DVE", 1))
            tok_nrm = p.emit("dve", lambda v, klen=klen: v.tensor_scalar_mul(
                EXP[:, 0:klen], EXP[:, 0:klen], RS[:, 2:3]),
                reads=["abexp", "rs"], writes=["abexp"], inc=("DVE", 1))
            ctx_last = None
            for kb in range(qi + 1):
                bpar = kb % 2
                tok_tp = p.emit("pe", lambda t, kb=kb, bpar=bpar: t.transpose(
                    cm.PST[bpar][:, 0:128],
                    EXP[:, kb * 128:(kb + 1) * 128], cm.IDENT[:]),
                    reads=["abexp", "ident"], writes=[f"pst{bpar}"],
                    wait=[("DVE", tok_nrm)] + cm.pst_free[bpar],
                    inc=("PE", 1))
                tok_ev = p.emit("act", lambda a, kb=kb, bpar=bpar: a.copy(
                    ATTS[:, kb * 128:(kb + 1) * 128],
                    cm.PST[bpar][:, 0:128]),
                    reads=[f"pst{bpar}"], writes=["abatts"],
                    wait=[("PE", tok_tp)] + atts_free, inc=("ACT", 1))
                cm.pst_free[bpar] = [("ACT", tok_ev)]
                ctx_last = p.emit("pe", lambda t, kb=kb, qi=qi: t.matmul(
                    cm.PSX[0][:, 0:128],
                    VB[:, kb * DK:(kb + 1) * DK],
                    ATTS[:, kb * 128:(kb + 1) * 128],
                    start=(kb == 0), stop=(kb == qi)),
                    reads=["abvb", "abatts"], writes=["psx0"],
                    wait=[("ACT", tok_ev), ("DVE", tok_v)] +
                         (cm.psx_free[0] if kb == 0 else []),
                    inc=("PE", 1) if kb == qi else None)
            atts_free = [("PE", ctx_last)]
            exp_free = [("PE", ctx_last)]
            tok_cv = p.emit("dve", lambda v, qi=qi: v.tensor_copy(
                CTXS[:, qi * 128:(qi + 1) * 128], cm.PSX[0][:, 0:128]),
                reads=["psx0"], writes=["abctxs"],
                wait=[("PE", ctx_last)], inc=("DVE", 1))
            cm.psx_free[0] = [("DVE", tok_cv)]
        tok_ctx = p.tok("DVE")

        def delta_cb(ti, tok_ctx=tok_ctx):
            last = None
            for nh in range(2):
                last = p.emit("pe", lambda t, ti=ti, nh=nh: t.matmul(
                    cm.PSM[:, nh * 512:(nh + 1) * 512],
                    CTXS[:, ti * 128:(ti + 1) * 128],
                    WO[:, nh * 512:(nh + 1) * 512],
                    start=True, stop=True),
                    reads=["abctxs"], writes=["psm"],
                    wait=[("DVE", tok_ctx)] + cm.psm_free,
                    inc=("PE", 1) if nh == 1 else None)
            return [("PE", last)]

        cm.emit_residual_update(rin_seq, rout_seq, delta_cb)

    p.barrier("sp", [("ST0", p.tok("ST0")), ("ST1", p.tok("ST1"))])



def build_l0_attn_b():
    p = Prog()
    nc = p.nc
    rin = nc.dram_tensor("rin", [NSEQ * T, D], F32, kind="ExternalInput").ap()
    wq = nc.dram_tensor("wq", [D, 128], F32, kind="ExternalInput").ap()
    wk = nc.dram_tensor("wk", [D, 128], F32, kind="ExternalInput").ap()
    wv = nc.dram_tensor("wv", [D, 128], F32, kind="ExternalInput").ap()
    wo = nc.dram_tensor("wo", [128, D], F32, kind="ExternalInput").ap()
    cstf = nc.dram_tensor("cstf", [128 * 128], F32, kind="ExternalInput").ap()
    rout = nc.dram_tensor("rout", [NSEQ * T, D], F32,
                          kind="ExternalOutput").ap()
    cm = Common(p)
    cm.emit_setup(cstf)
    emit_l0_attn_b(p, cm, rin, rout, wq, wk, wv, wo)
    p.finish()
    return p



def emit_l0_ffn(p: Prog, cm: Common, rin, rout, gate_w, w1, w2, w3):
    XT = cm.XT
    NE, DF = 8, 2048
    gwb, _ = cast_weight(p, "fgwb", gate_w, [NE, D])
    w1b, _ = cast_weight(p, "fw1b", w1, [NE * DF, D])
    w3b, _ = cast_weight(p, "fw3b", w3, [NE * DF, D])
    w2b, _ = cast_weight(p, "fw2b", w2, [NE * D, DF])
    tok_cast = p.tok("CAST")

    GWS = p.abuf("fGWS", [8, D], BF16)
    GWT = p.abuf("fGWT", [128, NC * NE], BF16)
    p.emit("sp", lambda s: s.dma_start(out=GWS[:], in_=gwb.ap()),
           dma=True, wait=[("CAST", tok_cast)], inc=("WO", 16))
    tok_gw = p.tok("WO")
    for ci in range(NC):
        bpar = ci % 2
        tok_tp = p.emit("pe", lambda t, ci=ci, bpar=bpar: t.transpose(
            cm.PST[bpar][:, 0:8], GWS[0:8, ci * 128:(ci + 1) * 128],
            cm.IDENT[0:8, 0:8]),
            reads=["fgws", "ident"], writes=[f"pst{bpar}"],
            wait=[("WO", tok_gw), ("GPC", cm.tok_setup_gp)] +
                 cm.pst_free[bpar],
            inc=("PE", 1))
        tok_ev = p.emit("act", lambda a, ci=ci, bpar=bpar: a.copy(
            GWT[:, ci * NE:(ci + 1) * NE], cm.PST[bpar][:, 0:8]),
            reads=[f"pst{bpar}"], writes=["fgwt"],
            wait=[("PE", tok_tp)], inc=("ACT", 1))
        cm.pst_free[bpar] = [("ACT", tok_ev)]
    tok_gwt = p.tok("ACT")

    IOTA8 = p.abuf("fIOTA8", [128, 8], F32)
    p.emit("gp", lambda g: g.iota(
        IOTA8[:], pattern=[[1, 8]], base=0, channel_multiplier=0,
        allow_small_or_imprecise_dtypes=True), writes=["fiota"],
        inc=("GPC", 1))
    tok_iota = p.tok("GPC")

    W1T = p.abuf("fW1T", [128, NC * DF], BF16)
    W3T = p.abuf("fW3T", [128, NC * DF], BF16)
    W2TF = p.abuf("fW2TF", [128, NT * D], BF16)
    S1 = p.abuf("fS1", [128, 1024], BF16)
    GATED = p.abuf("fGATED", [128, NT * 1024], BF16)
    ACC = p.abuf("fACC", [128, 8 * 1024], F32)
    GT = p.abuf("fGT", [128, NT * NE], F32)
    GS = p.abuf("fGS", [128, 32], F32)
    GIX = p.abuf("fGIX", [128, 8], U32)

    state = {"w13_free": [], "w2t_free": [], "acc_free": [],
             "gated_free": []}

    def load_expert_w13(e):
        sem = f"W13{e % 2}"
        for (W, wb) in ((W1T, w1b), (W3T, w3b)):
            for ci in range(NC):
                p.emit(
                    "sp", lambda s, W=W, wb=wb, ci=ci, e=e:
                    s.dma_start_transpose(
                        W[:, ci * DF:(ci + 1) * DF],
                        wb.ap()[e * DF:(e + 1) * DF,
                                ci * 128:(ci + 1) * 128]),
                    dma=True, wait=([("CAST", tok_cast)] + state["w13_free"]),
                    inc=(sem, 16))
        return (sem, p.tok(sem))

    def load_expert_w2(e):
        sem = f"W2{e % 2}"
        for fc in range(NT):
            p.emit(
                "sp", lambda s, e=e, fc=fc: s.dma_start_transpose(
                    W2TF[:, fc * D:(fc + 1) * D],
                    w2b.ap()[e * D:(e + 1) * D, fc * 128:(fc + 1) * 128]),
                dma=True, wait=([("CAST", tok_cast)] + state["w2t_free"]),
                inc=(sem, 16))
        return (sem, p.tok(sem))

    for seq in range(NSEQ):
        rin_seq = rin[seq * T:(seq + 1) * T, :]
        rout_seq = rout[seq * T:(seq + 1) * T, :]
        for th in range(2):
            XTK = 1024
            cm.emit_ln_seq(rin_seq, tile0=th * 8, ntiles=8, xt_tokens=XTK)
            tok_xt = p.tok("ACT")

            for lt in range(8):
                px = cm.PSX[1]
                last = None
                for ci in range(NC):
                    last = p.emit(
                        "pe", lambda t, ci=ci, lt=lt, px=px: t.matmul(
                            px[:, 0:NE],
                            XT[:, ci * XTK + lt * 128:
                                  ci * XTK + (lt + 1) * 128],
                            GWT[:, ci * NE:(ci + 1) * NE],
                            start=(ci == 0), stop=(ci == NC - 1)),
                        reads=["xt", "fgwt"], writes=["psx1"],
                        wait=[("ACT", tok_xt), ("ACT", tok_gwt)] +
                             cm.psx_free[1],
                        inc=("PE", 1) if ci == NC - 1 else None)
                tok_sc = p.emit("dve", lambda v, px=px: v.tensor_copy(
                    GS[:, 0:8], px[:, 0:NE]), reads=["psx1", "fgs"],
                    writes=["fgs"], wait=[("PE", last)], inc=("DVE", 1))
                cm.psx_free[1] = [("DVE", tok_sc)]
                p.emit("dve", lambda v: v.max(GS[:, 8:16], GS[:, 0:8]),
                       reads=["fgs"], writes=["fgs"])
                p.emit("dve", lambda v: v.max_index(
                    GIX[:], GS[:, 8:16], GS[:, 0:8]),
                    reads=["fgs", "fgix"], writes=["fgix"])
                p.emit("dve", lambda v: v.tensor_copy(
                    GS[:, 16:18], GIX[:, 0:2]),
                    reads=["fgix", "fgs"], writes=["fgs"])
                tok_d = p.emit("dve", lambda v: v.tensor_tensor(
                    GS[:, 18:19], GS[:, 8:9], GS[:, 9:10], ALU.subtract),
                    reads=["fgs"], writes=["fgs"], inc=("DVE", 1))
                tok_sig = p.emit("act", lambda a: a.activation(
                    out=GS[:, 19:20], in_=GS[:, 18:19], func=AF.Sigmoid),
                    reads=["fgs"], writes=["fgs2"],
                    wait=[("DVE", tok_d)], inc=("ACT", 1))
                p.emit("dve", lambda v: v.tensor_scalar(
                    out=GS[:, 20:21], in0=GS[:, 19:20], scalar1=-1.0,
                    scalar2=1.0, op0=ALU.mult, op1=ALU.add),
                    reads=["fgs2", "fgs"], writes=["fgs"],
                    wait=[("ACT", tok_sig)])
                p.emit("dve", lambda v: v.tensor_scalar(
                    out=GS[:, 24:32], in0=IOTA8[:], scalar1=GS[:, 16:17],
                    scalar2=None, op0=ALU.is_equal),
                    reads=["fiota", "fgs"], writes=["fgs"],
                    wait=[("GPC", tok_iota)])
                p.emit("dve", lambda v, lt=lt: v.tensor_scalar_mul(
                    GT[:, lt * NE:(lt + 1) * NE], GS[:, 24:32],
                    GS[:, 19:20]),
                    reads=["fgs", "fgs2", "fgt"], writes=["fgt"])
                p.emit("dve", lambda v: v.tensor_scalar(
                    out=GS[:, 24:32], in0=IOTA8[:], scalar1=GS[:, 17:18],
                    scalar2=None, op0=ALU.is_equal),
                    reads=["fiota", "fgs"], writes=["fgs"])
                p.emit("dve", lambda v, lt=lt: v.scalar_tensor_tensor(
                    out=GT[:, lt * NE:(lt + 1) * NE], in0=GS[:, 24:32],
                    scalar=GS[:, 20:21], in1=GT[:, lt * NE:(lt + 1) * NE],
                    op0=ALU.mult, op1=ALU.add),
                    reads=["fgs", "fgt"], writes=["fgt"], inc=("DVE", 1))
            tok_gate = p.tok("DVE")

            tok_z = p.emit("dve", lambda v: v.memset(ACC[:], 0.0),
                           reads=[], writes=["facc"],
                           wait=state["acc_free"], inc=("DVE", 1))
            tok_w13 = load_expert_w13(0)
            tok_w2 = load_expert_w2(0)
            for e in range(NE):
                pst_h1 = [cm.PST[0][:].bitcast(F32), cm.PST[1][:].bitcast(F32)]
                alt_free = state.get("alt_free", [])
                for fc in range(NT):
                    par = fc % 2
                    if par == 0:
                        h1v = [cm.PSM[:, 0:512], cm.PSM[:, 512:1024]]
                        h3v = [cm.PSM[:, 1024:1536], cm.PSM[:, 1536:2048]]
                        pkey = "psm"
                        pw = cm.psm_free
                    else:
                        h1v = pst_h1
                        h3v = [cm.PSX[0][:], cm.PSX[1][:]]
                        pkey = "psalt"
                        pw = alt_free + cm.pst_free[0] + cm.pst_free[1] + \
                            cm.psx_free[0] + cm.psx_free[1]
                    lastpair = None
                    for HW, hvv, hk in ((W1T, h1v, "fw1t"),
                                        (W3T, h3v, "fw3t")):
                        for ci in range(NC):
                            for n2 in range(2):
                                lastpair = p.emit(
                                    "pe", lambda t, HW=HW, hv=hvv[n2],
                                    ci=ci, n2=n2, fc=fc: t.matmul(
                                        hv,
                                        HW[:, ci * DF + fc * 128:
                                           ci * DF + (fc + 1) * 128],
                                        XT[:, ci * XTK + n2 * 512:
                                              ci * XTK + (n2 + 1) * 512],
                                        start=(ci == 0),
                                        stop=(ci == NC - 1)),
                                    reads=["xt", hk], writes=[pkey],
                                    wait=[tok_w13, ("ACT", tok_xt)] + pw,
                                    inc=("PE", 1) if (
                                        ci == NC - 1 and n2 == 1 and
                                        hk == "fw3t") else None)
                    tok_s1a = p.emit("act", lambda a, h1v=h1v: a.activation(
                        out=S1[:, 0:512], in_=h1v[0], func=AF.Silu),
                        reads=[pkey, "fs1"], writes=["fs1"],
                        wait=[("PE", lastpair)], inc=("ACT", 1))
                    tok_s1 = p.emit("act", lambda a, h1v=h1v: a.activation(
                        out=S1[:, 512:1024], in_=h1v[1], func=AF.Silu),
                        reads=[pkey, "fs1"], writes=["fs1"],
                        inc=("ACT", 1))
                    p.emit(
                        "dve", lambda v, fc=fc, h3v=h3v: v.tensor_tensor(
                            GATED[:, fc * 1024:fc * 1024 + 512],
                            S1[:, 0:512], h3v[0], ALU.mult),
                        reads=["fs1", pkey, "fgated"], writes=["fgated"],
                        wait=[("ACT", tok_s1)] + state["gated_free"],
                        inc=("DVE", 1))
                    tok_gt = p.emit(
                        "dve", lambda v, fc=fc, h3v=h3v: v.tensor_tensor(
                            GATED[:, fc * 1024 + 512:(fc + 1) * 1024],
                            S1[:, 512:1024], h3v[1], ALU.mult),
                        reads=["fs1", pkey, "fgated"], writes=["fgated"],
                        inc=("DVE", 1))
                    if par == 0:
                        cm.psm_free = [("DVE", tok_gt)]
                    else:
                        alt_free = [("DVE", tok_gt)]
                        state["alt_free"] = alt_free
                        cm.pst_free[0] = [("DVE", tok_gt)]
                        cm.pst_free[1] = [("DVE", tok_gt)]
                        cm.psx_free[0] = [("DVE", tok_gt)]
                        cm.psx_free[1] = [("DVE", tok_gt)]
                tok_gated_all = p.tok("DVE")
                state["w13_free"] = [("PE", p.tok("PE"))]
                if e + 1 < NE:
                    tok_w13_next = load_expert_w13(e + 1)
                w2banks = {
                    0: [cm.PSM[:, 0:512], cm.PSM[:, 512:1024],
                        cm.PSM[:, 1024:1536], cm.PSM[:, 1536:2048]],
                    1: [pst_h1[0], pst_h1[1],
                        cm.PSX[0][:], cm.PSX[1][:]],
                }
                w2free = {0: cm.psm_free,
                          1: state.get("alt_free", []) +
                          cm.pst_free[0] + cm.pst_free[1] +
                          cm.psx_free[0] + cm.psx_free[1]}
                for tbp in range(4):
                    wp = tbp % 2
                    pk = "psm" if wp == 0 else "psalt"
                    last = None
                    for fc in range(NT):
                        for half in range(4):
                            tb = tbp * 2 + half // 2
                            o512 = (half % 2) * 512 + (half // 2) * 1024
                            last = p.emit(
                                "pe", lambda t, fc=fc, tb=tb, o512=o512,
                                bk=w2banks[wp][half]:
                                t.matmul(
                                    bk,
                                    GATED[:, fc * 1024 + tb * 128:
                                          fc * 1024 + (tb + 1) * 128],
                                    W2TF[:, fc * D + (o512 % 1024):
                                         fc * D + (o512 % 1024) + 512],
                                    start=(fc == 0), stop=(fc == NT - 1)),
                                reads=["fgated", "fw2tf"], writes=[pk],
                                wait=[tok_w2,
                                      ("DVE", tok_gated_all)] + w2free[wp],
                                inc=("PE", 1) if (
                                    fc == NT - 1 and half == 3) else None)
                    accs = []
                    for half in range(2):
                        tb = tbp * 2 + half
                        g_col = GT[:].rearrange(
                            "p (t e) -> p t e", e=NE)[:, tb, e:e + 1]
                        for qh2 in range(2):
                            tok_acc = p.emit(
                                "dve", lambda v, tb=tb, half=half,
                                g_col=g_col, qh2=qh2,
                                bk=w2banks[wp][half * 2 + qh2]:
                                v.scalar_tensor_tensor(
                                    out=ACC[:, tb * 1024 + qh2 * 512:
                                            tb * 1024 + qh2 * 512 + 512],
                                    in0=bk,
                                    scalar=g_col,
                                    in1=ACC[:, tb * 1024 + qh2 * 512:
                                            tb * 1024 + qh2 * 512 + 512],
                                    op0=ALU.mult, op1=ALU.add),
                                reads=[pk, "facc", "fgt"], writes=["facc"],
                                wait=[("PE", last), ("DVE", tok_gate),
                                      ("DVE", tok_z)], inc=("DVE", 1))
                            accs.append(tok_acc)
                    w2free[wp] = [("DVE", accs[-1])]
                    if wp == 0:
                        cm.psm_free = w2free[0]
                    else:
                        state["alt_free"] = w2free[1]
                        cm.pst_free[0] = w2free[1]
                        cm.pst_free[1] = w2free[1]
                        cm.psx_free[0] = w2free[1]
                        cm.psx_free[1] = w2free[1]
                state["gated_free"] = [("PE", p.tok("PE"))]
                state["w2t_free"] = [("PE", p.tok("PE"))]
                if e + 1 < NE:
                    tok_w13 = tok_w13_next
                    tok_w2 = load_expert_w2(e + 1)
            for tb in range(8):
                ti = th * 8 + tb
                par = ti % 2
                RT = cm.RT[par]
                tok_ld = p.emit(
                    "sp", lambda s, RT=RT, ti=ti, rin_seq=rin_seq:
                    s.dma_start(
                        out=RT[:], in_=rin_seq[ti * 128:(ti + 1) * 128, :]),
                    dma=True, wait=cm.rt_free[par] + cm.rt_store[par],
                    inc=(f"RL{par}", 16))
                tok_add = p.emit(
                    "dve", lambda v, RT=RT, tb=tb: v.tensor_tensor(
                        RT[:], RT[:], ACC[:, tb * 1024:(tb + 1) * 1024],
                        ALU.add),
                    reads=[f"RT{par}", "facc"], writes=[f"RT{par}"],
                    wait=[(f"RL{par}", tok_ld), ("DVE", p.tok("DVE"))],
                    inc=("DVE", 1))
                cm.rt_free[par] = [("DVE", tok_add)]
                tok_st = p.emit(
                    "sp", lambda s, RT=RT, ti=ti, rout_seq=rout_seq:
                    s.dma_start(
                        out=rout_seq[ti * 128:(ti + 1) * 128, :], in_=RT[:]),
                    dma=True, wait=[("DVE", tok_add)], inc=(f"ST{par}", 16))
                cm.rt_store[par] = [(f"ST{par}", tok_st)]
            state["acc_free"] = [("DVE", p.tok("DVE"))]
    p.barrier("sp", [("ST0", p.tok("ST0")), ("ST1", p.tok("ST1"))])


def build_l0_ffn():
    p = Prog()
    nc = p.nc
    rin = nc.dram_tensor("rin", [NSEQ * T, D], F32, kind="ExternalInput").ap()
    gate_w = nc.dram_tensor("gate_w", [8, D], F32, kind="ExternalInput").ap()
    w1 = nc.dram_tensor("w1", [8, 2048, D], F32, kind="ExternalInput").ap()
    w2 = nc.dram_tensor("w2", [8, D, 2048], F32, kind="ExternalInput").ap()
    w3 = nc.dram_tensor("w3", [8, 2048, D], F32, kind="ExternalInput").ap()
    cstf = nc.dram_tensor("cstf", [128 * 128], F32, kind="ExternalInput").ap()
    rout = nc.dram_tensor("rout", [NSEQ * T, D], F32,
                          kind="ExternalOutput").ap()
    cm = Common(p, xt_tokens=1024)
    cm.emit_setup(cstf)
    emit_l0_ffn(p, cm, rin, rout, gate_w,
                w1.rearrange("e f c -> (e f) c"),
                w2.rearrange("e d f -> (e d) f"),
                w3.rearrange("e f c -> (e f) c"))
    p.finish()
    return p



def emit_l1_ffn(p: Prog, cm: Common, rin, rout, wg, bg, wu, bu, wd, bd,
                wqp, keys, down_embed, up_embed):
    def _setph(c):
        p.pe_drain_on = (c == "C")
    XT = cm.XT
    _setph("A")
    DCD, RET, NEXP, NKEYS, H, KPH = 4096, 128, 4096, 64, 4, 8
    NFC = DCD // 128
    wgb, _ = cast_weight(p, "pwgb", wg, [D, DCD])
    wub, _ = cast_weight(p, "pwub", wu, [D, DCD])
    wdb, _ = cast_weight(p, "pwdb", wd, [DCD, D])
    wqb, _ = cast_weight(p, "pwqb", wqp, [D, 512])
    keyb, _ = cast_weight(p, "pkeyb", keys, [H * NKEYS, 2 * 64])
    deb, _ = cast_weight(p, "pdeb", down_embed, [NEXP, D])
    ueb, _ = cast_weight(p, "pueb", up_embed, [NEXP, D])
    tok_cast = p.tok("CAST")

    BGF = p.abuf("pBGF", [128, NFC], F32)
    BUF = p.abuf("pBUF", [128, NFC], F32)
    BDR = p.abuf("pBDR", [1, D], BF16)
    p.emit("sp", lambda s: s.dma_start(
        out=BGF[:], in_=bg.rearrange("(a p) -> p a", p=128),
        allow_slow_non_contiguous=True), dma=True, inc=("WO", 16))
    p.emit("sp", lambda s: s.dma_start(
        out=BUF[:], in_=bu.rearrange("(a p) -> p a", p=128),
        allow_slow_non_contiguous=True), dma=True, inc=("WO", 16))
    p.emit("gp", lambda g: g.dma_start(
        out=BDR[:], in_=bd.rearrange("(a b) -> a b", a=1)),
        dma=True, inc=("CAST", 16))
    WQP = p.abuf("pWQP", [128, NC * 512], BF16)
    p.emit("sp", lambda s: s.dma_start(
        out=WQP[:].rearrange("p (c n) -> p c n", c=NC),
        in_=wqb.ap().rearrange("(c p) n -> p c n", p=128)),
        dma=True, wait=[("CAST", tok_cast)], inc=("WO", 16))
    KEY0 = p.abuf("pKEY0", [128, 2 * 128], BF16)
    KEYST = p.abuf("pKEYST", [128, 8 * 64], BF16)
    p.emit("sp", lambda s: s.dma_start(
        out=KEY0[:].rearrange("p (a x) -> p a x", a=2),
        in_=keyb.ap().rearrange("(a p) x -> p a x", p=128)),
        dma=True, wait=[("CAST", tok_cast)], inc=("WO", 16))
    tok_keys = p.tok("WO")
    tok_bias = tok_keys
    tok_wqp = tok_keys
    for h in range(H):
        for pp in range(2):
            src_pb = (h % 2) * 64
            grp = pp * 4 + h
            for dst_pb in (0, 64):
                tok_tp = p.emit(
                    "pe", lambda t, h=h, pp=pp, src_pb=src_pb,
                    dst_pb=dst_pb: t.transpose(
                        cm.PST[0][dst_pb:dst_pb + 64, 0:64],
                        KEY0[src_pb:src_pb + 64,
                             (h // 2) * 128 + pp * 64:
                             (h // 2) * 128 + (pp + 1) * 64],
                        cm.IDENT[src_pb:src_pb + 64, src_pb:src_pb + 64]),
                    reads=["pkey0", "ident"], writes=["pst0"],
                    wait=[("WO", tok_keys), ("GPC", cm.tok_setup_gp)] +
                         cm.pst_free[0],
                    inc=("PE", 1))
                tok_ev = p.emit(
                    "act", lambda a, grp=grp, dst_pb=dst_pb: a.copy(
                        KEYST[dst_pb:dst_pb + 64, grp * 64:(grp + 1) * 64],
                        cm.PST[0][dst_pb:dst_pb + 64, 0:64]),
                    reads=["pst0"], writes=["pkeyst"],
                    wait=[("PE", tok_tp)], inc=("ACT", 1))
                cm.pst_free[0] = [("ACT", tok_ev)]
    tok_keyst = p.tok("ACT")

    WGF = [p.abuf(f"pWGF{i}", [128, NC * 128], BF16) for i in range(2)]
    WUF = [p.abuf(f"pWUF{i}", [128, NC * 128], BF16) for i in range(2)]
    WDF = [p.abuf(f"pWDF{i}", [128, D], BF16) for i in range(4)]
    XLNS = p.abuf("pXLNS", [128, 8 * 1024], BF16)
    GATED = p.abuf("pGATED", [128, NFC * 1024], BF16)
    S1 = p.abuf("pS1", [128, 1024], BF16)
    QP = p.abuf("pQP", [128, 4 * 1024], BF16)
    SIMS = p.abuf("pSIMS", [128, 512], F32)
    SC1 = p.abuf("pSC1", [128, 64], F32)
    IX1 = p.abuf("pIX1", [128, 64], F32)
    IXU = p.abuf("pIXU", [128, 8], U32)
    CMB = p.abuf("pCMB", [128, 4 * 64], F32)
    SC2 = p.abuf("pSC2", [128, 32], F32)
    PKU = p.abuf("pPKU", [128, 8], U32)
    PKF = p.abuf("pPKF", [128, 32], F32)
    AB2 = p.abuf("pAB2", [128, 64], F32)
    SEL = p.abuf("pSEL", [128, 64], F32)
    MSK = p.abuf("pMSK", [128, 64], F32)
    TMP = p.abuf("pTMP", [128, 64], F32)
    EID = p.abuf("pEID", [128, 32], I32)
    ABI = p.abuf("pABI", [128, 32], I32)
    EIDF = p.abuf("pEIDF", [128, 32], F32)
    PP = p.abuf("pPP", [128, 48], F32)
    WCOL = p.abuf("pWCOL", [128, 32], F32)
    WPR = p.abuf("pWPR", [128, 32], F32)
    DEG = [p.abuf(f"pDEG{i}", [128, 1024], BF16) for i in range(8)]
    UEG = [p.abuf(f"pUEG{i}", [128, 1024], BF16) for i in range(8)]
    JNK = p.abuf("pJNK", [128, 1024], BF16)
    EST = p.abuf("pEST", [128, 1024], F32)
    DELTA = [p.abuf(f"pDELTA{i}", [128, 1024], F32) for i in range(2)]

    state = {"wgf_free": [[], []], "wuf_free": [[], []],
             "wdf_free": [[], [], [], []],
             "ueg_free": [[] for _ in range(8)],
             "deg_free": [[] for _ in range(8)], "est_free": [],
             "eid_free": [],
             "delta_free": [[], []]}

    def load_w_fc(W, wb, fc, key, freelist):
        buf = fc % 2
        tok = p.emit(
            "sp", lambda s, W=W, wb=wb, fc=fc, buf=buf: s.dma_start(
                out=W[buf][:].rearrange("p (c n) -> p c n", c=NC),
                in_=wb.ap().rearrange("(c p) n -> p c n", p=128)[
                    :, :, fc * 128:(fc + 1) * 128],
                allow_slow_non_contiguous=True),
            dma=True, wait=([("CAST", tok_cast)] + freelist[buf]),
            inc=(f"WGU{buf}", 16))
        return ("WGU%d" % buf, p.tok("WGU%d" % buf))

    def load_wd_fc(fc):
        buf = fc % 4
        tok = p.emit(
            "sp", lambda s, fc=fc, buf=buf: s.dma_start(
                out=WDF[buf][:],
                in_=wdb.ap()[fc * 128:(fc + 1) * 128, :]),
            dma=True, wait=([("CAST", tok_cast)] +
                            state["wdf_free"][buf]),
            inc=(f"WD{buf}", 16))
        return ("WD%d" % buf, p.tok("WD%d" % buf))

    for seq in range(NSEQ):
        rin_seq = rin[seq * T:(seq + 1) * T, :]
        rout_seq = rout[seq * T:(seq + 1) * T, :]
        for th in range(2):
            XTK = 1024
            xlns_cb_tok = []

            def xlns_cb(ti, XLN, tok_xln, th=th):
                lt = ti - th * 8
                p.emit("dve", lambda v, XLN=XLN, lt=lt: v.tensor_copy(
                    XLNS[:, lt * 1024:(lt + 1) * 1024], XLN[:]),
                    reads=[f"XLN{ti % 2}", "pxlns"], writes=["pxlns"],
                    inc=("DVE", 1))

            cm.emit_ln_seq(rin_seq, extra_tile_cb=xlns_cb, tile0=th * 8,
                           ntiles=8, xt_tokens=XTK)
            tok_xt = p.tok("ACT")
            tok_xlns = p.tok("DVE")

            _setph("Q")
            for qb in range(4):
                last = None
                for n2 in range(2):
                    for ci in range(NC):
                        last = p.emit(
                            "pe", lambda t, qb=qb, ci=ci, n2=n2: t.matmul(
                                cm.PSM[:, n2 * 512:(n2 + 1) * 512],
                                WQP[:, ci * 512 + qb * 128:
                                    ci * 512 + (qb + 1) * 128],
                                XT[:, ci * XTK + n2 * 512:
                                      ci * XTK + (n2 + 1) * 512],
                                start=(ci == 0), stop=(ci == NC - 1)),
                            reads=["xt", "pwqp"], writes=["psm"],
                            wait=[("ACT", tok_xt), ("WO", tok_wqp)] +
                                 cm.psm_free,
                            inc=("PE", 1) if (ci == NC - 1 and n2 == 1)
                            else None)
                tok_e = p.emit("act", lambda a, qb=qb: a.copy(
                    QP[:, qb * 1024:(qb + 1) * 1024], cm.PSM[:, 0:1024]),
                    reads=["psm"], writes=["pqp"], wait=[("PE", last)],
                    inc=("ACT", 1))
                cm.psm_free = [("ACT", tok_e)]
            tok_qp = p.tok("ACT")

            _setph("B")
            tok_wg = load_w_fc(WGF, wgb, 0, "pwgf", state["wgf_free"])
            tok_wu = load_w_fc(WUF, wub, 0, "pwuf", state["wuf_free"])
            wtoks = {0: (tok_wg, tok_wu)}
            for fc in range(NFC):
                buf = fc % 2
                if fc + 1 < NFC:
                    wtoks[fc + 1] = (
                        load_w_fc(WGF, wgb, fc + 1, "pwgf",
                                  state["wgf_free"]),
                        load_w_fc(WUF, wub, fc + 1, "pwuf",
                                  state["wuf_free"]))
                tok_wg, tok_wu = wtoks.pop(fc)
                lastg = None
                for ci in range(NC):
                    for n2 in range(2):
                        lastg = p.emit(
                            "pe", lambda t, ci=ci, n2=n2, buf=buf: t.matmul(
                                cm.PSM[:, n2 * 512:(n2 + 1) * 512],
                                WGF[buf][:, ci * 128:(ci + 1) * 128],
                                XT[:, ci * XTK + n2 * 512:
                                      ci * XTK + (n2 + 1) * 512],
                                start=(ci == 0), stop=(ci == NC - 1)),
                            reads=["xt", f"pwgf{buf}"], writes=["psm"],
                            wait=[tok_wu, ("ACT", tok_xt)] +
                                 cm.psm_free,
                            inc=("PE", 1) if (ci == NC - 1 and n2 == 1)
                            else None)
                lastu = None
                for ci in range(NC):
                    for n2 in range(2):
                        lastu = p.emit(
                            "pe", lambda t, ci=ci, n2=n2, buf=buf: t.matmul(
                                cm.PSM[:, 1024 + n2 * 512:
                                       1024 + (n2 + 1) * 512],
                                WUF[buf][:, ci * 128:(ci + 1) * 128],
                                XT[:, ci * XTK + n2 * 512:
                                      ci * XTK + (n2 + 1) * 512],
                                start=(ci == 0), stop=(ci == NC - 1)),
                            reads=["xt", f"pwuf{buf}"], writes=["psm"],
                            wait=[tok_wu],
                            inc=("PE", 1) if (ci == NC - 1 and n2 == 1)
                            else None)
                state["wgf_free"][buf] = [("PE", lastg)]
                state["wuf_free"][buf] = [("PE", lastu)]
                tok_s1 = p.emit("act", lambda a, fc=fc: a.activation(
                    out=S1[:], in_=cm.PSM[:, 0:1024], func=AF.Silu,
                    bias=BGF[:, fc:fc + 1]),
                    reads=["psm", "ps1", "pbgf"], writes=["ps1"],
                    wait=[("PE", lastg), ("WO", tok_bias)], inc=("ACT", 1))
                tok_gt = p.emit("dve", lambda v, fc=fc: v.scalar_tensor_tensor(
                    out=GATED[:, fc * 1024:(fc + 1) * 1024],
                    in0=cm.PSM[:, 1024:2048], scalar=BUF[:, fc:fc + 1],
                    in1=S1[:], op0=ALU.add, op1=ALU.mult),
                    reads=["psm", "ps1", "pbuf", "pgated"],
                    writes=["pgated"],
                    wait=[("ACT", tok_s1), ("PE", lastu)], inc=("DVE", 1))
                cm.psm_free = [("DVE", tok_gt)]
            tok_gated = p.tok("DVE")

            tok_wd = load_wd_fc(0)
            wdtoks = {0: tok_wd}
            for lt in range(8):
                ti = th * 8 + lt
                _setph("C")
                last = None
                for pp in range(2):
                    for h in range(H):
                        grp = pp * 4 + h
                        base = ((4 * pp + h) % 2) * 64
                        qb = (4 * pp + h) // 2
                        last = p.emit(
                            "pe", lambda t, grp=grp, base=base, qb=qb,
                            lt=lt: t.matmul(
                                cm.PSM[:, 1536 + grp * 64:
                                       1536 + (grp + 1) * 64],
                                QP[base:base + 64,
                                   qb * 1024 + lt * 128:
                                   qb * 1024 + (lt + 1) * 128],
                                KEYST[base:base + 64,
                                      grp * 64:(grp + 1) * 64],
                                start=(grp == 0), stop=(grp == 7)),
                            reads=["pqp", "pkeyst"], writes=["psm"],
                            wait=[("ACT", tok_qp), ("ACT", tok_keyst)] +
                                 (cm.psm_free if grp == 0 else []),
                            inc=("PE", 1) if grp == 7 else None)
                tok_sims = p.emit("dve", lambda v: v.tensor_copy(
                    SIMS[:], cm.PSM[:, 1536:2048]),
                    reads=["psm", "psims"], writes=["psims"],
                    wait=[("PE", last)], inc=("DVE", 1))
                cm.psm_free = [("DVE", tok_sims)]
                for grp in range(8):
                    p.emit("dve", lambda v, grp=grp: v.max(
                        SC1[:, grp * 8:(grp + 1) * 8],
                        SIMS[:, grp * 64:(grp + 1) * 64]),
                        reads=["psims", "psc1"], writes=["psc1"])
                    p.emit("dve", lambda v, grp=grp: v.max_index(
                        IXU[:], SC1[:, grp * 8:(grp + 1) * 8],
                        SIMS[:, grp * 64:(grp + 1) * 64]),
                        reads=["psims", "psc1", "pixu"], writes=["pixu"])
                    p.emit("dve", lambda v, grp=grp: v.tensor_copy(
                        IX1[:, grp * 8:(grp + 1) * 8], IXU[:]),
                        reads=["pixu", "pix1"], writes=["pix1"])
                for h in range(H):
                    sx = SC1[:, h * 8:(h + 1) * 8]
                    sy = SC1[:, (4 + h) * 8:(5 + h) * 8]
                    sx_b = bass.AP(sx.tensor, sx.offset,
                                   [sx.ap[0], [sx.ap[1][0], 8], [0, 8]])
                    sy_b = bass.AP(sy.tensor, sy.offset,
                                   [sy.ap[0], [0, 8], [sy.ap[1][0], 8]])
                    p.emit("dve", lambda v, h=h, sx_b=sx_b, sy_b=sy_b:
                           v.tensor_tensor(
                        CMB[:, h * 64:(h + 1) * 64].rearrange(
                            "p (a b) -> p a b", a=8),
                        sx_b, sy_b, ALU.add),
                        reads=["psc1", "pcmb"], writes=["pcmb"])
                for h in range(H):
                    p.emit("dve", lambda v, h=h: v.max(
                        SC2[:, h * 8:(h + 1) * 8],
                        CMB[:, h * 64:(h + 1) * 64]),
                        reads=["pcmb", "psc2"], writes=["psc2"])
                    p.emit("dve", lambda v, h=h: v.max_index(
                        PKU[:], SC2[:, h * 8:(h + 1) * 8],
                        CMB[:, h * 64:(h + 1) * 64]),
                        reads=["pcmb", "psc2", "ppku"], writes=["ppku"])
                    p.emit("dve", lambda v, h=h: v.tensor_copy(
                        PKF[:, h * 8:(h + 1) * 8], PKU[:]),
                        reads=["ppku", "ppkf"], writes=["ppkf"])
                p.emit("dve", lambda v: v.tensor_scalar(
                    out=ABI[:], in0=PKF[:], scalar1=0.125, scalar2=-0.4375,
                    op0=ALU.mult, op1=ALU.add),
                    reads=["ppkf", "pabi"], writes=["pabi"])
                p.emit("dve", lambda v: v.tensor_copy(AB2[:, 0:32], ABI[:]),
                    reads=["pabi", "pab2"], writes=["pab2"])
                p.emit("dve", lambda v: v.scalar_tensor_tensor(
                    out=AB2[:, 32:64], in0=AB2[:, 0:32], scalar=-8.0,
                    in1=PKF[:], op0=ALU.mult, op1=ALU.add),
                    reads=["ppkf", "pab2"], writes=["pab2"])
                p.emit("dve", lambda v: v.memset(SEL[:], 0.0),
                       reads=["psel"], writes=["psel"])
                ix_b = bass.AP(IX1[:].tensor, IX1[:].offset,
                               [IX1[:].ap[0], [8, 8], [0, 8]])
                for j in range(8):
                    ix_j = bass.AP(IX1[:].tensor, IX1[:].offset + j,
                                   [IX1[:].ap[0], [8, 8], [0, 8]])
                    p.emit("dve", lambda v, j=j: v.tensor_single_scalar(
                        MSK[:], AB2[:], float(j), ALU.is_equal),
                        reads=["pab2", "pmsk"], writes=["pmsk"])
                    p.emit("dve", lambda v, ix_j=ix_j: v.tensor_tensor(
                        TMP[:].rearrange("p (a b) -> p a b", a=8),
                        MSK[:].rearrange("p (a b) -> p a b", a=8),
                        ix_j, ALU.mult),
                        reads=["pmsk", "pix1", "ptmp"], writes=["ptmp"])
                    p.emit("dve", lambda v: v.tensor_tensor(
                        SEL[:], SEL[:], TMP[:], ALU.add),
                        reads=["ptmp", "psel"], writes=["psel"])
                p.emit("dve", lambda v: v.scalar_tensor_tensor(
                    out=EIDF[:], in0=SEL[:, 0:32], scalar=64.0,
                    in1=SEL[:, 32:64], op0=ALU.mult, op1=ALU.add),
                    reads=["psel", "peidf"], writes=["peidf"])
                tok_eid = p.emit("dve", lambda v: v.tensor_copy(
                    EID[:], EIDF[:]), reads=["peidf", "peid"],
                    writes=["peid"], wait=state["eid_free"],
                    inc=("DVE", 1))
                tok_ex = p.emit("act", lambda a: a.activation(
                    out=PP[:, 0:32], in_=SC2[:], func=AF.Exp),
                    reads=["psc2", "ppp"], writes=["ppp"],
                    wait=[("DVE", tok_eid)], inc=("ACT", 1))
                for h in range(H):
                    p.emit("dve", lambda v, h=h: v.reduce_sum(
                        PP[:, 32 + h:33 + h], PP[:, h * 8:(h + 1) * 8],
                        axis=AX.X),
                        reads=["ppp"], writes=["ppp2"],
                        wait=[("ACT", tok_ex)])
                p.emit("dve", lambda v: v.reciprocal(
                    PP[:, 40:44], PP[:, 32:36]), reads=["ppp2"],
                    writes=["ppp2"])
                for h in range(H):
                    p.emit("dve", lambda v, h=h: v.tensor_scalar_mul(
                        PP[:, h * 8:(h + 1) * 8], PP[:, h * 8:(h + 1) * 8],
                        PP[:, 40 + h:41 + h]),
                        reads=["ppp", "ppp2"], writes=["ppp"])
                for sl in range(32):
                    buf = sl % 8
                    tok_g = p.emit(
                        "gp", lambda g, sl=sl, buf=buf: g.indirect_dma_start(
                            out=DEG[buf][:], out_offset=None,
                            in_=deb.ap()[:, :],
                            in_offset=IndirectOffsetOnAxis(
                                EID[:, sl:sl + 1], 0)),
                        dma=True,
                        wait=[("DVE", tok_eid)] + state["deg_free"][buf],
                        inc=(f"DE{buf}", 16))
                    tok_w = p.emit(
                        "dve", lambda v, sl=sl, lt=lt, buf=buf:
                        v.scalar_tensor_tensor(
                            out=JNK[:], in0=XLNS[:, lt * 1024:
                                                 (lt + 1) * 1024],
                            scalar=1.0, in1=DEG[buf][:], op0=ALU.mult,
                            op1=ALU.mult,
                            accum_out=WCOL[:, sl:sl + 1]),
                        reads=["pxlns", f"pdeg{buf}", "pjnk", "pwcol"],
                        writes=["pjnk", "pwcol"],
                        wait=[(f"DE{buf}", tok_g), ("DVE", tok_xlns)],
                        inc=("DVE", 1))
                    state["deg_free"][buf] = [("DVE", tok_w)]
                tok_wdots = p.tok("DVE")
                tok_sw = p.emit("act", lambda a: a.activation(
                    out=WPR[:], in_=WCOL[:], func=AF.Silu),
                    reads=["pwcol", "pwpr"], writes=["pwpr"],
                    wait=[("DVE", tok_wdots)], inc=("ACT", 1))
                tok_wp = p.emit("dve", lambda v: v.tensor_tensor(
                    WPR[:], WPR[:], PP[:, 0:32], ALU.mult),
                    reads=["pwpr", "ppp"], writes=["pwpr"],
                    wait=[("ACT", tok_sw)], inc=("DVE", 1))
                p.emit("dve", lambda v: v.memset(EST[:], 0.0),
                       reads=["pest"], writes=["pest"],
                       wait=state["est_free"])
                for sl in range(32):
                    buf = sl % 8
                    tok_g = p.emit(
                        "gp", lambda g, sl=sl, buf=buf:
                        g.indirect_dma_start(
                            out=UEG[buf][:], out_offset=None,
                            in_=ueb.ap()[:, :],
                            in_offset=IndirectOffsetOnAxis(
                                EID[:, sl:sl + 1], 0)),
                        dma=True,
                        wait=[("DVE", tok_eid)] + state["ueg_free"][buf],
                        inc=(f"UE{buf}", 16))
                    tok_es = p.emit(
                        "dve", lambda v, sl=sl, buf=buf:
                        v.scalar_tensor_tensor(
                            out=EST[:], in0=UEG[buf][:],
                            scalar=WPR[:, sl:sl + 1], in1=EST[:],
                            op0=ALU.mult, op1=ALU.add),
                        reads=[f"pueg{buf}", "pwpr", "pest"],
                        writes=["pest"],
                        wait=[(f"UE{buf}", tok_g), ("DVE", tok_wp)],
                        inc=("DVE", 1))
                    state["ueg_free"][buf] = [("DVE", tok_es)]
                state["eid_free"] = [(f"UE{b}", p.tok(f"UE{b}"))
                                     for b in range(8)]
                tok_est = p.tok("DVE")
                _setph("D")
                last = None
                for fc in range(NFC):
                    for pf in range(4 if fc == 0 else 1):
                        nxt = fc + pf + (0 if fc == 0 else 3)
                        if 0 < nxt < NFC and nxt not in wdtoks:
                            wdtoks[nxt] = load_wd_fc(nxt)
                    tok_wdc = wdtoks.pop(fc)
                    buf = fc % 4
                    for n2 in range(2):
                        last = p.emit(
                            "pe", lambda t, fc=fc, n2=n2, buf=buf, lt=lt:
                            t.matmul(
                                cm.PSM[:, n2 * 512:(n2 + 1) * 512],
                                GATED[:, fc * 1024 + lt * 128:
                                      fc * 1024 + (lt + 1) * 128],
                                WDF[buf][:, n2 * 512:(n2 + 1) * 512],
                                start=(fc == 0), stop=False),
                            reads=["pgated", f"pwdf{buf}"], writes=["psm"],
                            wait=[tok_wdc, ("DVE", tok_gated)] +
                                 cm.psm_free,
                            inc=("PE", 1) if n2 == 1 else None)
                    state["wdf_free"][buf] = [("PE", p.tok("PE"))]
                wdtoks = {0: load_wd_fc(0)} if lt + 1 < 8 else {}
                state["wdf_free"][0] = [("PE", p.tok("PE"))]
                for n2 in range(2):
                    last = p.emit(
                        "pe", lambda t, n2=n2, lt=lt: t.matmul(
                            cm.PSM[:, n2 * 512:(n2 + 1) * 512],
                            cm.ONES[0:1, lt * 128:(lt + 1) * 128],
                            BDR[0:1, n2 * 512:(n2 + 1) * 512],
                            start=False, stop=(n2 == 1)),
                        reads=["pbdr", "ones"], writes=["psm"],
                        wait=[("CAST", tok_cast), ("GPC",
                                                   cm.tok_setup_gp)],
                        inc=("PE", 1) if n2 == 1 else None)
                par = ti % 2
                RT = cm.RT[par]
                DL = DELTA[par]
                tok_dl = p.emit("act", lambda a, DL=DL: a.copy(
                    DL[:], cm.PSM[:, 0:1024]),
                    reads=["psm", f"pdelta{par}"], writes=[f"pdelta{par}"],
                    wait=[("PE", last)] + state["delta_free"][par],
                    inc=("ACT", 1))
                cm.psm_free = [("ACT", tok_dl)]
                tok_ld = p.emit(
                    "sp", lambda s, RT=RT, ti=ti, rin_seq=rin_seq:
                    s.dma_start(
                        out=RT[:], in_=rin_seq[ti * 128:(ti + 1) * 128, :]),
                    dma=True, wait=cm.rt_free[par] + cm.rt_store[par],
                    inc=(f"RL{par}", 16))
                tok_a1 = p.emit("dve", lambda v, RT=RT, DL=DL: v.tensor_tensor(
                    RT[:], RT[:], DL[:], ALU.add),
                    reads=[f"RT{par}", f"pdelta{par}"], writes=[f"RT{par}"],
                    wait=[("ACT", tok_dl), (f"RL{par}", tok_ld)],
                    inc=("DVE", 1))
                state["delta_free"][par] = [("DVE", tok_a1)]
                tok_a2 = p.emit("dve", lambda v, RT=RT: v.tensor_tensor(
                    RT[:], RT[:], EST[:], ALU.add),
                    reads=[f"RT{par}", "pest"], writes=[f"RT{par}"],
                    wait=[("DVE", tok_est)], inc=("DVE", 1))
                state["est_free"] = [("DVE", tok_a2)]
                cm.rt_free[par] = [("DVE", tok_a2)]
                tok_st = p.emit(
                    "sp", lambda s, RT=RT, ti=ti, rout_seq=rout_seq:
                    s.dma_start(
                        out=rout_seq[ti * 128:(ti + 1) * 128, :], in_=RT[:]),
                    dma=True, wait=[("DVE", tok_a2)], inc=(f"ST{par}", 16))
                cm.rt_store[par] = [(f"ST{par}", tok_st)]
    p.barrier("sp", [("ST0", p.tok("ST0")), ("ST1", p.tok("ST1"))])


def build_l1_ffn():
    p = Prog()
    nc = p.nc
    rin = nc.dram_tensor("rin", [NSEQ * T, D], F32, kind="ExternalInput").ap()
    wg = nc.dram_tensor("wg", [D, 4096], F32, kind="ExternalInput").ap()
    bg = nc.dram_tensor("bg", [4096], F32, kind="ExternalInput").ap()
    wu = nc.dram_tensor("wu", [D, 4096], F32, kind="ExternalInput").ap()
    bu = nc.dram_tensor("bu", [4096], F32, kind="ExternalInput").ap()
    wd = nc.dram_tensor("wd", [4096, D], F32, kind="ExternalInput").ap()
    bd = nc.dram_tensor("bd", [D], F32, kind="ExternalInput").ap()
    wqp = nc.dram_tensor("wq", [D, 512], F32, kind="ExternalInput").ap()
    keys = nc.dram_tensor("keys", [4, 64, 2, 64], F32,
                          kind="ExternalInput").ap()
    de = nc.dram_tensor("down_embed", [4096, D], F32,
                        kind="ExternalInput").ap()
    ue = nc.dram_tensor("up_embed", [4096, D], F32,
                        kind="ExternalInput").ap()
    cstf = nc.dram_tensor("cstf", [128 * 128], F32, kind="ExternalInput").ap()
    rout = nc.dram_tensor("rout", [NSEQ * T, D], F32,
                          kind="ExternalOutput").ap()
    cm = Common(p, xt_tokens=1024)
    cm.emit_setup(cstf)
    emit_l1_ffn(p, cm, rin, rout, wg, bg,
                wu, bu, wd, bd, wqp,
                keys.rearrange("h k t n -> (h k) (t n)"), de, ue)
    p.finish()
    return p



def emit_l1_attn_a(p: Prog, cm: Common, rin, rout, wq, wk, wv):
    XT = cm.XT
    DH = 64
    wqb, _ = cast_weight(p, "aawqb", wq, [DH, DH])
    wkb, _ = cast_weight(p, "aawkb", wk, [DH, DH])
    wvb, _ = cast_weight(p, "aawvb", wv, [DH, DH])
    tok_cast = p.tok("CAST")

    scl = p.dram("aa_scl", [NSEQ, 16 * T], F32)
    lmax = p.dram("aa_lmax", [128, 256], F32)
    gmax = p.dram("aa_gmax", [128, 256], F32)
    lsum = p.dram("aa_lsum", [128, 256], F32)
    gsum = p.dram("aa_gsum", [128, 256], F32)
    prb = p.dram("aa_prb", [NSEQ, 16 * T], F32)

    WS = p.abuf("aaWS", [64, 3 * DH], BF16)
    WT = [p.abuf(f"aaWT{i}", [128, 3 * DH], BF16) for i in range(2)]
    QA = p.abuf("aaQA", [64, T], F32)
    PROD = p.abuf("aaPROD", [64, T], BF16)
    ONEC = p.abuf("aaONEC", [64, 1], BF16)
    SR = p.abuf("aaSR", [1, T], F32)
    VS = p.abuf("aaVS", [64, 2], F32)
    VSB = p.abuf("aaVSB", [64, 1], BF16)
    VSR = p.abuf("aaVSR", [1, 16 * DH * NSEQ], BF16)
    PRB = p.abuf("aaPRB", [1, 16 * T], BF16)
    SA = p.abuf("aaSA", [128, 256], F32)
    SB = p.abuf("aaSB", [128, 256 * 5], F32)

    p.emit("sp", lambda s: s.dma_start(out=WS[:, 0:64], in_=wqb.ap()),
           dma=True, wait=[("CAST", tok_cast)], inc=("WO", 16))
    p.emit("sp", lambda s: s.dma_start(out=WS[:, 64:128], in_=wkb.ap()),
           dma=True, wait=[("CAST", tok_cast)], inc=("WO", 16))
    p.emit("sp", lambda s: s.dma_start(out=WS[:, 128:192], in_=wvb.ap()),
           dma=True, wait=[("CAST", tok_cast)], inc=("WO", 16))
    tok_ws = p.tok("WO")
    for i in range(2):
        p.emit("dve", lambda v, i=i: v.memset(WT[i][:], 0.0),
               writes=[f"aawt{i}"], inc=("DVE", 1))
    p.emit("gp", lambda g: g.memset(ONEC[:], 1.0), writes=["aaonec"],
           inc=("GPC", 1))
    tok_one = p.tok("GPC")
    tok_wz = p.tok("DVE")
    for m in range(3):
        for half in range(2):
            pb = half * 64
            tok_tp = p.emit("pe", lambda t, m=m, pb=pb: t.transpose(
                cm.PST[0][pb:pb + 64, 0:64],
                WS[0:64, m * 64:(m + 1) * 64], cm.IDENT[0:64, 0:64]),
                reads=["aaws", "ident"], writes=["pst0"],
                wait=[("WO", tok_ws), ("GPC", cm.tok_setup_gp)] +
                     cm.pst_free[0],
                inc=("PE", 1))
            tok_ev = p.emit("act", lambda a, m=m, pb=pb, half=half: a.copy(
                WT[half][pb:pb + 64, m * 64:(m + 1) * 64],
                cm.PST[0][pb:pb + 64, 0:64]),
                reads=["pst0"], writes=[f"aawt{half}"],
                wait=[("PE", tok_tp), ("DVE", tok_wz)], inc=("ACT", 1))
            cm.pst_free[0] = [("ACT", tok_ev)]
    tok_wt = p.tok("ACT")

    sc_store = []
    sr_free = []
    for c in range(NSEQ):
        rin_seq = rin[c * T:(c + 1) * T, :]
        cm.emit_ln_seq(rin_seq)
        tok_xt = p.tok("ACT")
        for h in range(16):
            par = h % 2
            chunk = h // 2
            last = None
            for n4 in range(4):
                last = p.emit("pe", lambda t, n4=n4, par=par, chunk=chunk:
                              t.matmul(
                    cm.PSM[0:64, n4 * 512:(n4 + 1) * 512],
                    WT[par][:, 0:64],
                    XT[:, chunk * T + n4 * 512:chunk * T + (n4 + 1) * 512],
                    start=True, stop=True),
                    reads=["xt", f"aawt{par}"], writes=["psm"],
                    wait=[("ACT", tok_wt), ("ACT", tok_xt)] + cm.psm_free,
                    inc=("PE", 1) if n4 == 3 else None)
            tok_q = p.emit("dve", lambda v: v.tensor_copy(
                QA[:], cm.PSM[0:64, 0:T]), reads=["psm", "aaqa"],
                writes=["aaqa"], wait=[("PE", last)], inc=("DVE", 1))
            cm.psm_free = [("DVE", tok_q)]
            last = None
            for n4 in range(4):
                last = p.emit("pe", lambda t, n4=n4, par=par, chunk=chunk:
                              t.matmul(
                    cm.PSM[0:64, n4 * 512:(n4 + 1) * 512],
                    WT[par][:, 64:128],
                    XT[:, chunk * T + n4 * 512:chunk * T + (n4 + 1) * 512],
                    start=True, stop=True),
                    reads=["xt", f"aawt{par}"], writes=["psm"],
                    wait=cm.psm_free, inc=("PE", 1) if n4 == 3 else None)
            tok_pr = p.emit("dve", lambda v: v.tensor_tensor(
                PROD[:], QA[:], cm.PSM[0:64, 0:T], ALU.mult),
                reads=["psm", "aaqa", "aaprod"], writes=["aaprod"],
                wait=[("PE", last)], inc=("DVE", 1))
            cm.psm_free = [("DVE", tok_pr)]
            last = None
            for n4 in range(4):
                last = p.emit("pe", lambda t, n4=n4: t.matmul(
                    cm.PSM[0:1, n4 * 512:(n4 + 1) * 512],
                    ONEC[:], PROD[:, n4 * 512:(n4 + 1) * 512],
                    start=True, stop=True),
                    reads=["aaprod", "aaonec"], writes=["psm"],
                    wait=[("GPC", tok_one), ("DVE", tok_pr)] + cm.psm_free,
                    inc=("PE", 1) if n4 == 3 else None)
            tok_sr = p.emit("act", lambda a: a.activation(
                out=SR[:], in_=cm.PSM[0:1, 0:T], func=AF.Copy,
                scale=1.0 / 32.0),
                reads=["psm", "aasr"], writes=["aasr"],
                wait=[("PE", last)] + sr_free, inc=("ACT", 1))
            cm.psm_free = [("ACT", tok_sr)]
            tok_scst = p.emit("sp", lambda s, c=c, h=h: s.dma_start(
                out=scl.ap()[c, h * T:(h + 1) * T].rearrange(
                    "(a b) -> a b", a=1),
                in_=SR[:]), dma=True, wait=[("ACT", tok_sr)],
                inc=("SC", 16))
            sc_store.append(tok_scst)
            sr_free = [("SC", tok_scst)]
            last = None
            for n4 in range(4):
                last = p.emit("pe", lambda t, n4=n4, par=par, chunk=chunk:
                              t.matmul(
                    cm.PSM[0:64, n4 * 512:(n4 + 1) * 512],
                    WT[par][:, 128:192],
                    XT[:, chunk * T + n4 * 512:chunk * T + (n4 + 1) * 512],
                    start=True, stop=True),
                    reads=["xt", f"aawt{par}"], writes=["psm"],
                    wait=cm.psm_free, inc=("PE", 1) if n4 == 3 else None)
            p.emit("dve", lambda v: v.reduce_sum(
                VS[:, 0:1], cm.PSM[0:64, 0:T], axis=AX.X),
                reads=["psm", "aavs"], writes=["aavs"],
                wait=[("PE", last)], inc=("DVE", 1))
            tok_vsb = p.emit("dve", lambda v: v.tensor_copy(
                VSB[:], VS[:, 0:1]), reads=["aavs", "aavsb"],
                writes=["aavsb"], inc=("DVE", 1))
            cm.psm_free = [("DVE", tok_vsb)]
            tok_vt = p.emit("pe", lambda t: t.transpose(
                cm.PST[1][0:1, 0:64], VSB[:], cm.IDENT[0:64, 0:64]),
                reads=["aavsb", "ident"], writes=["pst1"],
                wait=[("DVE", tok_vsb)] + cm.pst_free[1], inc=("PE", 1))
            tok_vr = p.emit("act", lambda a, c=c, h=h: a.copy(
                VSR[:, (c * 16 + h) * DH:(c * 16 + h + 1) * DH],
                cm.PST[1][0:1, 0:64]),
                reads=["pst1"], writes=["aavsr"],
                wait=[("PE", tok_vt)], inc=("ACT", 1))
            cm.pst_free[1] = [("ACT", tok_vr)]
    tok_ph1_dma = p.tok("SC")
    tok_ph1_act = p.tok("ACT")

    p.emit("sp", lambda s: s.dma_start(
        out=SA[:], in_=scl.ap()[0, :].rearrange("(a b) -> a b", a=128)),
        dma=True, wait=[("SC", tok_ph1_dma)], inc=("WO", 16))
    p.emit("sp", lambda s: s.dma_start(
        out=SB[:, 0:256], in_=scl.ap()[1, :].rearrange("(a b) -> a b",
                                                       a=128)),
        dma=True, wait=[("SC", tok_ph1_dma)], inc=("WO", 16))
    tok_s2 = p.tok("WO")
    tok_m = p.emit("dve", lambda v: v.tensor_max(
        SB[:, 256:512], SA[:], SB[:, 0:256]),
        reads=["aasa", "aasb"], writes=["aasb"],
        wait=[("WO", tok_s2)], inc=("DVE", 1))
    p.emit("sp", lambda s: s.dma_start(
        out=lmax.ap(), in_=SB[:, 256:512]),
        dma=True, wait=[("DVE", tok_m)], inc=("SC", 16))
    tok_lm = p.tok("SC")
    tok_cc1 = p.emit("gp", lambda g: g.collective_compute(
        "AllReduce", ALU.max, replica_groups=[list(range(NCORES))],
        ins=[lmax.ap().opt()], outs=[gmax.ap().opt()]),
        wait=[("SC", tok_lm)], inc=("CC", 1))
    p.emit("sp", lambda s: s.dma_start(
        out=SB[:, 256:512], in_=gmax.ap()),
        dma=True, wait=[("CC", tok_cc1)], inc=("WO", 16))
    tok_gm = p.tok("WO")
    for c in range(NSEQ):
        src_t = SA if c == 0 else None
        p.emit("dve", lambda v, c=c: v.tensor_tensor(
            SB[:, (3 + c) * 256:(4 + c) * 256],
            (SA[:] if c == 0 else SB[:, 0:256]), SB[:, 256:512],
            ALU.subtract),
            reads=["aasa", "aasb"], writes=["aasb"],
            wait=[("WO", tok_gm)], inc=("DVE", 1))
        p.emit("act", lambda a, c=c: a.activation(
            out=SB[:, (3 + c) * 256:(4 + c) * 256],
            in_=SB[:, (3 + c) * 256:(4 + c) * 256], func=AF.Exp),
            reads=["aasb"], writes=["aasb2"],
            wait=[("DVE", p.tok("DVE"))], inc=("ACT", 1))
    tok_exps = p.tok("ACT")
    tok_ls = p.emit("dve", lambda v: v.tensor_add(
        SB[:, 512:768], SB[:, 768:1024], SB[:, 1024:1280]),
        reads=["aasb", "aasb2"], writes=["aasb"],
        wait=[("ACT", tok_exps)], inc=("DVE", 1))
    p.emit("sp", lambda s: s.dma_start(
        out=lsum.ap(), in_=SB[:, 512:768]),
        dma=True, wait=[("DVE", tok_ls)], inc=("SC", 16))
    tok_ls2 = p.tok("SC")
    tok_cc2 = p.emit("gp", lambda g: g.collective_compute(
        "AllReduce", ALU.add, replica_groups=[list(range(NCORES))],
        ins=[lsum.ap().opt()], outs=[gsum.ap().opt()]),
        wait=[("SC", tok_ls2)], inc=("CC", 1))
    p.emit("sp", lambda s: s.dma_start(
        out=SB[:, 512:768], in_=gsum.ap()),
        dma=True, wait=[("CC", tok_cc2)], inc=("WO", 16))
    tok_gs = p.tok("WO")
    tok_rc = p.emit("dve", lambda v: v.reciprocal(
        SA[:], SB[:, 512:768]), reads=["aasb"], writes=["aasa"],
        wait=[("WO", tok_gs)], inc=("DVE", 1))
    for c in range(NSEQ):
        tok_pc = p.emit("dve", lambda v, c=c: v.tensor_mul(
            SB[:, (3 + c) * 256:(4 + c) * 256],
            SB[:, (3 + c) * 256:(4 + c) * 256], SA[:]),
            reads=["aasa", "aasb", "aasb2"], writes=["aasb"],
            wait=[("DVE", tok_rc)], inc=("DVE", 1))
        p.emit("sp", lambda s, c=c: s.dma_start(
            out=prb.ap()[c, :].rearrange("(a b) -> a b", a=128),
            in_=SB[:, (3 + c) * 256:(4 + c) * 256]),
            dma=True, wait=[("DVE", tok_pc)], inc=("SC", 16))
    tok_prb_st = p.tok("SC")

    for c in range(NSEQ):
        rin_seq = rin[c * T:(c + 1) * T, :]
        rout_seq = rout[c * T:(c + 1) * T, :]
        for h in range(16):
            tok_l = p.emit("sp", lambda s, c=c, h=h: s.dma_start(
                out=SR[:], in_=prb.ap()[c, h * T:(h + 1) * T].rearrange(
                    "(a b) -> a b", a=1)),
                dma=True, wait=[("SC", tok_prb_st), ("ACT", p.tok("ACT"))],
                inc=("WO", 16))
            p.emit("act", lambda a, h=h: a.copy(
                PRB[:, h * T:(h + 1) * T], SR[:]),
                reads=["aasr"], writes=["aaprb"],
                wait=[("WO", tok_l)], inc=("ACT", 1))
        tok_prb_all = p.tok("ACT")
        for tb in range(NT):
            par = tb % 2
            RT = cm.RT[par]
            tok_ld = p.emit(
                "sp", lambda s, RT=RT, tb=tb, rin_seq=rin_seq: s.dma_start(
                    out=RT[:], in_=rin_seq[tb * 128:(tb + 1) * 128, :]),
                dma=True, wait=cm.rt_free[par] + cm.rt_store[par],
                inc=(f"RL{par}", 16))
            lasts = [None, None]
            for h in range(16):
                bx = h // 8
                px = cm.PSX[bx]
                lasts[bx] = p.emit(
                    "pe", lambda t, h=h, tb=tb, c=c, px=px, bx=bx: t.matmul(
                        px[:, (h % 8) * 64:(h % 8 + 1) * 64],
                        PRB[0:1, h * T + tb * 128:h * T + (tb + 1) * 128],
                        VSR[0:1, (c * 16 + h) * DH:(c * 16 + h + 1) * DH],
                        start=(h % 8 == 0), stop=(h % 8 == 7)),
                    reads=["aaprb", "aavsr"], writes=[f"psx{bx}"],
                    wait=[("ACT", tok_prb_all), ("ACT", tok_ph1_act)] +
                         (cm.psx_free[bx] if h % 8 == 0 else []),
                    inc=("PE", 1) if h % 8 == 7 else None)
            tok_a1 = p.emit("dve", lambda v, RT=RT: v.tensor_tensor(
                RT[:, 0:512], RT[:, 0:512], cm.PSX[0][:], ALU.add),
                reads=[f"RT{par}", "psx0"], writes=[f"RT{par}"],
                wait=[("PE", lasts[0]), (f"RL{par}", tok_ld)],
                inc=("DVE", 1))
            tok_a2 = p.emit("dve", lambda v, RT=RT: v.tensor_tensor(
                RT[:, 512:1024], RT[:, 512:1024], cm.PSX[1][:], ALU.add),
                reads=[f"RT{par}", "psx1"], writes=[f"RT{par}"],
                wait=[("PE", lasts[1])], inc=("DVE", 1))
            cm.psx_free[0] = [("DVE", tok_a1)]
            cm.psx_free[1] = [("DVE", tok_a2)]
            cm.rt_free[par] = [("DVE", tok_a2)]
            tok_st = p.emit(
                "sp", lambda s, RT=RT, tb=tb, rout_seq=rout_seq: s.dma_start(
                    out=rout_seq[tb * 128:(tb + 1) * 128, :], in_=RT[:]),
                dma=True, wait=[("DVE", tok_a2)], inc=(f"ST{par}", 16))
            cm.rt_store[par] = [(f"ST{par}", tok_st)]
    p.barrier("sp", [("ST0", p.tok("ST0")), ("ST1", p.tok("ST1"))])


def build_l1_attn_a():
    p = Prog()
    nc = p.nc
    rin = nc.dram_tensor("rin", [NSEQ * T, D], F32, kind="ExternalInput").ap()
    wq = nc.dram_tensor("wq_a", [64, 64], F32, kind="ExternalInput").ap()
    wk = nc.dram_tensor("wk_a", [64, 64], F32, kind="ExternalInput").ap()
    wv = nc.dram_tensor("wv_a", [64, 64], F32, kind="ExternalInput").ap()
    cstf = nc.dram_tensor("cstf", [128 * 128], F32, kind="ExternalInput").ap()
    rout = nc.dram_tensor("rout", [NSEQ * T, D], F32,
                          kind="ExternalOutput").ap()
    cm = Common(p)
    cm.emit_setup(cstf)
    emit_l1_attn_a(p, cm, rin, rout, wq, wk, wv)
    p.finish()
    return p


def build_l0_attn_a():
    p = Prog()
    nc = p.nc
    rin = nc.dram_tensor("rin", [NSEQ * T, D], F32, kind="ExternalInput").ap()
    watt = nc.dram_tensor("w_attn", [3 * D, D], F32,
                          kind="ExternalInput").ap()
    wproj = nc.dram_tensor("w_proj", [D, D], F32, kind="ExternalInput").ap()
    cstf = nc.dram_tensor("cstf", [128 * 128], F32, kind="ExternalInput").ap()
    cstb = nc.dram_tensor("cstb", [CSTB_LEN], BF16, kind="ExternalInput").ap()
    rout = nc.dram_tensor("rout", [NSEQ * T, D], F32,
                          kind="ExternalOutput").ap()
    cm = Common(p)
    cm.emit_setup(cstf)
    emit_mha16(p, cm, rin, rout, watt, wproj, True, cstb, "aa")
    p.finish()
    return p


def build_l1_attn_b():
    p = Prog()
    nc = p.nc
    rin = nc.dram_tensor("rin", [NSEQ * T, D], F32, kind="ExternalInput").ap()
    wqkv = nc.dram_tensor("w_qkv", [3, 16, 64, D], F32,
                          kind="ExternalInput").ap()
    wo = nc.dram_tensor("w_o", [D, D], F32, kind="ExternalInput").ap()
    cstf = nc.dram_tensor("cstf", [128 * 128], F32, kind="ExternalInput").ap()
    rout = nc.dram_tensor("rout", [NSEQ * T, D], F32,
                          kind="ExternalOutput").ap()
    cm = Common(p)
    cm.emit_setup(cstf)
    emit_mha16(p, cm, rin, rout,
               wqkv.rearrange("q h d c -> (q h d) c"), wo, False, None, "bb")
    p.finish()
    return p


def build_fused():
    p = Prog()
    p.init_arena(212736)
    nc = p.nc
    A = lambda n, s: nc.dram_tensor(n, list(s), F32,
                                    kind="ExternalInput").ap()
    rin = A("rin", [NSEQ * T, D])
    w_attn = A("w_attn", [3 * D, D])
    w_proj = A("w_proj", [D, D])
    gate_w = A("gate_w", [8, D])
    w1 = A("w1", [8, 2048, D])
    w2 = A("w2", [8, D, 2048])
    w3 = A("w3", [8, 2048, D])
    wq_b = A("wq", [D, 128])
    wk_b = A("wk", [D, 128])
    wv_b = A("wv", [D, 128])
    wo_b = A("wo", [128, D])
    wq_a = A("wq_a", [64, 64])
    wk_a = A("wk_a", [64, 64])
    wv_a = A("wv_a", [64, 64])
    w_qkv = A("w_qkv", [3, 16, 64, D])
    w_o = A("w_o", [D, D])
    wg = A("wg", [D, 4096])
    bg = A("bg", [4096])
    wu = A("wu", [D, 4096])
    bu = A("bu", [4096])
    wd = A("wd", [4096, D])
    bd = A("bd", [D])
    wqp = A("wq_p", [D, 512])
    keys = A("keys", [4, 64, 2, 64])
    de = A("down_embed", [4096, D])
    ue = A("up_embed", [4096, D])
    cstf = nc.dram_tensor("cstf", [128 * 128], F32, kind="ExternalInput").ap()
    cstb = nc.dram_tensor("cstb", [CSTB_LEN], BF16, kind="ExternalInput").ap()
    rout = nc.dram_tensor("rout", [NSEQ * T, D], F32,
                          kind="ExternalOutput").ap()
    RA = nc.dram_tensor("resid_a", [NSEQ * T, D], F32).ap()
    RB = nc.dram_tensor("resid_b", [NSEQ * T, D], F32).ap()

    cm = Common(p)
    cm.emit_setup(cstf)

    steps = [
        (T, lambda i, o: emit_mha16(p, cm, i, o, w_attn, w_proj, True,
                                    cstb, "aa")),
        (1024, lambda i, o: emit_l0_ffn(
            p, cm, i, o, gate_w, w1.rearrange("e f c -> (e f) c"),
            w2.rearrange("e d f -> (e d) f"),
            w3.rearrange("e f c -> (e f) c"))),
        (T, lambda i, o: emit_l0_attn_b(p, cm, i, o, wq_b, wk_b, wv_b,
                                        wo_b)),
        (1024, lambda i, o: emit_l0_ffn(
            p, cm, i, o, gate_w, w1.rearrange("e f c -> (e f) c"),
            w2.rearrange("e d f -> (e d) f"),
            w3.rearrange("e f c -> (e f) c"))),
        (T, lambda i, o: emit_l1_attn_a(p, cm, i, o, wq_a, wk_a, wv_a)),
        (1024, lambda i, o: emit_l1_ffn(
            p, cm, i, o, wg, bg, wu, bu, wd, bd, wqp,
            keys.rearrange("h k t n -> (h k) (t n)"), de, ue)),
        (T, lambda i, o: emit_mha16(p, cm, i, o,
                                    w_qkv.rearrange("q h d c -> (q h d) c"),
                                    w_o, False, None, "bb")),
        (1024, lambda i, o: emit_l1_ffn(
            p, cm, i, o, wg, bg, wu, bu, wd, bd, wqp,
            keys.rearrange("h k t n -> (h k) (t n)"), de, ue)),
    ]
    chain = [rin, RA, RB, RA, RB, RA, RB, RA, rout]
    for i, (xtk, fn) in enumerate(steps):
        p.ascope()
        p.name_prefix = f"s{i}_"
        p.full_fence()
        cm.set_xt(xtk)
        fn(chain[i], chain[i + 1])
        p.pe_drain_on = False
        p.aend()
    p.name_prefix = ""
    n_ops = sum(len(v) for v in p.ops.values())
    p.finish()
    p._n_ops = n_ops
    return p


_PROG_CACHE = {}


def _get_prog(name):
    if name not in _PROG_CACHE:
        _PROG_CACHE[name] = {
            "l0aa": build_l0_attn_a,
            "l0f": build_l0_ffn,
            "l0ab": build_l0_attn_b,
            "l1aa": build_l1_attn_a,
            "l1f": build_l1_ffn,
            "l1ab": build_l1_attn_b,
        }[name]()
    return _PROG_CACHE[name]


def kernel(**inputs):
    f32 = lambda k: np.ascontiguousarray(np.asarray(inputs[k], np.float32))
    x = f32("L0_attn_a_x").reshape(16 * T, D)
    cstf, cstb = make_consts()
    if "fused" not in _PROG_CACHE:
        _PROG_CACHE["fused"] = build_fused()
    p = _PROG_CACHE["fused"]
    wmap = {
        "w_attn": f32("L0_attn_a_w_attn"), "w_proj": f32("L0_attn_a_w_proj"),
        "gate_w": f32("L0_ffn_gate_w"), "w1": f32("L0_ffn_w1"),
        "w2": f32("L0_ffn_w2"), "w3": f32("L0_ffn_w3"),
        "wq": f32("L0_attn_b_Wq"), "wk": f32("L0_attn_b_Wk"),
        "wv": f32("L0_attn_b_Wv"), "wo": f32("L0_attn_b_Wo"),
        "wq_a": f32("L1_attn_a_Wq"), "wk_a": f32("L1_attn_a_Wk"),
        "wv_a": f32("L1_attn_a_Wv"),
        "w_qkv": f32("L1_attn_b_W_qkv"), "w_o": f32("L1_attn_b_W_o"),
        "wg": f32("L1_ffn_Wg"), "bg": f32("L1_ffn_bg"),
        "wu": f32("L1_ffn_Wu"), "bu": f32("L1_ffn_bu"),
        "wd": f32("L1_ffn_Wd"), "bd": f32("L1_ffn_bd"),
        "wq_p": f32("L1_ffn_Wq"), "keys": f32("L1_ffn_keys"),
        "down_embed": f32("L1_ffn_down_embed"),
        "up_embed": f32("L1_ffn_up_embed"),
        "cstf": cstf, "cstb": cstb,
    }
    in_maps = []
    for c in range(NCORES):
        m = dict(wmap)
        m["rin"] = x[c * NSEQ * T:(c + 1) * NSEQ * T]
        in_maps.append(m)
    res = run_bass_kernel_spmd(p.nc, in_maps, core_ids=list(range(NCORES)))
    out = np.concatenate([r["rout"] for r in res.results], axis=0)
    return out.reshape(16, T, D).astype(np.float32)
